# version 13
# baseline (speedup 1.0000x reference)
import math
import numpy as np
import ml_dtypes
from contextlib import ExitStack
import concourse.bass as bass
import concourse.mybir as mybir
from concourse.bass_utils import run_bass_kernel_spmd

F32 = mybir.dt.float32
BF16 = mybir.dt.bfloat16
AF = mybir.ActivationFunctionType
ALU = mybir.AluOpType
AXX = mybir.AxisListType.X

D = 1024
NCK = 8
T = 4096
TOWN = 2048
H = 16
EXPN = 8
DFF = 2048
EPS = 1e-5
SQS = float(0.125 ** 0.5)
NEG = -1.0e30
ENGS = ("tensor", "vector", "scalar", "sync", "gpsimd")
CF = [1.0 / math.factorial(k) for k in range(14)]


class Sched:
    def __init__(self, blk, sems):
        self.blk, self.sems = blk, sems
        self.q = {e: [] for e in ENGS}
        self.n = {s: 0 for s in sems}
        self.tr = {e: [] for e in ENGS}

    def w(self, eng, sem, val):
        if val is None or val <= 0:
            return
        self.q[eng].append(("w", sem, int(val)))
        self.tr[eng].append(("w", sem, int(val)))

    def wd(self, eng, dep):
        if dep is None:
            return
        if isinstance(dep, list):
            for d in dep:
                self.wd(eng, d)
            return
        self.w(eng, dep[0], dep[1])

    def op(self, eng, fn, inc=None, k=1):
        it = ["o", fn, inc, k]
        self.q[eng].append(it)
        self.tr[eng].append(it)
        if inc:
            self.n[inc] += k
            return self.n[inc]
        return None

    def vdr(self):
        self.op("vector", lambda e: e.drain())

    def tok(self, eng, inc):
        for it in reversed(self.q[eng]):
            if it[0] == "o":
                if it[2] is None:
                    it[2] = inc
                    it[3] = 1
                    self.n[inc] += 1
                    return self.n[inc]
                if it[2] == inc:
                    it[3] += 1
                    self.n[inc] += 1
                    return self.n[inc]
                break
        raise RuntimeError(f"tok({eng},{inc}): no attachable instruction")

    def flush(self):
        for e in ENGS:
            ops = self.q[e]
            self.q[e] = []
            if not ops:
                continue

            def run(engine, ops=ops, sems=self.sems):
                for it in ops:
                    if it[0] == "w":
                        engine.wait_ge(sems[it[1]], it[2])
                    else:
                        ins = it[1](engine)
                        if it[2] is not None:
                            ins.then_inc(sems[it[2]], it[3])
            getattr(self.blk, e)(run)

    def barrier(self):
        snap = dict(self.n)
        own = {"tensor": "PE", "scalar": "AC", "vector": "VE"}
        for e in ENGS:
            for s, v in snap.items():
                if v > 0 and own.get(e) != s:
                    self.w(e, s, v)

    def check(self):
        ptr = {e: 0 for e in ENGS}
        cnt = {s: 0 for s in self.sems}
        total = sum(len(v) for v in self.tr.values())
        done, progress = 0, True
        while progress:
            progress = False
            for e in ENGS:
                tr = self.tr[e]
                while ptr[e] < len(tr):
                    it = tr[ptr[e]]
                    if it[0] == "w" and cnt[it[1]] < it[2]:
                        break
                    if it[0] == "o" and it[2] is not None:
                        cnt[it[2]] += it[3]
                    ptr[e] += 1
                    done += 1
                    progress = True
        if done != total:
            stuck = {e: (ptr[e], len(self.tr[e]),
                         self.tr[e][ptr[e]] if ptr[e] < len(self.tr[e]) else None)
                     for e in ENGS}
            raise RuntimeError(f"schedule deadlock: {stuck} sems={cnt}")


class Bank:
    def __init__(self, ap):
        self.ap = ap
        self.ready = None
        self.free = None


def build_nc(dbg=(), stop=None):
    nc = bass.Bass("TRN2", target_bir_lowering=False)

    def din(name, shape, dt):
        return nc.dram_tensor(name, shape, dt, kind="ExternalInput").ap()

    def dscr(name, shape, dt):
        return nc.dram_tensor(name, shape, dt, kind="Internal").ap()

    xb = din("xb", [T, D], F32)
    wq = din("wq", [D, D], F32)
    wk = din("wk", [D, D], F32)
    wv = din("wv", [D, D], F32)
    wo = din("wo", [D, D], F32)
    abel_w = din("abel_w", [D, D], F32)
    gate_w = din("gate_w", [D, EXPN], F32)
    ew1 = din("ew1", [EXPN, D, DFF], F32)
    ew2r = din("ew2r", [EXPN, NCK, 128, 16 * 128], F32)
    gens = din("gens", [3, D, D], F32)
    wmod = din("wmod", [D, 6 * D], F32)
    eb2 = din("eb2", [EXPN, D], F32)
    bq_f = din("bq_f", [128, NCK], F32)
    bk_f = din("bk_f", [128, NCK], F32)
    bo_f = din("bo_f", [128, NCK], F32)
    bv_row = din("bv_row", [1, D], F32)
    abel_b_f = din("abel_b_f", [128, NCK], F32)
    ln_g_f = din("ln_g_f", [128, NCK], F32)
    ln_b_f = din("ln_b_f", [128, NCK], F32)
    c_f = din("c_f", [128, NCK], F32)
    eb1_f = din("eb1_f", [128, EXPN * 16], F32)
    gate_b = din("gate_b", [EXPN, 1], F32)
    gcw_f = din("gcw_f", [128, 16], F32)
    gcb = din("gcb", [1, 1], F32)
    ident_bf = din("ident_bf", [128, 128], BF16)
    ident_f = din("ident_f", [128, 128], F32)
    ones_row_bf = din("ones_row_bf", [1, 128], BF16)
    ones_col_bf = din("ones_col_bf", [128, 1], BF16)
    onesf_c = din("onesf_c", [128, 1], F32)
    eye_fm = din("eye_fm", [128, NCK * D], BF16)

    out_hbm = nc.dram_tensor("out", [TOWN, D], F32, kind="ExternalOutput").ap()

    s_q = dscr("s_q", [NCK, 128, TOWN], BF16)
    s_k = dscr("s_k", [NCK, 128, T], BF16)
    s_v = dscr("s_v", [32, 128, H * 65], BF16)
    s_E = dscr("s_E", [3, 128, NCK * D], BF16)
    s_x2 = dscr("s_x2", [16, 128, D], F32)
    s_h2 = dscr("s_h2", [NCK, 128, TOWN], BF16)

    dbg_aps = {}
    for name, shape, dt in dbg:
        dbg_aps[name] = nc.dram_tensor("dbg_" + name, shape, dt,
                                       kind="ExternalOutput").ap()

    es = ExitStack()
    sems = {}
    for s in ("DW", "DX", "DS", "PE", "AC", "VE", "WA", "WB", "XA", "XB",
              "KA", "KB", "HH", "EE", "SA", "SB", "E1", "AB", "WQ", "H2S",
              "OS", "KS", "QS", "VS", "X2A", "X2B"):
        sems[s] = es.enter_context(nc.semaphore(s))
    blk = es.enter_context(nc.Block())
    S = Sched(blk, sems)

    def sbt(name, shape, dt, stack):
        return stack.enter_context(nc.sbuf_tensor(name, shape, dt))

    def pst(name, shape, dt, stack):
        return stack.enter_context(nc.psum_tensor(name, shape, dt))

    def tap(name, src_fn):
        if name in dbg_aps:
            S.op("sync", lambda e, n=name, f=src_fn: e.dma_start(dbg_aps[n], f()),
                 inc="DS", k=16)

    P = ExitStack()
    idb = sbt("idb", [128, 128], BF16, P)
    idf = sbt("idf", [128, 128], F32, P)
    ones_r = sbt("ones_r", [1, 128], BF16, P)
    ones_c = sbt("ones_c", [128, 1], BF16, P)
    onef_c = sbt("onef_c", [128, 1], F32, P)
    bias_q = sbt("bias_q", [128, NCK], F32, P)
    bias_k = sbt("bias_k", [128, NCK], F32, P)
    bias_o = sbt("bias_o", [128, NCK], F32, P)
    abel_bb = sbt("abel_bb", [128, NCK], F32, P)
    ln_g = sbt("ln_g", [128, NCK], F32, P)
    ln_b = sbt("ln_b", [128, NCK], F32, P)
    c_sb = sbt("c_sb", [128, NCK], F32, P)
    eb1_sb = sbt("eb1_sb", [128, EXPN * 16], F32, P)
    gate_b_sb = sbt("gate_b_sb", [EXPN, 1], F32, P)
    gcb_sb = sbt("gcb_sb", [1, 1], F32, P)
    gcw_bf = sbt("gcw_bf", [128, 16], BF16, P)
    eb2b = sbt("eb2b", [EXPN, D], BF16, P)
    gw_bf = sbt("gw_bf", [128, NCK * EXPN], BF16, P)
    mod_sb = sbt("mod_sb", [128, 64], F32, P)
    ca1p = sbt("ca1p", [128, NCK], F32, P)
    cb1p = sbt("cb1p", [128, NCK], F32, P)
    sa_v = sbt("sa_v", [128, NCK], F32, P)
    sbv_v = sbt("sbv_v", [128, NCK], F32, P)
    ga_v = sbt("ga_v", [128, NCK], F32, P)
    gb_v = sbt("gb_v", [128, NCK], F32, P)
    gabo = sbt("gabo", [128, NCK], F32, P)
    eps_c = sbt("eps_c", [128, 1], F32, P)
    zero_c = sbt("zero_c", [128, 1], F32, P)
    mask_sb = sbt("mask_sb", [128, 32], F32, P)
    mk2_sb = sbt("mk2_sb", [128, 16], F32, P)
    ca0_bc = sbt("ca0_bc", [128, 1], F32, P)
    sa0_bc = sbt("sa0_bc", [128, 1], F32, P)
    bvr_bf = sbt("bvr_bf", [1, D], BF16, P)
    cst = sbt("cst", [128, 4], F32, P)

    for dst, src in ((idb, ident_bf), (idf, ident_f), (ones_r, ones_row_bf),
                     (ones_c, ones_col_bf), (onef_c, onesf_c), (bias_q, bq_f),
                     (bias_k, bk_f), (bias_o, bo_f), (abel_bb, abel_b_f),
                     (ln_g, ln_g_f), (ln_b, ln_b_f), (c_sb, c_f),
                     (eb1_sb, eb1_f), (gate_b_sb, gate_b), (gcb_sb, gcb)):
        S.op("sync", lambda e, d=dst, s=src: e.dma_start(d[:, :], s), inc="DX", k=16)
    nDX0 = S.n["DX"]
    S.op("gpsimd", lambda e: e.dma_start(eb2b[:, :], eb2), inc="DW", k=16)
    S.op("gpsimd", lambda e: e.dma_start(gcw_bf[:, :], gcw_f), inc="DW", k=16)
    S.op("gpsimd", lambda e: e.dma_start(bvr_bf[:, :], bv_row), inc="DW", k=16)
    S.op("gpsimd", lambda e: e.dma_start(
        gw_bf[:, :].rearrange("p (c n) -> p c n", n=EXPN),
        gate_w.rearrange("(c p) n -> p c n", p=128)), inc="DW", k=16)
    nDWsmall = S.n["DW"]

    S.w("vector", "DX", nDX0)
    S.op("vector", lambda e: e.memset(eps_c[:, :], EPS))
    S.op("vector", lambda e: e.memset(zero_c[:, :], 0.0))
    S.op("vector", lambda e: e.memset(cst[:, 0:1], CF[8]))
    S.op("vector", lambda e: e.memset(cst[:, 1:2], CF[9]))
    S.op("vector", lambda e: e.memset(mk2_sb[:, :], 0.0))
    S.op("vector", lambda e: e.tensor_scalar(mod_sb[:, 48:56], bias_q[:, :], SQS,
                                             None, ALU.mult))
    S.op("vector", lambda e: e.drain())
    S.op("vector", lambda e: e.tensor_copy(bias_q[:, :], mod_sb[:, 48:56]))
    S.op("vector", lambda e: e.drain())
    S.op("vector", lambda e: e.tensor_scalar(mod_sb[:, 48:56], bias_k[:, :], SQS,
                                             None, ALU.mult))
    S.op("vector", lambda e: e.drain())
    S.op("vector", lambda e: e.tensor_copy(bias_k[:, :], mod_sb[:, 48:56]))
    vINIT = S.tok("vector", "VE")

    P0 = ExitStack()
    cs_f = sbt("cs_f", [128, NCK], F32, P0)
    wm_buf = [sbt(f"wm{i}", [128, 6 * D], F32, P0) for i in range(2)]
    ps_mod = pst("ps_mod", [128, 64], F32, P0)
    ps_bc0 = pst("ps_bc0", [128, 512], F32, P0)
    t_bc = sbt("t_bc", [1, 8], BF16, P0)

    S.w("scalar", "DX", nDX0)
    aSIL = S.op("scalar", lambda e: e.activation(cs_f[:, :], c_sb[:, :], AF.Silu),
                inc="AC")
    S.w("vector", "VE", vINIT)
    S.op("vector", lambda e: e.memset(mod_sb[:, 0:48], 0.0))
    vMZ = S.tok("vector", "VE")
    wm_free = [None, None]
    acc_tok = vMZ
    for kc in range(NCK):
        b = kc % 2
        S.wd("sync", wm_free[b])
        wsem = "WA" if b == 0 else "WB"
        S.op("sync", lambda e, kc=kc, b=b: e.dma_start(
            wm_buf[b][:, :], wmod[kc * 128:(kc + 1) * 128, :]), inc=wsem, k=16)
        S.w("tensor", wsem, S.n[wsem])
        if kc == 0:
            S.w("tensor", "AC", aSIL)
        S.w("tensor", "VE", acc_tok)
        for oc in range(48):
            S.op("tensor", lambda e, b=b, oc=oc, kc=kc: e.matmul(
                ps_mod[:, oc:oc + 1], wm_buf[b][:, oc * 128:(oc + 1) * 128],
                cs_f[:, kc:kc + 1], start=True, stop=True))
        wm_free[b] = ("PE", S.tok("tensor", "PE"))
        S.wd("vector", wm_free[b])
        S.op("vector", lambda e: e.tensor_tensor(
            mod_sb[:, 0:48], mod_sb[:, 0:48], ps_mod[:, 0:48], ALU.add))
        S.op("vector", lambda e: e.drain())
        acc_tok = S.tok("vector", "VE")
    pMOD = S.n["PE"]

    S.w("vector", "VE", vINIT)
    S.w("vector", "PE", pMOD)
    S.op("vector", lambda e: e.tensor_scalar(ca1p[:, :], mod_sb[:, 8:16], 1.0, None,
                                             ALU.add))
    S.op("vector", lambda e: e.tensor_scalar(cb1p[:, :], mod_sb[:, 32:40], 1.0, None,
                                             ALU.add))
    S.op("vector", lambda e: e.tensor_copy(sa_v[:, :], mod_sb[:, 0:8]))
    S.op("vector", lambda e: e.tensor_copy(sbv_v[:, :], mod_sb[:, 24:32]))
    for gdst, c0 in ((ga_v, 16), (gb_v, 40)):
        S.op("vector", lambda e, c0=c0: e.tensor_scalar(
            mod_sb[:, 48:56], mod_sb[:, c0:c0 + 8], -1.0, None, ALU.mult))
        S.vdr()
        S.op("vector", lambda e, c0=c0: e.tensor_tensor(
            mod_sb[:, 56:64], mod_sb[:, 48:56], mod_sb[:, c0:c0 + 8], ALU.max))
        S.vdr()
        S.op("vector", lambda e, g=gdst: e.tensor_scalar(
            g[:, :], mod_sb[:, 56:64], 0.25, None, ALU.max))
        S.vdr()
        S.op("vector", lambda e, c0=c0: e.tensor_scalar(
            mod_sb[:, 48:56], mod_sb[:, c0:c0 + 8], zero_c[:, :], None, ALU.is_ge))
        S.vdr()
        S.op("vector", lambda e: e.tensor_scalar(
            mod_sb[:, 56:64], mod_sb[:, 48:56], 2.0, None, ALU.mult))
        S.vdr()
        S.op("vector", lambda e: e.tensor_scalar(
            mod_sb[:, 48:56], mod_sb[:, 56:64], -1.0, None, ALU.add))
        S.vdr()
        S.op("vector", lambda e, g=gdst: e.tensor_tensor(
            mod_sb[:, 56:64], g[:, :], mod_sb[:, 48:56], ALU.mult))
        S.vdr()
        S.op("vector", lambda e, g=gdst: e.tensor_copy(g[:, :], mod_sb[:, 56:64]))
        S.vdr()
    S.op("vector", lambda e: e.tensor_tensor(gabo[:, :], ga_v[:, :], bias_o[:, :],
                                             ALU.mult))
    S.op("vector", lambda e: e.tensor_copy(t_bc[0:1, 0:1], ca1p[0:1, 0:1]))
    S.op("vector", lambda e: e.tensor_copy(t_bc[0:1, 1:2], sa_v[0:1, 0:1]))
    vPAR = S.tok("vector", "VE")

    S.w("tensor", "VE", vPAR)
    S.w("tensor", "DW", nDWsmall)
    S.op("tensor", lambda e: e.matmul(ps_bc0[:, 0:2], ones_r[:, :], t_bc[0:1, 0:2],
                                      start=True, stop=True))
    pB0 = S.tok("tensor", "PE")
    S.w("vector", "PE", pB0)
    S.op("vector", lambda e: e.tensor_copy(ca0_bc[:, :], ps_bc0[:, 0:1]))
    S.op("vector", lambda e: e.tensor_copy(sa0_bc[:, :], ps_bc0[:, 1:2]))
    vSETUP = S.tok("vector", "VE")
    if "mod" in dbg_aps:
        S.w("sync", "VE", vSETUP)
        S.op("sync", lambda e: e.dma_start(dbg_aps["mod"], mod_sb[:, 0:48]), inc="DS", k=16)
        S.op("sync", lambda e: e.dma_start(dbg_aps["ga"], ga_v[:, :]), inc="DS", k=16)
        S.op("sync", lambda e: e.dma_start(dbg_aps["gb"], gb_v[:, :]), inc="DS", k=16)
        S.op("sync", lambda e: e.dma_start(dbg_aps["ca1p"], ca1p[:, :]), inc="DS", k=16)
        S.op("sync", lambda e: e.dma_start(dbg_aps["sa"], sa_v[:, :]), inc="DS", k=16)
    S.barrier()
    S.flush()
    P0.close()
    if stop == "p0":
        P.close(); es.close(); S.check(); return nc, dbg_aps

    PX = ExitStack()
    A_sb = sbt("A_sb", [128, NCK * D], BF16, PX)
    X2 = sbt("X2", [128, NCK * D], BF16, PX)
    X4 = sbt("X4", [128, NCK * D], BF16, PX)
    X6 = sbt("X6", [128, NCK * D], BF16, PX)
    TMP = sbt("TMP", [128, NCK * D], BF16, PX)
    WES = sbt("WES", [128, NCK * D], BF16, PX)
    EB = sbt("EB", [128, NCK * D], BF16, PX)
    OB = sbt("OB", [128, NCK * D], BF16, PX)
    EYE = sbt("EYE", [128, NCK * D], BF16, PX)
    ps_x = [Bank(pst(f"ps_x{i}", [128, 512], F32, PX)) for i in range(2)]

    S.op("sync", lambda e: e.dma_start(EYE[:, :], eye_fm), inc="DX", k=16)
    dEYE = S.n["DX"]

    def mm_big(dst, lhs, rhs, scale):
        for mc in range(NCK):
            for nb in range(2):
                bk = ps_x[(mc * 2 + nb) % 2]
                S.wd("tensor", bk.free)
                for kc in range(NCK):
                    S.op("tensor", lambda e, mc=mc, nb=nb, kc=kc, bk=bk, lhs=lhs,
                         rhs=rhs: e.matmul(
                             bk.ap[:, :],
                             lhs[:, kc * D + mc * 128: kc * D + mc * 128 + 128],
                             rhs[:, kc * D + nb * 512: kc * D + nb * 512 + 512],
                             start=(kc == 0), stop=(kc == NCK - 1)))
                bk.ready = ("PE", S.tok("tensor", "PE"))
                S.wd("scalar", bk.ready)
                tk = S.op("scalar", lambda e, mc=mc, nb=nb, bk=bk, dst=dst,
                          scale=scale: e.activation(
                              dst[:, mc * D + nb * 512: mc * D + nb * 512 + 512],
                              bk.ap[:, :], AF.Identity, scale=scale), inc="AC")
                bk.free = ("AC", tk)
        return S.n["AC"]

    def stt(dst, in0, sc, in1):
        S.op("vector", lambda e, dst=dst, in0=in0, sc=sc, in1=in1:
             e.scalar_tensor_tensor(dst[:, :], in0[:, :], sc, in1[:, :],
                                    ALU.mult, ALU.add))
        S.vdr()

    for gi in range(3):
        S.op("gpsimd", lambda e, gi=gi: e.dma_start(
            A_sb[:, :].rearrange("p (c n) -> p c n", n=D),
            gens[gi].rearrange("(c p) n -> p c n", p=128)), inc="WQ", k=16)
        S.w("tensor", "WQ", S.n["WQ"])
        if gi == 0:
            S.w("tensor", "VE", vSETUP)
            S.w("scalar", "VE", vSETUP)
        a2 = mm_big(X2, A_sb, A_sb, -1.0)
        if stop == "xa2":
            S.flush(); PX.close(); P.close(); es.close(); S.check()
            return nc, dbg_aps
        S.w("tensor", "AC", a2)
        a4 = mm_big(X4, X2, X2, 1.0)
        S.w("tensor", "AC", a4)
        a6 = mm_big(X6, X2, X4, 1.0)
        S.w("vector", "AC", a6)
        if gi == 0:
            S.w("vector", "DX", dEYE)
        stt(TMP, X4, CF[10] / CF[8], X2)
        stt(TMP, X6, CF[12] / CF[8], TMP)
        v1 = S.tok("vector", "VE")
        S.w("tensor", "VE", v1)
        we = mm_big(WES, X6, TMP, cst[:, 0:1])
        S.w("vector", "AC", we)
        stt(TMP, X4, CF[4] / CF[2], X2)
        stt(TMP, X6, CF[6] / CF[2], TMP)
        stt(EB, TMP, CF[2], WES)
        S.op("vector", lambda e: e.tensor_tensor(EB[:, :], EB[:, :], EYE[:, :],
                                                 ALU.add))
        stt(TMP, X4, CF[11] / CF[9], X2)
        stt(TMP, X6, CF[13] / CF[9], TMP)
        v2 = S.tok("vector", "VE")
        S.w("tensor", "VE", v2)
        ws = mm_big(WES, X6, TMP, cst[:, 1:2])
        S.w("vector", "AC", ws)
        stt(TMP, X4, CF[5] / CF[3], X2)
        stt(TMP, X6, CF[7] / CF[3], TMP)
        stt(TMP, TMP, CF[3], WES)
        S.op("vector", lambda e: e.tensor_tensor(TMP[:, :], TMP[:, :], EYE[:, :],
                                                 ALU.add))
        v3 = S.tok("vector", "VE")
        S.w("tensor", "VE", v3)
        ob = mm_big(OB, A_sb, TMP, -1.0)
        S.w("vector", "AC", ob)
        S.op("vector", lambda e: e.tensor_tensor(EB[:, :], EB[:, :], OB[:, :],
                                                 ALU.add))
        v4 = S.tok("vector", "VE")
        S.w("sync", "VE", v4)
        S.op("sync", lambda e, gi=gi: e.dma_start(s_E[gi], EB[:, :]), inc="DS", k=16)
        S.w("gpsimd", "AC", ob)
        S.w("vector", "DS", S.n["DS"])
        if stop == "xg0":
            S.flush(); PX.close(); P.close(); es.close(); S.check()
            return nc, dbg_aps
    if "E0" in dbg_aps:
        S.w("sync", "DS", S.n["DS"])
        for gi in range(3):
            S.op("sync", lambda e, gi=gi: e.dma_start(dbg_aps[f"E{gi}"], s_E[gi]),
                 inc="DS", k=16)
    S.barrier()
    S.flush()
    PX.close()
    if stop == "x":
        P.close(); es.close(); S.check(); return nc, dbg_aps

    PA = ExitStack()
    wq_T = sbt("wq_T", [128, NCK * D], BF16, PA)
    wk_T = sbt("wk_T", [128, NCK * D], BF16, PA)
    wv_T = sbt("wv_T", [128, NCK * D], BF16, PA)
    wraw = sbt("wraw", [128, NCK * D], BF16, PA)
    x_t = [sbt(f"x_t{i}", [128, D], F32, PA) for i in range(2)]
    scr = sbt("scr", [128, D], F32, PA)
    scr_b = sbt("scr_b", [128, D], F32, PA)
    z_t = [sbt(f"z_t{i}", [128, D], BF16, PA) for i in range(2)]
    h_sl = [sbt(f"h_sl{i}", [128, NCK * 512], BF16, PA) for i in range(2)]
    k_out = sbt("k_out", [128, NCK * 512], BF16, PA)
    q_out = sbt("q_out", [128, NCK * 512], BF16, PA)
    v_out = sbt("v_out", [128, 4 * H * 65], BF16, PA)
    ksq = sbt("ksq", [128, NCK * 512], BF16, PA)
    st_c = sbt("st_c", [128, 12], F32, PA)
    bv_bc = sbt("bv_bc", [128, D], F32, PA)
    ps_tr = [Bank(pst(f"ps_tr{i}", [128, 1024], BF16, PA)) for i in range(2)]
    ps_pj = [Bank(pst(f"ps_pj{i}", [128, 512], F32, PA)) for i in range(2)]
    ps_nb = Bank(pst("ps_nb", [128, 512], F32, PA))

    wraw_free = None
    for wsrc, wdst, wscale in ((wq, wq_T, SQS), (wk, wk_T, SQS), (wv, wv_T, 1.0)):
        S.wd("gpsimd", wraw_free)
        S.op("gpsimd", lambda e, wsrc=wsrc: e.dma_start(
            wraw[:, :].rearrange("p (c n) -> p c n", n=D),
            wsrc.rearrange("(c p) n -> p c n", p=128)), inc="WQ", k=16)
        S.w("tensor", "WQ", S.n["WQ"])
        for oc in range(NCK):
            for ic in range(NCK):
                bk = ps_tr[(oc * NCK + ic) % 2]
                S.wd("tensor", bk.free)
                S.op("tensor", lambda e, oc=oc, ic=ic, bk=bk: e.transpose(
                    bk.ap[:, 0:128],
                    wraw[:, oc * D + ic * 128: oc * D + ic * 128 + 128], idb[:, :]))
                bk.ready = ("PE", S.tok("tensor", "PE"))
                S.wd("scalar", bk.ready)
                tk = S.op("scalar", lambda e, oc=oc, ic=ic, bk=bk, wdst=wdst,
                          wscale=wscale: e.activation(
                              wdst[:, ic * D + oc * 128: ic * D + oc * 128 + 128],
                              bk.ap[:, 0:128], AF.Identity, scale=wscale), inc="AC")
                bk.free = ("AC", tk)
        wraw_free = ("AC", S.n["AC"])
    aWT = S.n["AC"]
    if "wkT" in dbg_aps:
        S.w("sync", "AC", aWT)
        S.op("sync", lambda e: e.dma_start(dbg_aps["wkT"], wk_T[:, :]), inc="DS", k=16)
        S.op("sync", lambda e: e.dma_start(dbg_aps["wqT"], wq_T[:, :]), inc="DS", k=16)
        S.op("sync", lambda e: e.dma_start(dbg_aps["wvT"], wv_T[:, :]), inc="DS", k=16)

    S.w("tensor", "VE", vSETUP)
    for half in range(2):
        S.wd("tensor", ps_nb.free)
        S.op("tensor", lambda e, half=half: e.matmul(
            ps_nb.ap[:, :], ones_r[:, :], bvr_bf[0:1, half * 512:(half + 1) * 512],
            start=True, stop=True))
        ps_nb.ready = ("PE", S.tok("tensor", "PE"))
        S.wd("vector", ps_nb.ready)
        tk = S.op("vector", lambda e, half=half: e.tensor_copy(
            bv_bc[:, half * 512:(half + 1) * 512], ps_nb.ap[:, :]), inc="VE")
        ps_nb.free = ("VE", tk)

    x_free = [None, None]
    z_free = [None, None]
    h_proj_done = [None, None]
    kout_free = None
    qout_free = None
    vout_free = None

    for slab in range(NCK):
        hb = slab % 2
        for tt in range(4):
            ti = slab * 4 + tt
            xbuf = x_t[ti % 2]
            zbuf = z_t[ti % 2]
            S.wd("sync", x_free[ti % 2])
            xsem = "XA" if ti % 2 == 0 else "XB"
            S.op("sync", lambda e, ti=ti, xbuf=xbuf: e.dma_start(
                xbuf[:, :], xb[ti * 128:(ti + 1) * 128, :]), inc=xsem, k=16)
            dx = S.n[xsem]
            S.w("scalar", xsem, dx)
            if ti == 0:
                S.w("scalar", "VE", vSETUP)
                S.w("vector", "VE", vSETUP)
            S.op("scalar", lambda e, xbuf=xbuf: e.activation(
                scr[:, :], xbuf[:, :], AF.Identity, accum_out=st_c[:, 0:1]))
            S.op("scalar", lambda e, xbuf=xbuf: e.activation(
                scr_b[:, :], xbuf[:, :], AF.Square, accum_out=st_c[:, 1:2]), inc="AC")
            ac1 = S.n["AC"]
            S.w("vector", "AC", ac1)
            S.op("vector", lambda e: e.tensor_scalar(
                st_c[:, 2:3], st_c[:, 0:1], 1.0 / D, None, ALU.mult))
            S.vdr()
            S.op("vector", lambda e: e.tensor_tensor(
                st_c[:, 3:4], st_c[:, 2:3], st_c[:, 2:3], ALU.mult))
            S.vdr()
            S.op("vector", lambda e: e.scalar_tensor_tensor(
                st_c[:, 4:5], st_c[:, 1:2], 1.0 / D, st_c[:, 3:4], ALU.mult,
                ALU.subtract))
            S.vdr()
            ve1 = S.tok("vector", "VE")
            S.w("scalar", "VE", ve1)
            S.op("scalar", lambda e: e.activation(
                st_c[:, 5:6], st_c[:, 4:5], AF.Sqrt, bias=eps_c[:, :]), inc="AC")
            ac2 = S.n["AC"]
            S.w("vector", "AC", ac2)
            S.op("vector", lambda e: e.reciprocal(st_c[:, 5:6], st_c[:, 5:6]))
            S.vdr()
            S.op("vector", lambda e: e.scalar_tensor_tensor(
                st_c[:, 6:7], st_c[:, 2:3], -1.0, st_c[:, 5:6], ALU.mult, ALU.mult))
            S.vdr()
            S.op("vector", lambda e, xbuf=xbuf: e.tensor_scalar(
                st_c[:, 7:8], xbuf[:, 0:1], st_c[:, 2:3], None, ALU.subtract))
            S.vdr()
            S.op("vector", lambda e: e.tensor_tensor(
                st_c[:, 8:9], st_c[:, 7:8], st_c[:, 5:6], ALU.mult))
            S.vdr()
            S.op("vector", lambda e: e.tensor_tensor(
                st_c[:, 7:8], st_c[:, 8:9], ca0_bc[:, :], ALU.mult))
            S.vdr()
            S.op("vector", lambda e: e.tensor_tensor(
                st_c[:, 8:9], st_c[:, 7:8], sa0_bc[:, :], ALU.add))
            S.vdr()
            S.op("vector", lambda e: e.tensor_scalar(
                st_c[:, 7:8], st_c[:, 8:9], zero_c[:, :], None, ALU.is_equal))
            S.vdr()
            S.op("vector", lambda e: e.tensor_scalar(
                st_c[:, 9:10], st_c[:, 7:8], -1.0, None, ALU.mult))
            S.vdr()
            S.op("vector", lambda e, ti=ti: e.tensor_scalar(
                mask_sb[:, ti:ti + 1], st_c[:, 9:10], 1.0, None, ALU.add))
            S.vdr()
            ve2 = S.tok("vector", "VE")
            S.w("scalar", "VE", ve2)
            S.wd("scalar", z_free[ti % 2])
            S.op("scalar", lambda e, xbuf=xbuf, zbuf=zbuf: e.activation(
                zbuf[:, :], xbuf[:, :], AF.Identity, bias=st_c[:, 6:7],
                scale=st_c[:, 5:6]), inc="AC")
            acz = S.n["AC"]
            x_free[ti % 2] = ("AC", acz)
            S.w("tensor", "AC", acz)
            bk = ps_tr[ti % 2]
            S.wd("tensor", bk.free)
            if tt == 0:
                S.wd("tensor", h_proj_done[hb])
            for j in range(NCK):
                S.op("tensor", lambda e, j=j, zbuf=zbuf, bk=bk: e.transpose(
                    bk.ap[:, j * 128:(j + 1) * 128], zbuf[:, j * 128:(j + 1) * 128],
                    idb[:, :]))
            bk.ready = ("PE", S.tok("tensor", "PE"))
            z_free[ti % 2] = bk.ready
            S.wd("scalar", bk.ready)
            for j in range(NCK):
                S.op("scalar", lambda e, j=j, tt=tt, hb=hb, bk=bk: e.activation(
                    h_sl[hb][:, j * 512 + tt * 128: j * 512 + tt * 128 + 128],
                    bk.ap[:, j * 128:(j + 1) * 128], AF.Identity,
                    bias=sa_v[:, j:j + 1], scale=ca1p[:, j:j + 1]))
            tk = S.tok("scalar", "AC")
            bk.free = ("AC", tk)
        h_ready = S.n["AC"]
        if slab == 0 and "hsl0" in dbg_aps:
            S.w("sync", "AC", h_ready)
            S.op("sync", lambda e, hb=hb: e.dma_start(dbg_aps["hsl0"], h_sl[hb][:, :]),
                 inc="DS", k=16)
            S.op("sync", lambda e: e.dma_start(dbg_aps["stc0"], st_c[:, 0:12]), inc="DS", k=16)

        S.w("tensor", "AC", max(h_ready, aWT))
        S.wd("tensor", kout_free)
        for oc in range(NCK):
            bk = ps_pj[oc % 2]
            S.wd("tensor", bk.free)
            for kc in range(NCK):
                S.op("tensor", lambda e, oc=oc, kc=kc, bk=bk, hb=hb: e.matmul(
                    bk.ap[:, :], wk_T[:, kc * D + oc * 128: kc * D + oc * 128 + 128],
                    h_sl[hb][:, kc * 512:(kc + 1) * 512],
                    start=(kc == 0), stop=(kc == NCK - 1)))
            bk.ready = ("PE", S.tok("tensor", "PE"))
            S.wd("scalar", bk.ready)
            tk = S.op("scalar", lambda e, oc=oc, bk=bk: e.activation(
                k_out[:, oc * 512:(oc + 1) * 512], bk.ap[:, :], AF.Identity,
                bias=bias_k[:, oc:oc + 1]), inc="AC")
            bk.free = ("AC", tk)
        ack = S.n["AC"]
        if slab == 0 and "kout0" in dbg_aps:
            S.w("sync", "AC", ack)
            S.op("sync", lambda e: e.dma_start(dbg_aps["kout0"], k_out[:, :]), inc="DS", k=16)
        S.w("vector", "AC", ack)
        S.op("vector", lambda e: e.tensor_tensor(ksq[:, :], k_out[:, :], k_out[:, :],
                                                 ALU.mult))
        vks = S.tok("vector", "VE")
        S.w("tensor", "VE", vks)
        for oc in range(NCK):
            S.wd("tensor", ps_nb.free)
            S.op("tensor", lambda e, oc=oc: e.matmul(
                ps_nb.ap[0:1, :], ones_c[0:64, :], ksq[0:64, oc * 512:(oc + 1) * 512],
                start=True, stop=True))
            S.op("tensor", lambda e, oc=oc: e.matmul(
                ps_nb.ap[64:65, :], ones_c[64:128, :],
                ksq[64:128, oc * 512:(oc + 1) * 512], start=True, stop=True))
            ps_nb.ready = ("PE", S.tok("tensor", "PE"))
            S.wd("vector", ps_nb.ready)
            S.op("vector", lambda e: e.tensor_reduce(st_c[:, 9:10], ps_nb.ap[:, :],
                                                     AXX, ALU.max))
            S.vdr()
            S.op("vector", lambda e, oc=oc: e.tensor_tensor(
                st_c[0:1, 10:11], mk2_sb[0:1, 2 * oc:2 * oc + 1],
                st_c[0:1, 9:10], ALU.max))
            S.vdr()
            S.op("vector", lambda e, oc=oc: e.tensor_copy(
                mk2_sb[0:1, 2 * oc:2 * oc + 1], st_c[0:1, 10:11]))
            S.vdr()
            S.op("vector", lambda e, oc=oc: e.tensor_tensor(
                st_c[64:65, 10:11], mk2_sb[64:65, 2 * oc + 1:2 * oc + 2],
                st_c[64:65, 9:10], ALU.max))
            S.vdr()
            tk = S.op("vector", lambda e, oc=oc: e.tensor_copy(
                mk2_sb[64:65, 2 * oc + 1:2 * oc + 2], st_c[64:65, 10:11]),
                inc="VE")
            ps_nb.free = ("VE", tk)

        if slab >= 4:
            S.wd("tensor", qout_free)
            for oc in range(NCK):
                bk = ps_pj[oc % 2]
                S.wd("tensor", bk.free)
                for kc in range(NCK):
                    S.op("tensor", lambda e, oc=oc, kc=kc, bk=bk, hb=hb: e.matmul(
                        bk.ap[:, :],
                        wq_T[:, kc * D + oc * 128: kc * D + oc * 128 + 128],
                        h_sl[hb][:, kc * 512:(kc + 1) * 512],
                        start=(kc == 0), stop=(kc == NCK - 1)))
                bk.ready = ("PE", S.tok("tensor", "PE"))
                S.wd("scalar", bk.ready)
                tk = S.op("scalar", lambda e, oc=oc, bk=bk: e.activation(
                    q_out[:, oc * 512:(oc + 1) * 512], bk.ap[:, :], AF.Identity,
                    bias=bias_q[:, oc:oc + 1]), inc="AC")
                bk.free = ("AC", tk)

        S.wd("vector", vout_free)
        for st4 in range(4):
            S.op("vector", lambda e, st4=st4: e.memset(
                v_out[:, st4 * H * 65:(st4 + 1) * H * 65]
                .rearrange("p (h n) -> p h n", n=65)[:, :, 64:65], 1.0))
        S.vdr()
        vme = S.tok("vector", "VE")
        for st4 in range(4):
            for on in range(2):
                bk = ps_pj[(st4 * 2 + on) % 2]
                S.wd("tensor", bk.free)
                for kc in range(NCK):
                    S.op("tensor", lambda e, st4=st4, on=on, kc=kc, bk=bk, hb=hb:
                         e.matmul(
                             bk.ap[:, :],
                             h_sl[hb][:, kc * 512 + st4 * 128: kc * 512 + st4 * 128 + 128],
                             wv_T[:, kc * D + on * 512: kc * D + on * 512 + 512],
                             start=(kc == 0), stop=(kc == NCK - 1)))
                bk.ready = ("PE", S.tok("tensor", "PE"))
                S.wd("vector", bk.ready)
                if st4 == 0 and on == 0:
                    S.w("vector", "VE", vme)
                tk = S.op("vector", lambda e, st4=st4, on=on, bk=bk: e.tensor_tensor(
                    v_out[:, st4 * H * 65 + on * 8 * 65:
                          st4 * H * 65 + (on + 1) * 8 * 65]
                    .rearrange("p (h n) -> p h n", n=65)[:, :, 0:64],
                    bk.ap[:, :].rearrange("p (h n) -> p h n", n=64),
                    bv_bc[:, on * 512:(on + 1) * 512]
                    .rearrange("p (h n) -> p h n", n=64), ALU.add), inc="VE")
                bk.free = ("VE", tk)
        S.vdr()
        for st4 in range(4):
            S.op("vector", lambda e, st4=st4, slab=slab: e.tensor_scalar(
                v_out[:, st4 * H * 65:(st4 + 1) * H * 65],
                v_out[:, st4 * H * 65:(st4 + 1) * H * 65],
                mask_sb[:, slab * 4 + st4: slab * 4 + st4 + 1], None, ALU.mult))
            S.vdr()
        S.tok("vector", "VE")
        h_proj_done[hb] = ("PE", S.n["PE"])
        vev = S.n["VE"]

        S.w("sync", "AC", S.n["AC"])
        S.w("sync", "VE", vev)
        S.op("sync", lambda e, slab=slab: e.dma_start(
            s_k[:, :, slab * 512:(slab + 1) * 512].rearrange("c p n -> p c n"),
            k_out[:, :].rearrange("p (c n) -> p c n", n=512)), inc="KS", k=16)
        kout_free = ("KS", S.n["KS"])
        if slab >= 4:
            S.op("sync", lambda e, slab=slab: e.dma_start(
                s_q[:, :, (slab - 4) * 512:(slab - 3) * 512].rearrange("c p n -> p c n"),
                q_out[:, :].rearrange("p (c n) -> p c n", n=512)), inc="QS", k=16)
            qout_free = ("QS", S.n["QS"])
        S.op("sync", lambda e, slab=slab: e.dma_start(
            s_v[slab * 4:(slab + 1) * 4, :, :].rearrange("s p n -> p s n"),
            v_out[:, :].rearrange("p (s n) -> p s n", n=H * 65)), inc="VS", k=16)
        vout_free = ("VS", S.n["VS"])
    dA_done = (S.n["KS"], S.n["QS"], S.n["VS"])
    if "sk" in dbg_aps:
        S.w("sync", "KS", dA_done[0])
        S.w("sync", "QS", dA_done[1])
        S.w("sync", "VS", dA_done[2])
        S.op("sync", lambda e: e.dma_start(dbg_aps["sk"], s_k), inc="DS", k=16)
        S.op("sync", lambda e: e.dma_start(dbg_aps["sq"], s_q), inc="DS", k=16)
        S.op("sync", lambda e: e.dma_start(dbg_aps["sv"], s_v), inc="DS", k=16)
        S.op("sync", lambda e: e.dma_start(dbg_aps["mask"], mask_sb[:, :]), inc="DS", k=16)
        dA_done = S.n["DS"]
    S.barrier()
    S.flush()
    PA.close()
    if stop == "a":
        P.close(); es.close(); S.check(); return nc, dbg_aps

    PBC = ExitStack()
    out_attn = sbt("out_attn", [128, NCK * TOWN], BF16, PBC)

    PB = ExitStack()
    q_sb = sbt("q_sb", [128, NCK * TOWN], BF16, PB)
    v_sb = sbt("v_sb", [128, 32 * H * 65], BF16, PB)
    k_pair = [sbt(f"k_pair{i}", [128, T], BF16, PB) for i in range(2)]
    p_t = [sbt(f"p_t{i}", [128, 1024], BF16, PB) for i in range(3)]
    qsq = sbt("qsq", [128, TOWN], BF16, PB)
    bias_all = sbt("bias_all", [128, H * 32], F32, PB)
    rrow = sbt("rrow", [1, 1024], F32, PB)
    rrow_bf = sbt("rrow_bf", [1, 1024], BF16, PB)
    rb_bc = sbt("rb_bc", [128, 512], F32, PB)
    nrm = sbt("nrm", [128, 4], F32, PB)
    crow = sbt("crow", [1, 16], F32, PB)
    c_bf = sbt("c_bf", [1, 16], BF16, PB)
    ps_sc = [Bank(pst(f"ps_sc{i}", [128, 1024], F32, PB)) for i in range(3)]
    ps_av = [Bank(pst(f"ps_av{i}", [65, 512], F32, PB)) for i in range(2)]

    S.w("sync", "KS", dA_done[0])
    S.w("sync", "QS", dA_done[1])
    S.w("sync", "VS", dA_done[2])
    S.op("sync", lambda e: e.dma_start(
        q_sb[:, :].rearrange("p (c n) -> p c n", n=TOWN),
        s_q.rearrange("c p n -> p c n")), inc="DX", k=16)
    S.op("sync", lambda e: e.dma_start(
        v_sb[:, :].rearrange("p (s n) -> p s n", n=H * 65),
        s_v.rearrange("s p n -> p s n")), inc="DX", k=16)
    dQV = S.n["DX"]

    S.w("vector", "DX", dQV)
    qsq_guard = None
    for cc in range(NCK):
        S.wd("vector", qsq_guard)
        S.op("vector", lambda e, cc=cc: e.tensor_tensor(
            qsq[:, :], q_sb[:, cc * TOWN:(cc + 1) * TOWN],
            q_sb[:, cc * TOWN:(cc + 1) * TOWN], ALU.mult))
        vq = S.tok("vector", "VE")
        S.w("tensor", "VE", vq)
        for nq in range(4):
            S.wd("tensor", ps_sc[0].free)
            S.op("tensor", lambda e, nq=nq: e.matmul(
                ps_sc[0].ap[0:1, 0:512], ones_c[0:64, :], qsq[0:64, nq * 512:(nq + 1) * 512],
                start=True, stop=True))
            S.op("tensor", lambda e, nq=nq: e.matmul(
                ps_sc[0].ap[64:65, 0:512], ones_c[64:128, :],
                qsq[64:128, nq * 512:(nq + 1) * 512], start=True, stop=True))
            ps_sc[0].ready = ("PE", S.tok("tensor", "PE"))
            S.wd("vector", ps_sc[0].ready)
            for rr in (slice(0, 1), slice(64, 65)):
                S.op("vector", lambda e, rr=rr: e.tensor_reduce(
                    nrm[rr, 0:1], ps_sc[0].ap[rr, 0:512], AXX, ALU.max))
            S.vdr()
            if nq == 0:
                for rr in (slice(0, 1), slice(64, 65)):
                    S.op("vector", lambda e, rr=rr: e.tensor_copy(
                        nrm[rr, 1:2], nrm[rr, 0:1]))
            else:
                for rr in (slice(0, 1), slice(64, 65)):
                    S.op("vector", lambda e, rr=rr: e.tensor_tensor(
                        nrm[rr, 2:3], nrm[rr, 1:2], nrm[rr, 0:1], ALU.max))
                S.vdr()
                for rr in (slice(0, 1), slice(64, 65)):
                    S.op("vector", lambda e, rr=rr: e.tensor_copy(
                        nrm[rr, 1:2], nrm[rr, 2:3]))
            S.vdr()
            tk = S.tok("vector", "VE")
            ps_sc[0].free = ("VE", tk)
        qsq_guard = ps_sc[0].ready
        S.op("vector", lambda e, cc=cc: e.tensor_tensor(
            nrm[0:1, 3:4], mk2_sb[0:1, 2 * cc:2 * cc + 1], nrm[0:1, 1:2], ALU.add))
        S.vdr()
        S.op("vector", lambda e, cc=cc: e.tensor_copy(
            mk2_sb[0:1, 2 * cc:2 * cc + 1], nrm[0:1, 3:4]))
        S.vdr()
        S.op("vector", lambda e, cc=cc: e.tensor_tensor(
            nrm[64:65, 3:4], mk2_sb[64:65, 2 * cc + 1:2 * cc + 2],
            nrm[64:65, 1:2], ALU.add))
        S.vdr()
        S.op("vector", lambda e, cc=cc: e.tensor_copy(
            mk2_sb[64:65, 2 * cc + 1:2 * cc + 2], nrm[64:65, 3:4]))
        S.vdr()
        S.tok("vector", "VE")
    vQN = S.n["VE"]
    S.w("tensor", "VE", vQN)
    S.wd("tensor", ps_sc[0].free)
    S.op("tensor", lambda e: e.matmul(ps_sc[0].ap[0:1, 0:16], onef_c[64:65, :],
                                      mk2_sb[64:65, 0:16], start=True, stop=True))
    pRM = S.tok("tensor", "PE")
    S.w("vector", "PE", pRM)
    S.op("vector", lambda e: e.tensor_tensor(crow[0:1, :], mk2_sb[0:1, 0:16],
                                             ps_sc[0].ap[0:1, 0:16], ALU.add))
    S.vdr()
    S.op("vector", lambda e: e.tensor_scalar(mk2_sb[0:1, 0:16], crow[0:1, :], 0.5,
                                             None, ALU.mult))
    S.vdr()
    S.op("vector", lambda e: e.tensor_copy(c_bf[0:1, :], mk2_sb[0:1, 0:16]))
    S.vdr()
    vCB = S.tok("vector", "VE")
    ps_sc[0].free = ("VE", vCB)
    S.w("tensor", "VE", vCB)
    S.wd("tensor", ps_sc[0].free)
    S.op("tensor", lambda e: e.matmul(ps_sc[0].ap[:, 0:16], ones_r[:, :], c_bf[0:1, :],
                                      start=True, stop=True))
    pCB = S.tok("tensor", "PE")
    S.w("vector", "PE", pCB)
    S.op("vector", lambda e: e.tensor_scalar(
        bias_all[:, 0:16], ps_sc[0].ap[:, 0:16], -0.5, None, ALU.mult))
    S.vdr()
    vBA = S.tok("vector", "VE")
    ps_sc[0].free = ("VE", vBA)
    S.w("scalar", "VE", vBA)

    kp_free = [None, None]
    kp_dx = [None, None]
    p_free = [None, None, None, None]
    for pr in range(NCK):
        kb = k_pair[pr % 2]
        S.wd("sync", kp_free[pr % 2])
        ksem = "KA" if pr % 2 == 0 else "KB"
        S.op("sync", lambda e, pr=pr, kb=kb: e.dma_start(kb[:, :], s_k[pr]),
             inc=ksem, k=16)
        kp_dx[pr % 2] = S.n[ksem]
        he, ho = 2 * pr, 2 * pr + 1
        for tq in range(4):
            ave = ps_av[0]
            avo = ps_av[1]

            def emit_scores(sl, tq=tq, pr=pr, kb=kb):
                g, par = sl // 2, sl % 2
                bk = ps_sc[sl % 3]
                S.wd("tensor", bk.free)
                if tq == 0 and sl == 0:
                    S.w("tensor", "KA" if pr % 2 == 0 else "KB",
                        kp_dx[pr % 2])
                base = 0 if par == 0 else 64
                for half in range(2):
                    st = 2 * g + half
                    S.op("tensor", lambda e, st=st, tq=tq, pr=pr, kb=kb, bk=bk,
                         base=base, half=half: e.matmul(
                             bk.ap[:, half * 512:(half + 1) * 512],
                             kb[base:base + 64, st * 128:(st + 1) * 128],
                             q_sb[base:base + 64,
                                  pr * TOWN + tq * 512: pr * TOWN + tq * 512 + 512],
                             start=True, stop=True))
                bk.ready = ("PE", S.tok("tensor", "PE"))
                hh = (2 * pr) + par
                S.wd("scalar", bk.ready)
                S.wd("scalar", p_free[sl % 3])
                tk = S.op("scalar", lambda e, hh=hh, bk=bk, sl=sl:
                          e.activation(p_t[sl % 3][:, :], bk.ap[:, :], AF.Exp,
                                       bias=bias_all[:, hh:hh + 1]), inc="AC")
                bk.free = ("AC", tk)
                return tk

            exp_tok = {}
            exp_tok[0] = emit_scores(0)
            for sl in range(32):
                if sl + 1 < 32:
                    exp_tok[sl + 1] = emit_scores(sl + 1)
                g, par = sl // 2, sl % 2
                av = ave if par == 0 else avo
                S.w("tensor", "AC", exp_tok[sl])
                if g == 0:
                    S.wd("tensor", av.free)
                for half in range(2):
                    st = 2 * g + half
                    S.op("tensor", lambda e, st=st, sl=sl, av=av, par=par, pr=pr,
                         half=half: e.matmul(
                             av.ap[:, :],
                             v_sb[:, st * H * 65 + (2 * pr + par) * 65:
                                  st * H * 65 + (2 * pr + par) * 65 + 65],
                             p_t[sl % 3][:, half * 512:(half + 1) * 512],
                             start=(st == 0), stop=(st == 31)))
                av_tok = ("PE", S.tok("tensor", "PE"))
                p_free[sl % 3] = av_tok
                if g == 15:
                    av.ready = av_tok
            for par, hh, av in ((0, he, ave), (1, ho, avo)):
                S.wd("vector", av.ready)
                S.op("vector", lambda e, par=par, av=av: e.reciprocal(
                    rrow[0:1, par * 512:par * 512 + 512], av.ap[64:65, :]))
                S.vdr()
                S.op("vector", lambda e, par=par: e.tensor_copy(
                    rrow_bf[0:1, par * 512:par * 512 + 512],
                    rrow[0:1, par * 512:par * 512 + 512]))
                vr = S.tok("vector", "VE")
                S.w("tensor", "VE", vr)
                bcb = ps_sc[par]
                S.wd("tensor", bcb.free)
                S.op("tensor", lambda e, par=par, bcb=bcb: e.matmul(
                    bcb.ap[0:64, 0:512], ones_r[0:1, 0:64],
                    rrow_bf[0:1, par * 512:par * 512 + 512], start=True, stop=True))
                bcb.ready = ("PE", S.tok("tensor", "PE"))
                S.wd("vector", bcb.ready)
                base = 0 if par == 0 else 64
                tkb = S.op("vector", lambda e, bcb=bcb: e.tensor_copy(
                    rb_bc[0:64, :], bcb.ap[0:64, 0:512]), inc="VE")
                bcb.free = ("VE", tkb)
                S.vdr()
                tk = S.op("vector", lambda e, par=par, hh=hh, tq=tq, base=base,
                          pr=pr, av=av: e.tensor_tensor(
                              out_attn[base:base + 64,
                                       pr * TOWN + tq * 512: pr * TOWN + tq * 512 + 512],
                              av.ap[0:64, :], rb_bc[0:64, :], ALU.mult),
                          inc="VE")
                av.free = ("VE", tk)
        kp_free[pr % 2] = ("PE", S.n["PE"])
    vATT = S.n["VE"]
    if "oat" in dbg_aps:
        S.w("sync", "VE", vATT)
        S.op("sync", lambda e: e.dma_start(dbg_aps["oat"], out_attn[:, :]), inc="DS", k=16)
        S.op("sync", lambda e: e.dma_start(dbg_aps["biasall"], bias_all[:, :]), inc="DS", k=16)
    S.barrier()
    S.flush()
    PB.close()
    if stop == "b":
        PBC.close(); P.close(); es.close(); S.check(); return nc, dbg_aps

    PC = ExitStack()
    wo_T = sbt("wo_T", [128, NCK * D], BF16, PC)
    wraw2 = sbt("wraw2", [128, NCK * D], BF16, PC)
    wo_fm = sbt("wo_fm", [128, NCK * TOWN], BF16, PC)
    xw = [sbt(f"xw{i}", [128, D], F32, PC) for i in range(2)]
    x2t = [sbt(f"x2t{i}", [128, D], F32, PC) for i in range(2)]
    z2t = [sbt(f"z2t{i}", [128, D], BF16, PC) for i in range(2)]
    h2sl = sbt("h2sl", [128, NCK * 512], BF16, PC)
    scr2 = sbt("scr2", [128, D], F32, PC)
    scr2b = sbt("scr2b", [128, D], F32, PC)
    st2 = sbt("st2", [128, 12], F32, PC)
    ps_wo = [Bank(pst(f"ps_wo{i}", [128, 512], F32, PC)) for i in range(2)]
    ps_tc = [Bank(pst(f"ps_tc{i}", [128, 1024], BF16, PC)) for i in range(2)]

    wraw2_free = None
    S.op("gpsimd", lambda e: e.dma_start(
        wraw2[:, :].rearrange("p (c n) -> p c n", n=D),
        wo.rearrange("(c p) n -> p c n", p=128)), inc="WQ", k=16)
    S.w("tensor", "WQ", S.n["WQ"])
    for oc in range(NCK):
        for ic in range(NCK):
            bk = ps_tc[(oc * NCK + ic) % 2]
            S.wd("tensor", bk.free)
            S.op("tensor", lambda e, oc=oc, ic=ic, bk=bk: e.transpose(
                bk.ap[:, 0:128],
                wraw2[:, oc * D + ic * 128: oc * D + ic * 128 + 128], idb[:, :]))
            bk.ready = ("PE", S.tok("tensor", "PE"))
            S.wd("scalar", bk.ready)
            tk = S.op("scalar", lambda e, oc=oc, ic=ic, bk=bk: e.activation(
                wo_T[:, ic * D + oc * 128: ic * D + oc * 128 + 128],
                bk.ap[:, 0:128], AF.Identity), inc="AC")
            bk.free = ("AC", tk)
    aWO = S.n["AC"]

    S.w("tensor", "AC", aWO)
    S.w("tensor", "VE", vATT)
    for oc in range(NCK):
        for tqc in range(4):
            bk = ps_wo[(oc * 4 + tqc) % 2]
            S.wd("tensor", bk.free)
            for kc in range(NCK):
                S.op("tensor", lambda e, oc=oc, tqc=tqc, kc=kc, bk=bk: e.matmul(
                    bk.ap[:, :], wo_T[:, kc * D + oc * 128: kc * D + oc * 128 + 128],
                    out_attn[:, kc * TOWN + tqc * 512: kc * TOWN + tqc * 512 + 512],
                    start=(kc == 0), stop=(kc == NCK - 1)))
            bk.ready = ("PE", S.tok("tensor", "PE"))
            S.wd("scalar", bk.ready)
            tk = S.op("scalar", lambda e, oc=oc, tqc=tqc, bk=bk: e.activation(
                wo_fm[:, oc * TOWN + tqc * 512: oc * TOWN + tqc * 512 + 512],
                bk.ap[:, :], AF.Identity, bias=gabo[:, oc:oc + 1],
                scale=ga_v[:, oc:oc + 1]), inc="AC")
            bk.free = ("AC", tk)
    aWOF = S.n["AC"]

    xw_free = [None, None]
    x2_free = [None, None]
    z2_free = [None, None]
    h2_slab_store = None
    for ti in range(16):
        tl = ti % 4
        xbuf = xw[ti % 2]
        x2b = x2t[ti % 2]
        z2b = z2t[ti % 2]
        S.wd("sync", xw_free[ti % 2])
        xsem = "XA" if ti % 2 == 0 else "XB"
        S.op("sync", lambda e, ti=ti, xbuf=xbuf: e.dma_start(
            xbuf[:, :], xb[TOWN + ti * 128: TOWN + (ti + 1) * 128, :]), inc=xsem,
            k=16)
        dxo = S.n[xsem]
        bk = ps_tc[ti % 2]
        S.wd("tensor", bk.free)
        if ti == 0:
            S.w("tensor", "AC", aWOF)
        for jc in range(NCK):
            S.op("tensor", lambda e, jc=jc, ti=ti, bk=bk: e.transpose(
                bk.ap[:, jc * 128:(jc + 1) * 128],
                wo_fm[:, jc * TOWN + ti * 128: jc * TOWN + ti * 128 + 128],
                idb[:, :]))
        bk.ready = ("PE", S.tok("tensor", "PE"))
        S.wd("vector", bk.ready)
        S.w("vector", xsem, dxo)
        S.wd("vector", x2_free[ti % 2])
        tk = S.op("vector", lambda e, bk=bk, x2b=x2b, xbuf=xbuf: e.tensor_tensor(
            x2b[:, :], bk.ap[:, :], xbuf[:, :], ALU.add), inc="VE")
        bk.free = ("VE", tk)
        xw_free[ti % 2] = ("VE", tk)
        vx2 = tk
        S.w("sync", "VE", vx2)
        x2sem = "X2A" if ti % 2 == 0 else "X2B"
        S.op("sync", lambda e, ti=ti, x2b=x2b: e.dma_start(s_x2[ti], x2b[:, :]),
             inc=x2sem, k=16)
        dsx2 = S.n[x2sem]
        S.w("scalar", "VE", vx2)
        S.op("scalar", lambda e, x2b=x2b: e.activation(
            scr2[:, :], x2b[:, :], AF.Identity, accum_out=st2[:, 0:1]))
        S.op("scalar", lambda e, x2b=x2b: e.activation(
            scr2b[:, :], x2b[:, :], AF.Square, accum_out=st2[:, 1:2]), inc="AC")
        ac1 = S.n["AC"]
        S.w("vector", "AC", ac1)
        S.op("vector", lambda e: e.tensor_scalar(
            st2[:, 2:3], st2[:, 0:1], 1.0 / D, None, ALU.mult))
        S.vdr()
        S.op("vector", lambda e: e.tensor_tensor(
            st2[:, 3:4], st2[:, 2:3], st2[:, 2:3], ALU.mult))
        S.vdr()
        S.op("vector", lambda e: e.scalar_tensor_tensor(
            st2[:, 4:5], st2[:, 1:2], 1.0 / D, st2[:, 3:4], ALU.mult, ALU.subtract))
        S.vdr()
        ve1 = S.tok("vector", "VE")
        S.w("scalar", "VE", ve1)
        S.op("scalar", lambda e: e.activation(
            st2[:, 5:6], st2[:, 4:5], AF.Sqrt, bias=eps_c[:, :]), inc="AC")
        ac2 = S.n["AC"]
        S.w("vector", "AC", ac2)
        S.op("vector", lambda e: e.reciprocal(st2[:, 5:6], st2[:, 5:6]))
        S.vdr()
        S.op("vector", lambda e: e.scalar_tensor_tensor(
            st2[:, 6:7], st2[:, 2:3], -1.0, st2[:, 5:6], ALU.mult, ALU.mult))
        S.vdr()
        ve2 = S.tok("vector", "VE")
        S.w("scalar", "VE", ve2)
        S.wd("scalar", z2_free[ti % 2])
        S.op("scalar", lambda e, x2b=x2b, z2b=z2b: e.activation(
            z2b[:, :], x2b[:, :], AF.Identity, bias=st2[:, 6:7], scale=st2[:, 5:6]),
            inc="AC")
        acz = S.n["AC"]
        x2_free[ti % 2] = [("AC", acz), (x2sem, dsx2)]
        S.w("tensor", "AC", acz)
        bk2 = ps_tc[(ti + 1) % 2]
        S.wd("tensor", bk2.free)
        if tl == 0:
            S.wd("tensor", h2_slab_store)
        for j in range(NCK):
            S.op("tensor", lambda e, j=j, z2b=z2b, bk2=bk2: e.transpose(
                bk2.ap[:, j * 128:(j + 1) * 128], z2b[:, j * 128:(j + 1) * 128],
                idb[:, :]))
        bk2.ready = ("PE", S.tok("tensor", "PE"))
        z2_free[ti % 2] = bk2.ready
        S.wd("scalar", bk2.ready)
        for j in range(NCK):
            S.op("scalar", lambda e, j=j, tl=tl, bk2=bk2: e.activation(
                h2sl[:, j * 512 + tl * 128: j * 512 + tl * 128 + 128],
                bk2.ap[:, j * 128:(j + 1) * 128], AF.Identity,
                bias=sbv_v[:, j:j + 1], scale=cb1p[:, j:j + 1]))
        tk = S.tok("scalar", "AC")
        bk2.free = ("AC", tk)
        if tl == 3:
            sl = ti // 4
            S.w("sync", "AC", tk)
            S.op("sync", lambda e, sl=sl: e.dma_start(
                s_h2[:, :, sl * 512:(sl + 1) * 512].rearrange("c p n -> p c n"),
                h2sl[:, :].rearrange("p (c n) -> p c n", n=512)), inc="H2S", k=16)
            h2_slab_store = ("H2S", S.n["H2S"])
    dC_done = (S.n["X2A"], S.n["X2B"])
    if "x2" in dbg_aps:
        S.w("sync", "X2A", dC_done[0])
        S.w("sync", "X2B", dC_done[1])
        S.w("sync", "H2S", S.n["H2S"])
        S.op("sync", lambda e: e.dma_start(dbg_aps["x2"], s_x2), inc="DS", k=16)
        S.op("sync", lambda e: e.dma_start(dbg_aps["h2"], s_h2), inc="DS", k=16)
    S.barrier()
    S.flush()
    PC.close()
    PBC.close()
    if stop == "c":
        P.close(); es.close(); S.check(); return nc, dbg_aps

    PD = ExitStack()
    h2blk = sbt("h2blk", [128, NCK * 1024], BF16, PD)
    xnblk = sbt("xnblk", [128, NCK * 1024], BF16, PD)
    abelblk = sbt("abelblk", [128, NCK * 1024], BF16, PD)
    moeblk = sbt("moeblk", [128, NCK * 1024], BF16, PD)
    he_sb2 = [sbt(f"he_sb{i}", [128, 16 * 512], BF16, PD) for i in range(2)]
    abwb = sbt("abwb", [128, NCK * D], BF16, PD)
    ew1b = sbt("ew1b", [128, NCK * DFF], BF16, PD)
    ew2s = [sbt(f"ew2s{i}", [128, 16 * 128], BF16, PD) for i in range(2)]
    Ebuf = abwb
    wd_bf = sbt("wd_bf", [EXPN, 1024], BF16, PD)
    r2bc = sbt("r2bc", [128, 1024], BF16, PD)
    m2bc = sbt("m2bc", [128, 1024], BF16, PD)
    gbc = sbt("gbc", [128, 1024], BF16, PD)
    wbc = sbt("wbc", [128, 1024], BF16, PD)
    rowa = sbt("rowa", [1, 1024], F32, PD)
    rowb = sbt("rowb", [1, 1024], F32, PD)
    rowc = sbt("rowc", [1, 1024], F32, PD)
    row_bf = sbt("row_bf", [1, 1024], BF16, PD)
    sq2 = [sbt(f"sq2_{i}", [128, 512], BF16, PD) for i in range(2)]
    ltok = sbt("ltok", [128, 64], F32, PD)
    sgin = sbt("sgin", [128, 64], F32, PD)
    sel = sbt("sel", [128, 64], F32, PD)
    mx8 = sbt("mx8", [128, 8], F32, PD)
    wdc = sbt("wdc", [128, 2], F32, PD)
    wdv = sbt("wdv", [128, 64], F32, PD)
    wrow = sbt("wrow", [1, 1024], BF16, PD)
    lg_fm = sbt("lg_fm", [EXPN, 1024], F32, PD)
    x2r = [sbt("x2r0", [128, D], F32, PD)]
    x2r = [x2r[0], x2r[0]]
    x3 = sbt("x3", [128, D], F32, PD)

    ps_h = [Bank(pst(f"ps_h{i}", [128, 512], F32, PD)) for i in range(2)]
    ps_m = [Bank(pst(f"ps_m{i}", [128, 512], F32, PD)) for i in range(2)]
    ps_g = Bank(pst("ps_g", [128, 512], F32, PD))
    ps_lg = Bank(pst("ps_lg", [EXPN, 512], F32, PD))
    ps_t8 = Bank(pst("ps_t8", [128, 64], F32, PD))
    ps_ty = Bank(pst("ps_ty", [128, 1024], BF16, PD))

    ew1_free = None
    ew2s_free = [None, None]
    e1_loaded = {}

    S.op("gpsimd", lambda e: e.dma_start(
        abwb[:, :].rearrange("p (c n) -> p c n", n=D),
        abel_w.rearrange("(c p) n -> p c n", p=128)), inc="AB", k=16)
    dABW = S.n["AB"]

    def load_e1(tb, ex):
        S.wd("gpsimd", ew1_free)
        S.op("gpsimd", lambda e, ex=ex: e.dma_start(
            ew1b[:, :].rearrange("p (c n) -> p c n", n=DFF),
            ew1[ex].rearrange("(c p) n -> p c n", p=128)), inc="E1", k=16)
        e1_loaded[(tb, ex)] = S.n["E1"]

    load_e1(0, 0)

    h2_free = None
    Ebuf_free = None
    gbc_free = None
    wd_free = None
    ltok_free = None
    x2r_free2 = [None]
    x2r_free = None
    x3_free = None
    out_final = 0
    stats_pe_guard = None
    sq_free = [None, None]
    stg_free = [None, None]

    for tb in range(2):
        S.wd("sync", h2_free)
        if tb == 0:
            S.w("sync", "X2A", dC_done[0])
            S.w("sync", "X2B", dC_done[1])
            S.w("sync", "H2S", S.n["H2S"])
        S.op("sync", lambda e, tb=tb: e.dma_start(
            h2blk[:, :].rearrange("p (c n) -> p c n", n=1024),
            s_h2[:, :, tb * 1024:(tb + 1) * 1024].rearrange("c p n -> p c n")),
            inc="HH", k=16)
        dH2 = S.n["HH"]

        S.w("tensor", "HH", dH2)
        S.wd("tensor", stats_pe_guard)
        for nb in range(2):
            S.wd("tensor", ps_g.free)
            for kc in range(NCK):
                S.op("tensor", lambda e, nb=nb, kc=kc: e.matmul(
                    ps_g.ap[0:1, :], ones_c[:, :],
                    h2blk[:, kc * 1024 + nb * 512: kc * 1024 + nb * 512 + 512],
                    start=(kc == 0), stop=(kc == NCK - 1)))
            ps_g.ready = ("PE", S.tok("tensor", "PE"))
            S.wd("vector", ps_g.ready)
            if tb == 0 and nb == 0:
                S.w("vector", "HH", dH2)
            tk = S.op("vector", lambda e, nb=nb: e.tensor_scalar(
                rowa[0:1, nb * 512:(nb + 1) * 512], ps_g.ap[0:1, :], 1.0 / D, None,
                ALU.mult), inc="VE")
            ps_g.free = ("VE", tk)
        for nb in range(2):
            for kc in range(NCK):
                sqb = sq2[kc % 2]
                S.wd("vector", sq_free[kc % 2])
                S.op("vector", lambda e, nb=nb, kc=kc, sqb=sqb: e.tensor_tensor(
                    sqb[:, :],
                    h2blk[:, kc * 1024 + nb * 512: kc * 1024 + nb * 512 + 512],
                    h2blk[:, kc * 1024 + nb * 512: kc * 1024 + nb * 512 + 512],
                    ALU.mult))
                vsq = S.tok("vector", "VE")
                S.w("tensor", "VE", vsq)
                if kc == 0:
                    S.wd("tensor", ps_g.free)
                S.op("tensor", lambda e, kc=kc, sqb=sqb: e.matmul(
                    ps_g.ap[0:1, :], ones_c[:, :], sqb[:, :],
                    start=(kc == 0), stop=(kc == NCK - 1)))
                sq_free[kc % 2] = ("PE", S.tok("tensor", "PE"))
            ps_g.ready = ("PE", S.tok("tensor", "PE"))
            S.wd("vector", ps_g.ready)
            tk = S.op("vector", lambda e, nb=nb: e.tensor_scalar(
                rowb[0:1, nb * 512:(nb + 1) * 512], ps_g.ap[0:1, :], 1.0 / D, None,
                ALU.mult), inc="VE")
            ps_g.free = ("VE", tk)
        S.op("vector", lambda e: e.tensor_tensor(rowc[0:1, :], rowa[0:1, :],
                                                 rowa[0:1, :], ALU.mult))
        S.vdr()
        S.op("vector", lambda e: e.tensor_tensor(rowb[0:1, :], rowb[0:1, :],
                                                 rowc[0:1, :], ALU.subtract))
        S.vdr()
        vve = S.tok("vector", "VE")
        S.w("scalar", "VE", vve)
        S.op("scalar", lambda e: e.activation(rowb[0:1, :], rowb[0:1, :], AF.Sqrt,
                                              bias=eps_c[0:1, :]), inc="AC")
        aSq = S.n["AC"]
        S.w("vector", "AC", aSq)
        S.op("vector", lambda e: e.reciprocal(rowb[0:1, :], rowb[0:1, :]))
        S.vdr()
        S.op("vector", lambda e: e.tensor_tensor(rowc[0:1, :], rowa[0:1, :],
                                                 rowb[0:1, :], ALU.mult))
        S.op("vector", lambda e: e.tensor_copy(row_bf[0:1, :], rowb[0:1, :]))
        vr2 = S.tok("vector", "VE")
        S.w("tensor", "VE", vr2)
        for nb in range(2):
            S.wd("tensor", ps_g.free)
            S.op("tensor", lambda e, nb=nb: e.matmul(
                ps_g.ap[:, :], ones_r[:, :], row_bf[0:1, nb * 512:(nb + 1) * 512],
                start=True, stop=True))
            ps_g.ready = ("PE", S.tok("tensor", "PE"))
            S.wd("vector", ps_g.ready)
            tk = S.op("vector", lambda e, nb=nb: e.tensor_copy(
                r2bc[:, nb * 512:(nb + 1) * 512], ps_g.ap[:, :]), inc="VE")
            ps_g.free = ("VE", tk)
        S.op("vector", lambda e: e.tensor_copy(row_bf[0:1, :], rowc[0:1, :]))
        vm2 = S.tok("vector", "VE")
        S.w("tensor", "VE", vm2)
        for nb in range(2):
            S.wd("tensor", ps_g.free)
            S.op("tensor", lambda e, nb=nb: e.matmul(
                ps_g.ap[:, :], ones_r[:, :], row_bf[0:1, nb * 512:(nb + 1) * 512],
                start=True, stop=True))
            ps_g.ready = ("PE", S.tok("tensor", "PE"))
            S.wd("vector", ps_g.ready)
            tk = S.op("vector", lambda e, nb=nb: e.tensor_copy(
                m2bc[:, nb * 512:(nb + 1) * 512], ps_g.ap[:, :]), inc="VE")
            ps_g.free = ("VE", tk)
        for kc in range(NCK):
            S.op("vector", lambda e, kc=kc: e.tensor_tensor(
                xnblk[:, kc * 1024:(kc + 1) * 1024],
                h2blk[:, kc * 1024:(kc + 1) * 1024], r2bc[:, :], ALU.mult))
        S.vdr()
        for kc in range(NCK):
            S.op("vector", lambda e, kc=kc: e.tensor_tensor(
                xnblk[:, kc * 1024:(kc + 1) * 1024],
                xnblk[:, kc * 1024:(kc + 1) * 1024], m2bc[:, :], ALU.subtract))
        S.vdr()
        for kc in range(NCK):
            S.op("vector", lambda e, kc=kc: e.tensor_scalar(
                xnblk[:, kc * 1024:(kc + 1) * 1024],
                xnblk[:, kc * 1024:(kc + 1) * 1024], ln_g[:, kc:kc + 1],
                ln_b[:, kc:kc + 1], ALU.mult, ALU.add))
        vXN = S.tok("vector", "VE")
        if tb == 0 and "xn" in dbg_aps:
            S.w("sync", "VE", vXN)
            S.op("sync", lambda e: e.dma_start(dbg_aps["xn"], xnblk[:, :]), inc="DS", k=16)

        S.w("tensor", "VE", vXN)
        S.w("tensor", "DW", nDWsmall)
        for nb in range(2):
            S.wd("tensor", ps_lg.free)
            for kc in range(NCK):
                S.op("tensor", lambda e, nb=nb, kc=kc: e.matmul(
                    ps_lg.ap[:, :], gw_bf[:, kc * EXPN:(kc + 1) * EXPN],
                    xnblk[:, kc * 1024 + nb * 512: kc * 1024 + nb * 512 + 512],
                    start=(kc == 0), stop=(kc == NCK - 1)))
            ps_lg.ready = ("PE", S.tok("tensor", "PE"))
            S.wd("scalar", ps_lg.ready)
            tk = S.op("scalar", lambda e, nb=nb: e.activation(
                lg_fm[:, nb * 512:(nb + 1) * 512], ps_lg.ap[:, :], AF.Identity,
                bias=gate_b_sb[:, :]), inc="AC")
            ps_lg.free = ("AC", tk)
        aLG = S.n["AC"]
        S.w("tensor", "AC", aLG)
        S.wd("tensor", ps_t8.free)
        S.wd("vector", ltok_free)
        for tl in range(8):
            S.op("tensor", lambda e, tl=tl: e.transpose(
                ps_t8.ap[:, tl * 8:(tl + 1) * 8],
                lg_fm[:, tl * 128:(tl + 1) * 128], idf[0:8, 0:8]))
        ps_t8.ready = ("PE", S.tok("tensor", "PE"))
        S.wd("vector", ps_t8.ready)
        tk = S.op("vector", lambda e: e.tensor_copy(ltok[:, :], ps_t8.ap[:, :]),
                  inc="VE")
        S.vdr()
        ps_t8.free = ("VE", tk)
        for tl in range(8):
            S.op("vector", lambda e, tl=tl: e.max(out=mx8[:, :],
                                                  in_=ltok[:, tl * 8:(tl + 1) * 8]))
            S.vdr()
            S.op("vector", lambda e: e.tensor_tensor(
                wdc[:, 0:1], mx8[:, 0:1], mx8[:, 1:2], ALU.add))
            S.vdr()
            S.op("vector", lambda e, tl=tl: e.tensor_scalar(
                wdv[:, tl * 8:(tl + 1) * 8], ltok[:, tl * 8:(tl + 1) * 8],
                2.0, None, ALU.mult))
            S.vdr()
            S.op("vector", lambda e, tl=tl: e.tensor_scalar(
                sgin[:, tl * 8:(tl + 1) * 8], wdv[:, tl * 8:(tl + 1) * 8],
                wdc[:, 0:1], None, ALU.subtract))
            S.vdr()
            S.op("vector", lambda e, tl=tl: e.tensor_scalar(
                sel[:, tl * 8:(tl + 1) * 8], ltok[:, tl * 8:(tl + 1) * 8],
                mx8[:, 1:2], None, ALU.is_ge))
            S.vdr()
        vTOP = S.tok("vector", "VE")
        S.w("scalar", "VE", vTOP)
        aSG = S.op("scalar", lambda e: e.activation(wdv[:, :], sgin[:, :],
                                                    AF.Sigmoid), inc="AC")
        S.w("vector", "AC", aSG)
        S.op("vector", lambda e: e.tensor_tensor(sgin[:, :], sel[:, :], wdv[:, :],
                                                 ALU.mult))
        vWD = S.tok("vector", "VE")
        ltok_free = ("VE", vWD)
        S.w("tensor", "VE", vWD)
        S.wd("vector", wd_free)
        for half in range(2):
            S.wd("tensor", ps_lg.free)
            for tl4 in range(4):
                tl = half * 4 + tl4
                S.op("tensor", lambda e, tl=tl, tl4=tl4: e.transpose(
                    ps_lg.ap[:, tl4 * 128:(tl4 + 1) * 128],
                    sgin[:, tl * 8:(tl + 1) * 8], idf[:, :]))
            ps_lg.ready = ("PE", S.tok("tensor", "PE"))
            S.wd("vector", ps_lg.ready)
            tk = S.op("vector", lambda e, half=half: e.tensor_copy(
                wd_bf[:, half * 512:(half + 1) * 512], ps_lg.ap[:, :]), inc="VE")
            ps_lg.free = ("VE", tk)
        vWDF = S.n["VE"]
        wd_free = ("VE", vWDF)
        if tb == 0 and "wd" in dbg_aps:
            S.w("sync", "VE", vWDF)
            S.op("sync", lambda e: e.dma_start(dbg_aps["wd"], wd_bf[:, :]), inc="DS", k=16)
            S.op("sync", lambda e: e.dma_start(dbg_aps["lg"], lg_fm[:, :]), inc="DS", k=16)

        S.w("tensor", "AB", dABW)
        for oc in range(NCK):
            for nb2 in range(2):
                bk = ps_h[(oc * 2 + nb2) % 2]
                S.wd("tensor", bk.free)
                for kc in range(NCK):
                    S.op("tensor", lambda e, oc=oc, nb2=nb2, kc=kc, bk=bk: e.matmul(
                        bk.ap[:, :],
                        abwb[:, kc * D + oc * 128: kc * D + oc * 128 + 128],
                        xnblk[:, kc * 1024 + nb2 * 512: kc * 1024 + nb2 * 512 + 512],
                        start=(kc == 0), stop=(kc == NCK - 1)))
                bk.ready = ("PE", S.tok("tensor", "PE"))
                S.wd("scalar", bk.ready)
                tk = S.op("scalar", lambda e, oc=oc, nb2=nb2, bk=bk: e.activation(
                    abelblk[:, oc * 1024 + nb2 * 512: oc * 1024 + nb2 * 512 + 512],
                    bk.ap[:, :], AF.Gelu, bias=abel_bb[:, oc:oc + 1]), inc="AC")
                bk.free = ("AC", tk)
        Ebuf_free = ("PE", S.n["PE"])

        he_free = [None, None]
        for ex in range(EXPN):
            S.w("tensor", "VE", vWDF)
            for half in range(2):
                S.wd("tensor", ps_lg.free)
                for tl4 in range(4):
                    tl = half * 4 + tl4
                    S.op("tensor", lambda e, ex=ex, tl=tl, tl4=tl4: e.transpose(
                        ps_lg.ap[0:1, tl4 * 128:(tl4 + 1) * 128],
                        sgin[:, tl * 8 + ex: tl * 8 + ex + 1], idf[:, :]))
                ps_lg.ready = ("PE", S.tok("tensor", "PE"))
                S.wd("vector", ps_lg.ready)
                tk = S.op("vector", lambda e, half=half: e.tensor_copy(
                    wrow[0:1, half * 512:(half + 1) * 512],
                    ps_lg.ap[0:1, 0:512]), inc="VE")
                ps_lg.free = ("VE", tk)
            vWR = S.n["VE"]
            S.w("tensor", "VE", vWR)
            for nb2 in range(2):
                bk = ps_m[nb2 % 2]
                S.wd("tensor", bk.free)
                S.op("tensor", lambda e, nb2=nb2, bk=bk: e.matmul(
                    bk.ap[:, :], ones_r[:, :],
                    wrow[0:1, nb2 * 512:(nb2 + 1) * 512], start=True, stop=True))
                bk.ready = ("PE", S.tok("tensor", "PE"))
                S.wd("vector", bk.ready)
                tk = S.op("vector", lambda e, nb2=nb2, bk=bk: e.tensor_copy(
                    wbc[:, nb2 * 512:(nb2 + 1) * 512], bk.ap[:, :]), inc="VE")
                bk.free = ("VE", tk)
            S.vdr()
            vWBC = S.n["VE"]
            S.w("tensor", "E1", e1_loaded[(tb, ex)])
            for tc in range(2):
                he_sb = he_sb2[tc]
                for hfc in range(16):
                    bk = ps_h[hfc % 2]
                    S.wd("tensor", bk.free)
                    if hfc == 0:
                        S.wd("tensor", he_free[tc])
                    for kc in range(NCK):
                        S.op("tensor", lambda e, hfc=hfc, tc=tc, kc=kc, bk=bk:
                             e.matmul(
                                 bk.ap[:, :],
                                 ew1b[:, kc * DFF + hfc * 128: kc * DFF + hfc * 128 + 128],
                                 xnblk[:, kc * 1024 + tc * 512: kc * 1024 + tc * 512 + 512],
                                 start=(kc == 0), stop=(kc == NCK - 1)))
                    bk.ready = ("PE", S.tok("tensor", "PE"))
                    S.wd("scalar", bk.ready)
                    S.wd("scalar", stg_free[hfc % 2])
                    stg = sq2[hfc % 2]
                    atk = S.op("scalar", lambda e, hfc=hfc, ex=ex, bk=bk,
                               stg=stg: e.activation(
                                   stg[:, :], bk.ap[:, :],
                                   AF.Gelu,
                                   bias=eb1_sb[:, ex * 16 + hfc: ex * 16 + hfc + 1]),
                               inc="AC")
                    bk.free = ("AC", atk)
                    S.w("vector", "AC", atk)
                    if hfc == 0 and tc == 0:
                        S.w("vector", "VE", vWBC)
                    tkh = S.op("vector", lambda e, hfc=hfc, tc=tc, he_sb=he_sb,
                               stg=stg: e.tensor_tensor(
                                   he_sb[:, hfc * 512:(hfc + 1) * 512],
                                   stg[:, :],
                                   wbc[:, tc * 512:(tc + 1) * 512], ALU.mult),
                               inc="VE")
                    stg_free[hfc % 2] = ("VE", tkh)
                vHE = S.n["VE"]
            pHE = S.n["PE"]
            nex = (tb, ex + 1) if ex < EXPN - 1 else ((1, 0) if tb == 0 else None)
            S.w("tensor", "VE", vHE)
            for oc in range(NCK):
                sb2 = ew2s[oc % 2]
                S.wd("gpsimd", ew2s_free[oc % 2])
                ssem = "SA" if oc % 2 == 0 else "SB"
                S.op("gpsimd", lambda e, ex=ex, oc=oc, sb2=sb2: e.dma_start(
                    sb2[:, :], ew2r[ex, oc]), inc=ssem, k=16)
                dE2 = S.n[ssem]
                if nex is not None:
                    if oc == 0:
                        S.wd("gpsimd", ("PE", pHE))
                    S.op("gpsimd", lambda e, nex=nex, oc=oc: e.dma_start(
                        ew1b[:, oc * DFF:(oc + 1) * DFF],
                        ew1[nex[1]][oc * 128:(oc + 1) * 128, :]), inc="E1", k=16)
                    if oc == NCK - 1:
                        e1_loaded[nex] = S.n["E1"]
                S.w("tensor", ssem, dE2)
                for tc in range(2):
                    he_sb = he_sb2[tc]
                    bk = ps_m[(oc * 2 + tc) % 2]
                    S.wd("tensor", bk.free)
                    if ex == 0:
                        S.op("tensor", lambda e, oc=oc, tc=tc, bk=bk: e.matmul(
                            bk.ap[:, :], eb2b[:, oc * 128:(oc + 1) * 128],
                            wd_bf[:, tc * 512:(tc + 1) * 512], start=True,
                            stop=False))
                    for hfc in range(16):
                        S.op("tensor", lambda e, oc=oc, hfc=hfc, tc=tc, bk=bk, ex=ex,
                             sb2=sb2, he_sb=he_sb: e.matmul(
                                 bk.ap[:, :], sb2[:, hfc * 128:(hfc + 1) * 128],
                                 he_sb[:, hfc * 512:(hfc + 1) * 512],
                                 start=(hfc == 0 and ex != 0), stop=(hfc == 15)))
                    bk.ready = ("PE", S.tok("tensor", "PE"))
                    S.wd("vector", bk.ready)
                    if ex == 0:
                        tk = S.op("vector", lambda e, oc=oc, tc=tc, bk=bk:
                                  e.tensor_copy(
                                      moeblk[:, oc * 1024 + tc * 512:
                                             oc * 1024 + tc * 512 + 512],
                                      bk.ap[:, :]), inc="VE")
                    else:
                        tk = S.op("vector", lambda e, oc=oc, tc=tc, bk=bk:
                                  e.tensor_tensor(
                                      moeblk[:, oc * 1024 + tc * 512:
                                             oc * 1024 + tc * 512 + 512],
                                      moeblk[:, oc * 1024 + tc * 512:
                                             oc * 1024 + tc * 512 + 512],
                                      bk.ap[:, :], ALU.add), inc="VE")
                        S.vdr()
                    bk.free = ("VE", tk)
                ew2s_free[oc % 2] = ("PE", S.n["PE"])
            he_free = [("PE", S.n["PE"]), ("PE", S.n["PE"])]
        vMOE = S.n["VE"]
        if tb == 0 and "moe" in dbg_aps:
            S.w("sync", "VE", vMOE)
            S.w("sync", "AC", S.n["AC"])
            S.op("sync", lambda e: e.dma_start(dbg_aps["moe"], moeblk[:, :]), inc="DS", k=16)
            S.op("sync", lambda e: e.dma_start(dbg_aps["abel"], abelblk[:, :]), inc="DS", k=16)

        rot_seq = [(moeblk, he_sb2[0]), (he_sb2[0], xnblk), (xnblk, moeblk)]
        for ri in range(3):
            rcur, rnext = rot_seq[ri]
            S.wd("sync", Ebuf_free)
            S.op("sync", lambda e, ri=ri: e.dma_start(Ebuf[:, :], s_E[ri]),
                 inc="EE", k=16)
            dEb = S.n["EE"]
            S.w("tensor", "EE", dEb)
            if ri == 0:
                S.w("tensor", "VE", vMOE)
            for mc in range(NCK):
                for nb2 in range(2):
                    bk = ps_h[(mc * 2 + nb2) % 2]
                    S.wd("tensor", bk.free)
                    for kc in range(NCK):
                        S.op("tensor", lambda e, mc=mc, nb2=nb2, kc=kc, bk=bk,
                             rcur=rcur: e.matmul(
                                 bk.ap[:, :],
                                 Ebuf[:, kc * D + mc * 128: kc * D + mc * 128 + 128],
                                 rcur[:, kc * 1024 + nb2 * 512:
                                      kc * 1024 + nb2 * 512 + 512],
                                 start=(kc == 0), stop=(kc == NCK - 1)))
                    bk.ready = ("PE", S.tok("tensor", "PE"))
                    S.wd("scalar", bk.ready)
                    tk = S.op("scalar", lambda e, mc=mc, nb2=nb2, bk=bk, rnext=rnext:
                              e.activation(
                                  rnext[:, mc * 1024 + nb2 * 512:
                                        mc * 1024 + nb2 * 512 + 512],
                                  bk.ap[:, :], AF.Identity), inc="AC")
                    bk.free = ("AC", tk)
            Ebuf_free = ("PE", S.n["PE"])
        aROT = S.n["AC"]
        stats_pe_guard = ("PE", S.n["PE"])
        if tb == 0:
            S.wd("gpsimd", Ebuf_free)
            S.op("gpsimd", lambda e: e.dma_start(
                abwb[:, :].rearrange("p (c n) -> p c n", n=D),
                abel_w.rearrange("(c p) n -> p c n", p=128)), inc="AB", k=16)
            dABW = S.n["AB"]

        S.w("tensor", "AC", aROT)
        for nb2 in range(2):
            S.wd("tensor", ps_g.free)
            for kc in range(16):
                src = abelblk if kc < 8 else moeblk
                kcc = kc % 8
                S.op("tensor", lambda e, nb2=nb2, kc=kc, kcc=kcc, src=src: e.matmul(
                    ps_g.ap[0:1, :], gcw_bf[:, kc:kc + 1],
                    src[:, kcc * 1024 + nb2 * 512: kcc * 1024 + nb2 * 512 + 512],
                    start=(kc == 0), stop=(kc == 15)))
            ps_g.ready = ("PE", S.tok("tensor", "PE"))
            S.wd("scalar", ps_g.ready)
            tk = S.op("scalar", lambda e, nb2=nb2: e.activation(
                rowa[0:1, nb2 * 512:(nb2 + 1) * 512], ps_g.ap[0:1, :], AF.Sigmoid,
                bias=gcb_sb[:, :]), inc="AC")
            ps_g.free = ("AC", tk)
        aG = S.n["AC"]
        S.w("vector", "AC", aG)
        S.wd("vector", gbc_free)
        S.op("vector", lambda e: e.tensor_copy(row_bf[0:1, :], rowa[0:1, :]))
        vG = S.tok("vector", "VE")
        S.w("tensor", "VE", vG)
        for nb2 in range(2):
            S.wd("tensor", ps_g.free)
            S.op("tensor", lambda e, nb2=nb2: e.matmul(
                ps_g.ap[:, :], ones_r[:, :], row_bf[0:1, nb2 * 512:(nb2 + 1) * 512],
                start=True, stop=True))
            ps_g.ready = ("PE", S.tok("tensor", "PE"))
            S.wd("vector", ps_g.ready)
            tk = S.op("vector", lambda e, nb2=nb2: e.tensor_copy(
                gbc[:, nb2 * 512:(nb2 + 1) * 512], ps_g.ap[:, :]), inc="VE")
            ps_g.free = ("VE", tk)
        vGB = S.n["VE"]
        if tb == 0 and "rot3" in dbg_aps:
            S.w("sync", "VE", vGB)
            S.w("sync", "AC", S.n["AC"])
            S.op("sync", lambda e: e.dma_start(dbg_aps["rot3"], moeblk[:, :]), inc="DS", k=16)
            S.op("sync", lambda e: e.dma_start(dbg_aps["grow"], gbc[:, :]), inc="DS", k=16)

        ybuf = xnblk
        ytmp = he_sb2[0]
        S.w("vector", "VE", vGB)
        for kc in range(NCK):
            S.op("vector", lambda e, kc=kc, ybuf=ybuf: e.tensor_tensor(
                ybuf[:, kc * 1024:(kc + 1) * 1024],
                moeblk[:, kc * 1024:(kc + 1) * 1024],
                abelblk[:, kc * 1024:(kc + 1) * 1024], ALU.subtract))
        S.vdr()
        for kc in range(NCK):
            S.op("vector", lambda e, kc=kc, ybuf=ybuf, ytmp=ytmp: e.tensor_tensor(
                ytmp[:, kc * 1024:(kc + 1) * 1024],
                ybuf[:, kc * 1024:(kc + 1) * 1024], gbc[:, :], ALU.mult))
        S.vdr()
        for kc in range(NCK):
            S.op("vector", lambda e, kc=kc, ybuf=ybuf, ytmp=ytmp: e.tensor_tensor(
                ybuf[:, kc * 1024:(kc + 1) * 1024],
                ytmp[:, kc * 1024:(kc + 1) * 1024],
                abelblk[:, kc * 1024:(kc + 1) * 1024], ALU.add))
        S.vdr()
        for kc in range(NCK):
            S.op("vector", lambda e, kc=kc, ybuf=ybuf, ytmp=ytmp: e.tensor_tensor(
                ytmp[:, kc * 1024:(kc + 1) * 1024],
                ybuf[:, kc * 1024:(kc + 1) * 1024],
                h2blk[:, kc * 1024:(kc + 1) * 1024], ALU.add))
        S.vdr()
        for kc in range(NCK):
            S.op("vector", lambda e, kc=kc, ybuf=ybuf, ytmp=ytmp: e.tensor_scalar(
                ybuf[:, kc * 1024:(kc + 1) * 1024],
                ytmp[:, kc * 1024:(kc + 1) * 1024], gb_v[:, kc:kc + 1], None,
                ALU.mult))
        vY = S.tok("vector", "VE")
        gbc_free = ("VE", vY)
        h2_free = ("VE", vY)
        he_free = [("VE", vY), he_free[1]]

        S.w("tensor", "VE", vY)
        for tl in range(8):
            ti = tb * 8 + tl
            S.wd("sync", x2r_free)
            xsem = "XA" if tl % 2 == 0 else "XB"
            S.op("sync", lambda e, ti=ti, tl=tl: e.dma_start(
                x2r[tl % 2][:, :], s_x2[ti]), inc=xsem, k=16)
            dX2R = S.n[xsem]
            S.wd("tensor", ps_ty.free)
            for jc in range(NCK):
                S.op("tensor", lambda e, jc=jc, tl=tl, ybuf=ybuf: e.transpose(
                    ps_ty.ap[:, jc * 128:(jc + 1) * 128],
                    ybuf[:, jc * 1024 + tl * 128: jc * 1024 + tl * 128 + 128],
                    idb[:, :]))
            ps_ty.ready = ("PE", S.tok("tensor", "PE"))
            S.wd("vector", ps_ty.ready)
            S.w("vector", xsem, dX2R)
            S.wd("vector", x3_free)
            tk = S.op("vector", lambda e, tl=tl: e.tensor_tensor(
                x3[:, :], ps_ty.ap[:, :], x2r[tl % 2][:, :], ALU.add), inc="VE")
            ps_ty.free = ("VE", tk)
            x2r_free = ("VE", tk)
            S.w("sync", "VE", tk)
            S.op("sync", lambda e, ti=ti: e.dma_start(
                out_hbm[ti * 128:(ti + 1) * 128, :], x3[:, :]), inc="OS", k=16)
            x3_free = ("OS", S.n["OS"])
        out_final = S.n["OS"]
    S.w("sync", "OS", out_final)
    S.flush()
    PD.close()
    P.close()
    es.close()
    S.check()
    return nc, dbg_aps


_CACHE = {}


def _vec_fm(v):
    return np.ascontiguousarray(np.asarray(v, dtype=np.float32).reshape(8, 128).T)


def kernel(**inputs):
    nc, _ = _CACHE.get("nc") or _CACHE.setdefault("nc", build_nc())

    x = np.asarray(inputs["L0_attn_x"], dtype=np.float32)
    BF = ml_dtypes.bfloat16
    common = {
        "wq": np.ascontiguousarray(inputs["L0_attn_wq_w"], dtype=np.float32),
        "wk": np.ascontiguousarray(inputs["L0_attn_wk_w"], dtype=np.float32),
        "wv": np.ascontiguousarray(inputs["L0_attn_wv_w"], dtype=np.float32),
        "wo": np.ascontiguousarray(inputs["L0_attn_wo_w"], dtype=np.float32),
        "abel_w": np.ascontiguousarray(inputs["L0_ffn_abel_w"], dtype=np.float32),
        "gate_w": np.ascontiguousarray(inputs["L0_ffn_moe_gate_w"], dtype=np.float32),
        "ew1": np.ascontiguousarray(inputs["L0_ffn_ew1"], dtype=np.float32),
        "ew2r": np.ascontiguousarray(
            np.asarray(inputs["L0_ffn_ew2"], dtype=np.float32)
            .reshape(EXPN, 16, 128, NCK, 128).transpose(0, 3, 2, 1, 4)
            .reshape(EXPN, NCK, 128, 16 * 128)),
        "gens": np.ascontiguousarray(inputs["L0_ffn_gens"], dtype=np.float32),
        "wmod": np.ascontiguousarray(inputs["L0_wmod"], dtype=np.float32),
        "eb2": np.ascontiguousarray(inputs["L0_ffn_eb2"], dtype=np.float32),
        "bq_f": _vec_fm(inputs["L0_attn_wq_b"]),
        "bk_f": _vec_fm(inputs["L0_attn_wk_b"]),
        "bo_f": _vec_fm(inputs["L0_attn_wo_b"]),
        "bv_row": np.ascontiguousarray(
            np.asarray(inputs["L0_attn_wv_b"], dtype=np.float32)[None, :]),
        "abel_b_f": _vec_fm(inputs["L0_ffn_abel_b"]),
        "ln_g_f": _vec_fm(inputs["L0_ffn_ln_gamma"]),
        "ln_b_f": _vec_fm(inputs["L0_ffn_ln_beta"]),
        "eb1_f": np.ascontiguousarray(
            np.asarray(inputs["L0_ffn_eb1"], dtype=np.float32)
            .reshape(EXPN, 16, 128).transpose(2, 0, 1).reshape(128, EXPN * 16)),
        "gate_b": np.ascontiguousarray(
            np.asarray(inputs["L0_ffn_moe_gate_b"], dtype=np.float32)[:, None]),
        "gcw_f": np.ascontiguousarray(
            np.asarray(inputs["L0_ffn_gc_w"], dtype=np.float32)
            .reshape(16, 128).T),
        "gcb": np.ascontiguousarray(
            np.asarray(inputs["L0_ffn_gc_b"], dtype=np.float32).reshape(1, 1)),
        "ident_bf": np.eye(128, dtype=BF),
        "ident_f": np.eye(128, dtype=np.float32),
        "ones_row_bf": np.ones((1, 128), dtype=BF),
        "ones_col_bf": np.ones((128, 1), dtype=BF),
        "onesf_c": np.ones((128, 1), dtype=np.float32),
        "eye_fm": np.ascontiguousarray(
            np.eye(D, dtype=BF).reshape(NCK, 128, D).transpose(1, 0, 2)
            .reshape(128, NCK * D)),
    }
    cvec = np.asarray(inputs["L0_c"], dtype=np.float32)
    in_maps = []
    for core in range(8):
        b, half = core // 2, core % 2
        other = x[b, (1 - half) * TOWN:(2 - half) * TOWN]
        own = x[b, half * TOWN:(half + 1) * TOWN]
        m = dict(common)
        m["xb"] = np.ascontiguousarray(np.concatenate([other, own], axis=0))
        m["c_f"] = _vec_fm(cvec[b])
        in_maps.append(m)
    res = run_bass_kernel_spmd(nc, in_maps, core_ids=list(range(8)))
    full = np.empty((4, T, D), dtype=np.float32)
    for core in range(8):
        b, half = core // 2, core % 2
        full[b, half * TOWN:(half + 1) * TOWN] = res.results[core]["out"]
    return full


# version 14
# speedup vs baseline: 1.0143x; 1.0143x over previous
import math
import numpy as np
import ml_dtypes
from contextlib import ExitStack
import concourse.bass as bass
import concourse.mybir as mybir
from concourse.bass_utils import run_bass_kernel_spmd

F32 = mybir.dt.float32
BF16 = mybir.dt.bfloat16
AF = mybir.ActivationFunctionType
ALU = mybir.AluOpType
AXX = mybir.AxisListType.X

D = 1024
NCK = 8
T = 4096
TOWN = 2048
H = 16
EXPN = 8
DFF = 2048
EPS = 1e-5
SQS = float(0.125 ** 0.5)
NEG = -1.0e30
ENGS = ("tensor", "vector", "scalar", "sync", "gpsimd")
CF = [1.0 / math.factorial(k) for k in range(14)]


class Sched:
    def __init__(self, blk, sems):
        self.blk, self.sems = blk, sems
        self.q = {e: [] for e in ENGS}
        self.n = {s: 0 for s in sems}
        self.tr = {e: [] for e in ENGS}

    def w(self, eng, sem, val):
        if val is None or val <= 0:
            return
        self.q[eng].append(("w", sem, int(val)))
        self.tr[eng].append(("w", sem, int(val)))

    def wd(self, eng, dep):
        if dep is None:
            return
        if isinstance(dep, list):
            for d in dep:
                self.wd(eng, d)
            return
        self.w(eng, dep[0], dep[1])

    def op(self, eng, fn, inc=None, k=1):
        it = ["o", fn, inc, k]
        self.q[eng].append(it)
        self.tr[eng].append(it)
        if inc:
            self.n[inc] += k
            return self.n[inc]
        return None

    def vdr(self):
        self.op("vector", lambda e: e.drain())

    def tok(self, eng, inc):
        for it in reversed(self.q[eng]):
            if it[0] == "o":
                if it[2] is None:
                    it[2] = inc
                    it[3] = 1
                    self.n[inc] += 1
                    return self.n[inc]
                if it[2] == inc:
                    it[3] += 1
                    self.n[inc] += 1
                    return self.n[inc]
                break
        raise RuntimeError(f"tok({eng},{inc}): no attachable instruction")

    def flush(self):
        for e in ENGS:
            ops = self.q[e]
            self.q[e] = []
            if not ops:
                continue

            def run(engine, ops=ops, sems=self.sems):
                for it in ops:
                    if it[0] == "w":
                        engine.wait_ge(sems[it[1]], it[2])
                    else:
                        ins = it[1](engine)
                        if it[2] is not None:
                            ins.then_inc(sems[it[2]], it[3])
            getattr(self.blk, e)(run)

    def barrier(self):
        snap = dict(self.n)
        own = {"tensor": "PE", "scalar": "AC", "vector": "VE"}
        for e in ENGS:
            for s, v in snap.items():
                if v > 0 and own.get(e) != s:
                    self.w(e, s, v)

    def check(self):
        ptr = {e: 0 for e in ENGS}
        cnt = {s: 0 for s in self.sems}
        total = sum(len(v) for v in self.tr.values())
        done, progress = 0, True
        while progress:
            progress = False
            for e in ENGS:
                tr = self.tr[e]
                while ptr[e] < len(tr):
                    it = tr[ptr[e]]
                    if it[0] == "w" and cnt[it[1]] < it[2]:
                        break
                    if it[0] == "o" and it[2] is not None:
                        cnt[it[2]] += it[3]
                    ptr[e] += 1
                    done += 1
                    progress = True
        if done != total:
            stuck = {e: (ptr[e], len(self.tr[e]),
                         self.tr[e][ptr[e]] if ptr[e] < len(self.tr[e]) else None)
                     for e in ENGS}
            raise RuntimeError(f"schedule deadlock: {stuck} sems={cnt}")


class Bank:
    def __init__(self, ap):
        self.ap = ap
        self.ready = None
        self.free = None


def build_nc(dbg=(), stop=None):
    nc = bass.Bass("TRN2", target_bir_lowering=False)

    def din(name, shape, dt):
        return nc.dram_tensor(name, shape, dt, kind="ExternalInput").ap()

    def dscr(name, shape, dt):
        return nc.dram_tensor(name, shape, dt, kind="Internal").ap()

    xb = din("xb", [T, D], F32)
    wq = din("wq", [D, D], F32)
    wk = din("wk", [D, D], F32)
    wv = din("wv", [D, D], F32)
    wo = din("wo", [D, D], F32)
    abel_w = din("abel_w", [D, D], F32)
    gate_w = din("gate_w", [D, EXPN], F32)
    ew1 = din("ew1", [EXPN, D, DFF], F32)
    ew2r = din("ew2r", [EXPN, NCK, 128, 16 * 128], F32)
    gens = din("gens", [3, D, D], F32)
    wmod = din("wmod", [D, 6 * D], F32)
    eb2 = din("eb2", [EXPN, D], F32)
    bq_f = din("bq_f", [128, NCK], F32)
    bk_f = din("bk_f", [128, NCK], F32)
    bo_f = din("bo_f", [128, NCK], F32)
    bv_row = din("bv_row", [1, D], F32)
    abel_b_f = din("abel_b_f", [128, NCK], F32)
    ln_g_f = din("ln_g_f", [128, NCK], F32)
    ln_b_f = din("ln_b_f", [128, NCK], F32)
    c_f = din("c_f", [128, NCK], F32)
    eb1_f = din("eb1_f", [128, EXPN * 16], F32)
    gate_b = din("gate_b", [EXPN, 1], F32)
    gcw_f = din("gcw_f", [128, 16], F32)
    gcb = din("gcb", [1, 1], F32)
    ident_bf = din("ident_bf", [128, 128], BF16)
    ident_f = din("ident_f", [128, 128], F32)
    ones_row_bf = din("ones_row_bf", [1, 128], BF16)
    ones_col_bf = din("ones_col_bf", [128, 1], BF16)
    onesf_c = din("onesf_c", [128, 1], F32)
    eye_fm = din("eye_fm", [128, NCK * D], BF16)

    out_hbm = nc.dram_tensor("out", [TOWN, D], F32, kind="ExternalOutput").ap()

    s_q = dscr("s_q", [NCK, 128, TOWN], BF16)
    s_k = dscr("s_k", [NCK, 128, T], BF16)
    s_v = dscr("s_v", [32, 128, H * 65], BF16)
    s_E = dscr("s_E", [3, 128, NCK * D], BF16)
    s_x2 = dscr("s_x2", [16, 128, D], F32)
    s_h2 = dscr("s_h2", [NCK, 128, TOWN], BF16)

    dbg_aps = {}
    for name, shape, dt in dbg:
        dbg_aps[name] = nc.dram_tensor("dbg_" + name, shape, dt,
                                       kind="ExternalOutput").ap()

    es = ExitStack()
    sems = {}
    for s in ("DW", "DX", "DS", "PE", "AC", "VE", "WA", "WB", "XA", "XB",
              "KA", "KB", "HH", "EE", "SA", "SB", "E1", "AB", "WQ", "H2S",
              "OS", "KS", "QS", "VS", "X2A", "X2B", "XC", "XD"):
        sems[s] = es.enter_context(nc.semaphore(s))
    blk = es.enter_context(nc.Block())
    S = Sched(blk, sems)

    def sbt(name, shape, dt, stack):
        return stack.enter_context(nc.sbuf_tensor(name, shape, dt))

    def pst(name, shape, dt, stack):
        return stack.enter_context(nc.psum_tensor(name, shape, dt))

    def tap(name, src_fn):
        if name in dbg_aps:
            S.op("sync", lambda e, n=name, f=src_fn: e.dma_start(dbg_aps[n], f()),
                 inc="DS", k=16)

    P = ExitStack()
    idb = sbt("idb", [128, 128], BF16, P)
    idf = sbt("idf", [128, 128], F32, P)
    ones_r = sbt("ones_r", [1, 128], BF16, P)
    ones_c = sbt("ones_c", [128, 1], BF16, P)
    onef_c = sbt("onef_c", [128, 1], F32, P)
    bias_q = sbt("bias_q", [128, NCK], F32, P)
    bias_k = sbt("bias_k", [128, NCK], F32, P)
    bias_o = sbt("bias_o", [128, NCK], F32, P)
    abel_bb = sbt("abel_bb", [128, NCK], F32, P)
    ln_g = sbt("ln_g", [128, NCK], F32, P)
    ln_b = sbt("ln_b", [128, NCK], F32, P)
    c_sb = sbt("c_sb", [128, NCK], F32, P)
    eb1_sb = sbt("eb1_sb", [128, EXPN * 16], F32, P)
    gate_b_sb = sbt("gate_b_sb", [EXPN, 1], F32, P)
    gcb_sb = sbt("gcb_sb", [1, 1], F32, P)
    gcw_bf = sbt("gcw_bf", [128, 16], BF16, P)
    eb2b = sbt("eb2b", [EXPN, D], BF16, P)
    gw_bf = sbt("gw_bf", [128, NCK * EXPN], BF16, P)
    mod_sb = sbt("mod_sb", [128, 64], F32, P)
    ca1p = sbt("ca1p", [128, NCK], F32, P)
    cb1p = sbt("cb1p", [128, NCK], F32, P)
    sa_v = sbt("sa_v", [128, NCK], F32, P)
    sbv_v = sbt("sbv_v", [128, NCK], F32, P)
    ga_v = sbt("ga_v", [128, NCK], F32, P)
    gb_v = sbt("gb_v", [128, NCK], F32, P)
    gabo = sbt("gabo", [128, NCK], F32, P)
    eps_c = sbt("eps_c", [128, 1], F32, P)
    zero_c = sbt("zero_c", [128, 1], F32, P)
    mask_sb = sbt("mask_sb", [128, 32], F32, P)
    mk2_sb = sbt("mk2_sb", [128, 16], F32, P)
    ca0_bc = sbt("ca0_bc", [128, 1], F32, P)
    sa0_bc = sbt("sa0_bc", [128, 1], F32, P)
    bvr_bf = sbt("bvr_bf", [1, D], BF16, P)
    cst = sbt("cst", [128, 4], F32, P)

    for dst, src in ((idb, ident_bf), (idf, ident_f), (ones_r, ones_row_bf),
                     (ones_c, ones_col_bf), (onef_c, onesf_c), (bias_q, bq_f),
                     (bias_k, bk_f), (bias_o, bo_f), (abel_bb, abel_b_f),
                     (ln_g, ln_g_f), (ln_b, ln_b_f), (c_sb, c_f),
                     (eb1_sb, eb1_f), (gate_b_sb, gate_b), (gcb_sb, gcb)):
        S.op("sync", lambda e, d=dst, s=src: e.dma_start(d[:, :], s), inc="DX", k=16)
    nDX0 = S.n["DX"]
    S.op("gpsimd", lambda e: e.dma_start(eb2b[:, :], eb2), inc="DW", k=16)
    S.op("gpsimd", lambda e: e.dma_start(gcw_bf[:, :], gcw_f), inc="DW", k=16)
    S.op("gpsimd", lambda e: e.dma_start(bvr_bf[:, :], bv_row), inc="DW", k=16)
    S.op("gpsimd", lambda e: e.dma_start(
        gw_bf[:, :].rearrange("p (c n) -> p c n", n=EXPN),
        gate_w.rearrange("(c p) n -> p c n", p=128)), inc="DW", k=16)
    nDWsmall = S.n["DW"]

    S.w("vector", "DX", nDX0)
    S.op("vector", lambda e: e.memset(eps_c[:, :], EPS))
    S.op("vector", lambda e: e.memset(zero_c[:, :], 0.0))
    S.op("vector", lambda e: e.memset(cst[:, 0:1], CF[8]))
    S.op("vector", lambda e: e.memset(cst[:, 1:2], CF[9]))
    S.op("vector", lambda e: e.memset(mk2_sb[:, :], 0.0))
    S.op("vector", lambda e: e.tensor_scalar(mod_sb[:, 48:56], bias_q[:, :], SQS,
                                             None, ALU.mult))
    S.op("vector", lambda e: e.drain())
    S.op("vector", lambda e: e.tensor_copy(bias_q[:, :], mod_sb[:, 48:56]))
    S.op("vector", lambda e: e.drain())
    S.op("vector", lambda e: e.tensor_scalar(mod_sb[:, 48:56], bias_k[:, :], SQS,
                                             None, ALU.mult))
    S.op("vector", lambda e: e.drain())
    S.op("vector", lambda e: e.tensor_copy(bias_k[:, :], mod_sb[:, 48:56]))
    vINIT = S.tok("vector", "VE")

    P0 = ExitStack()
    cs_f = sbt("cs_f", [128, NCK], F32, P0)
    wm_buf = [sbt(f"wm{i}", [128, 6 * D], F32, P0) for i in range(2)]
    ps_mod = pst("ps_mod", [128, 64], F32, P0)
    ps_bc0 = pst("ps_bc0", [128, 512], F32, P0)
    t_bc = sbt("t_bc", [1, 8], BF16, P0)

    S.w("scalar", "DX", nDX0)
    aSIL = S.op("scalar", lambda e: e.activation(cs_f[:, :], c_sb[:, :], AF.Silu),
                inc="AC")
    S.w("vector", "VE", vINIT)
    S.op("vector", lambda e: e.memset(mod_sb[:, 0:48], 0.0))
    vMZ = S.tok("vector", "VE")
    wm_free = [None, None]
    acc_tok = vMZ
    for kc in range(NCK):
        b = kc % 2
        S.wd("sync", wm_free[b])
        wsem = "WA" if b == 0 else "WB"
        S.op("sync", lambda e, kc=kc, b=b: e.dma_start(
            wm_buf[b][:, :], wmod[kc * 128:(kc + 1) * 128, :]), inc=wsem, k=16)
        S.w("tensor", wsem, S.n[wsem])
        if kc == 0:
            S.w("tensor", "AC", aSIL)
        S.w("tensor", "VE", acc_tok)
        for oc in range(48):
            S.op("tensor", lambda e, b=b, oc=oc, kc=kc: e.matmul(
                ps_mod[:, oc:oc + 1], wm_buf[b][:, oc * 128:(oc + 1) * 128],
                cs_f[:, kc:kc + 1], start=True, stop=True))
        wm_free[b] = ("PE", S.tok("tensor", "PE"))
        S.wd("vector", wm_free[b])
        S.op("vector", lambda e: e.tensor_tensor(
            mod_sb[:, 0:48], mod_sb[:, 0:48], ps_mod[:, 0:48], ALU.add))
        S.op("vector", lambda e: e.drain())
        acc_tok = S.tok("vector", "VE")
    pMOD = S.n["PE"]

    S.w("vector", "VE", vINIT)
    S.w("vector", "PE", pMOD)
    S.op("vector", lambda e: e.tensor_scalar(ca1p[:, :], mod_sb[:, 8:16], 1.0, None,
                                             ALU.add))
    S.op("vector", lambda e: e.tensor_scalar(cb1p[:, :], mod_sb[:, 32:40], 1.0, None,
                                             ALU.add))
    S.op("vector", lambda e: e.tensor_copy(sa_v[:, :], mod_sb[:, 0:8]))
    S.op("vector", lambda e: e.tensor_copy(sbv_v[:, :], mod_sb[:, 24:32]))
    for gdst, c0 in ((ga_v, 16), (gb_v, 40)):
        S.op("vector", lambda e, c0=c0: e.tensor_scalar(
            mod_sb[:, 48:56], mod_sb[:, c0:c0 + 8], -1.0, None, ALU.mult))
        S.vdr()
        S.op("vector", lambda e, c0=c0: e.tensor_tensor(
            mod_sb[:, 56:64], mod_sb[:, 48:56], mod_sb[:, c0:c0 + 8], ALU.max))
        S.vdr()
        S.op("vector", lambda e, g=gdst: e.tensor_scalar(
            g[:, :], mod_sb[:, 56:64], 0.25, None, ALU.max))
        S.vdr()
        S.op("vector", lambda e, c0=c0: e.tensor_scalar(
            mod_sb[:, 48:56], mod_sb[:, c0:c0 + 8], zero_c[:, :], None, ALU.is_ge))
        S.vdr()
        S.op("vector", lambda e: e.tensor_scalar(
            mod_sb[:, 56:64], mod_sb[:, 48:56], 2.0, None, ALU.mult))
        S.vdr()
        S.op("vector", lambda e: e.tensor_scalar(
            mod_sb[:, 48:56], mod_sb[:, 56:64], -1.0, None, ALU.add))
        S.vdr()
        S.op("vector", lambda e, g=gdst: e.tensor_tensor(
            mod_sb[:, 56:64], g[:, :], mod_sb[:, 48:56], ALU.mult))
        S.vdr()
        S.op("vector", lambda e, g=gdst: e.tensor_copy(g[:, :], mod_sb[:, 56:64]))
        S.vdr()
    S.op("vector", lambda e: e.tensor_tensor(gabo[:, :], ga_v[:, :], bias_o[:, :],
                                             ALU.mult))
    S.op("vector", lambda e: e.tensor_copy(t_bc[0:1, 0:1], ca1p[0:1, 0:1]))
    S.op("vector", lambda e: e.tensor_copy(t_bc[0:1, 1:2], sa_v[0:1, 0:1]))
    vPAR = S.tok("vector", "VE")

    S.w("tensor", "VE", vPAR)
    S.w("tensor", "DW", nDWsmall)
    S.op("tensor", lambda e: e.matmul(ps_bc0[:, 0:2], ones_r[:, :], t_bc[0:1, 0:2],
                                      start=True, stop=True))
    pB0 = S.tok("tensor", "PE")
    S.w("vector", "PE", pB0)
    S.op("vector", lambda e: e.tensor_copy(ca0_bc[:, :], ps_bc0[:, 0:1]))
    S.op("vector", lambda e: e.tensor_copy(sa0_bc[:, :], ps_bc0[:, 1:2]))
    vSETUP = S.tok("vector", "VE")
    if "mod" in dbg_aps:
        S.w("sync", "VE", vSETUP)
        S.op("sync", lambda e: e.dma_start(dbg_aps["mod"], mod_sb[:, 0:48]), inc="DS", k=16)
        S.op("sync", lambda e: e.dma_start(dbg_aps["ga"], ga_v[:, :]), inc="DS", k=16)
        S.op("sync", lambda e: e.dma_start(dbg_aps["gb"], gb_v[:, :]), inc="DS", k=16)
        S.op("sync", lambda e: e.dma_start(dbg_aps["ca1p"], ca1p[:, :]), inc="DS", k=16)
        S.op("sync", lambda e: e.dma_start(dbg_aps["sa"], sa_v[:, :]), inc="DS", k=16)
    S.barrier()
    S.flush()
    P0.close()
    if stop == "p0":
        P.close(); es.close(); S.check(); return nc, dbg_aps

    PX = ExitStack()
    A_sb = sbt("A_sb", [128, NCK * D], BF16, PX)
    X2 = sbt("X2", [128, NCK * D], BF16, PX)
    X4 = sbt("X4", [128, NCK * D], BF16, PX)
    X6 = sbt("X6", [128, NCK * D], BF16, PX)
    TMP = sbt("TMP", [128, NCK * D], BF16, PX)
    WES = sbt("WES", [128, NCK * D], BF16, PX)
    EB = sbt("EB", [128, NCK * D], BF16, PX)
    OB = sbt("OB", [128, NCK * D], BF16, PX)
    EYE = sbt("EYE", [128, NCK * D], BF16, PX)
    ps_x = [Bank(pst(f"ps_x{i}", [128, 512], F32, PX)) for i in range(2)]

    S.op("sync", lambda e: e.dma_start(EYE[:, :], eye_fm), inc="DX", k=16)
    dEYE = S.n["DX"]

    def mm_big(dst, lhs, rhs, scale):
        for mc in range(NCK):
            for nb in range(2):
                bk = ps_x[(mc * 2 + nb) % 2]
                S.wd("tensor", bk.free)
                for kc in range(NCK):
                    S.op("tensor", lambda e, mc=mc, nb=nb, kc=kc, bk=bk, lhs=lhs,
                         rhs=rhs: e.matmul(
                             bk.ap[:, :],
                             lhs[:, kc * D + mc * 128: kc * D + mc * 128 + 128],
                             rhs[:, kc * D + nb * 512: kc * D + nb * 512 + 512],
                             start=(kc == 0), stop=(kc == NCK - 1)))
                bk.ready = ("PE", S.tok("tensor", "PE"))
                S.wd("scalar", bk.ready)
                tk = S.op("scalar", lambda e, mc=mc, nb=nb, bk=bk, dst=dst,
                          scale=scale: e.activation(
                              dst[:, mc * D + nb * 512: mc * D + nb * 512 + 512],
                              bk.ap[:, :], AF.Identity, scale=scale), inc="AC")
                bk.free = ("AC", tk)
        return S.n["AC"]

    def stt(dst, in0, sc, in1):
        S.op("vector", lambda e, dst=dst, in0=in0, sc=sc, in1=in1:
             e.scalar_tensor_tensor(dst[:, :], in0[:, :], sc, in1[:, :],
                                    ALU.mult, ALU.add))
        S.vdr()

    for gi in range(3):
        S.op("gpsimd", lambda e, gi=gi: e.dma_start(
            A_sb[:, :].rearrange("p (c n) -> p c n", n=D),
            gens[gi].rearrange("(c p) n -> p c n", p=128)), inc="WQ", k=16)
        S.w("tensor", "WQ", S.n["WQ"])
        if gi == 0:
            S.w("tensor", "VE", vSETUP)
            S.w("scalar", "VE", vSETUP)
        a2 = mm_big(X2, A_sb, A_sb, -1.0)
        if stop == "xa2":
            S.flush(); PX.close(); P.close(); es.close(); S.check()
            return nc, dbg_aps
        S.w("tensor", "AC", a2)
        a4 = mm_big(X4, X2, X2, 1.0)
        S.w("tensor", "AC", a4)
        a6 = mm_big(X6, X2, X4, 1.0)
        S.w("vector", "AC", a6)
        if gi == 0:
            S.w("vector", "DX", dEYE)
        stt(TMP, X4, CF[10] / CF[8], X2)
        stt(TMP, X6, CF[12] / CF[8], TMP)
        v1 = S.tok("vector", "VE")
        S.w("tensor", "VE", v1)
        we = mm_big(WES, X6, TMP, cst[:, 0:1])
        S.w("vector", "AC", we)
        stt(TMP, X4, CF[4] / CF[2], X2)
        stt(TMP, X6, CF[6] / CF[2], TMP)
        stt(EB, TMP, CF[2], WES)
        S.op("vector", lambda e: e.tensor_tensor(EB[:, :], EB[:, :], EYE[:, :],
                                                 ALU.add))
        stt(TMP, X4, CF[11] / CF[9], X2)
        stt(TMP, X6, CF[13] / CF[9], TMP)
        v2 = S.tok("vector", "VE")
        S.w("tensor", "VE", v2)
        ws = mm_big(WES, X6, TMP, cst[:, 1:2])
        S.w("vector", "AC", ws)
        stt(TMP, X4, CF[5] / CF[3], X2)
        stt(TMP, X6, CF[7] / CF[3], TMP)
        stt(TMP, TMP, CF[3], WES)
        S.op("vector", lambda e: e.tensor_tensor(TMP[:, :], TMP[:, :], EYE[:, :],
                                                 ALU.add))
        v3 = S.tok("vector", "VE")
        S.w("tensor", "VE", v3)
        ob = mm_big(OB, A_sb, TMP, -1.0)
        S.w("vector", "AC", ob)
        S.op("vector", lambda e: e.tensor_tensor(EB[:, :], EB[:, :], OB[:, :],
                                                 ALU.add))
        v4 = S.tok("vector", "VE")
        S.w("sync", "VE", v4)
        S.op("sync", lambda e, gi=gi: e.dma_start(s_E[gi], EB[:, :]), inc="DS", k=16)
        S.w("gpsimd", "AC", ob)
        S.w("vector", "DS", S.n["DS"])
        if stop == "xg0":
            S.flush(); PX.close(); P.close(); es.close(); S.check()
            return nc, dbg_aps
    if "E0" in dbg_aps:
        S.w("sync", "DS", S.n["DS"])
        for gi in range(3):
            S.op("sync", lambda e, gi=gi: e.dma_start(dbg_aps[f"E{gi}"], s_E[gi]),
                 inc="DS", k=16)
    S.barrier()
    S.flush()
    PX.close()
    if stop == "x":
        P.close(); es.close(); S.check(); return nc, dbg_aps

    PA = ExitStack()
    wq_T = sbt("wq_T", [128, NCK * D], BF16, PA)
    wk_T = sbt("wk_T", [128, NCK * D], BF16, PA)
    wv_T = sbt("wv_T", [128, NCK * D], BF16, PA)
    wraw = sbt("wraw", [128, NCK * D], BF16, PA)
    x_t = [sbt(f"x_t{i}", [128, D], F32, PA) for i in range(4)]
    scr = sbt("scr", [128, D], F32, PA)
    scr_b = sbt("scr_b", [128, D], F32, PA)
    z_t = [sbt(f"z_t{i}", [128, D], BF16, PA) for i in range(4)]
    h_sl = [sbt(f"h_sl{i}", [128, NCK * 512], BF16, PA) for i in range(2)]
    k_out = sbt("k_out", [128, NCK * 512], BF16, PA)
    q_out = sbt("q_out", [128, NCK * 512], BF16, PA)
    v_out = sbt("v_out", [128, 4 * H * 65], BF16, PA)
    ksq = sbt("ksq", [128, NCK * 512], BF16, PA)
    st_cs = [sbt(f"st_c{i}", [128, 12], F32, PA) for i in range(4)]
    bv_bc = sbt("bv_bc", [128, D], F32, PA)
    ps_tr = [Bank(pst(f"ps_tr{i}", [128, 1024], BF16, PA)) for i in range(2)]
    ps_pj = [Bank(pst(f"ps_pj{i}", [128, 512], F32, PA)) for i in range(2)]
    ps_nb = Bank(pst("ps_nb", [128, 512], F32, PA))

    wraw_free = None
    for wsrc, wdst, wscale in ((wq, wq_T, SQS), (wk, wk_T, SQS), (wv, wv_T, 1.0)):
        S.wd("gpsimd", wraw_free)
        S.op("gpsimd", lambda e, wsrc=wsrc: e.dma_start(
            wraw[:, :].rearrange("p (c n) -> p c n", n=D),
            wsrc.rearrange("(c p) n -> p c n", p=128)), inc="WQ", k=16)
        S.w("tensor", "WQ", S.n["WQ"])
        for oc in range(NCK):
            for ic in range(NCK):
                bk = ps_tr[(oc * NCK + ic) % 2]
                S.wd("tensor", bk.free)
                S.op("tensor", lambda e, oc=oc, ic=ic, bk=bk: e.transpose(
                    bk.ap[:, 0:128],
                    wraw[:, oc * D + ic * 128: oc * D + ic * 128 + 128], idb[:, :]))
                bk.ready = ("PE", S.tok("tensor", "PE"))
                S.wd("scalar", bk.ready)
                tk = S.op("scalar", lambda e, oc=oc, ic=ic, bk=bk, wdst=wdst,
                          wscale=wscale: e.activation(
                              wdst[:, ic * D + oc * 128: ic * D + oc * 128 + 128],
                              bk.ap[:, 0:128], AF.Identity, scale=wscale), inc="AC")
                bk.free = ("AC", tk)
        wraw_free = ("AC", S.n["AC"])
    aWT = S.n["AC"]
    if "wkT" in dbg_aps:
        S.w("sync", "AC", aWT)
        S.op("sync", lambda e: e.dma_start(dbg_aps["wkT"], wk_T[:, :]), inc="DS", k=16)
        S.op("sync", lambda e: e.dma_start(dbg_aps["wqT"], wq_T[:, :]), inc="DS", k=16)
        S.op("sync", lambda e: e.dma_start(dbg_aps["wvT"], wv_T[:, :]), inc="DS", k=16)

    S.w("tensor", "VE", vSETUP)
    for half in range(2):
        S.wd("tensor", ps_nb.free)
        S.op("tensor", lambda e, half=half: e.matmul(
            ps_nb.ap[:, :], ones_r[:, :], bvr_bf[0:1, half * 512:(half + 1) * 512],
            start=True, stop=True))
        ps_nb.ready = ("PE", S.tok("tensor", "PE"))
        S.wd("vector", ps_nb.ready)
        tk = S.op("vector", lambda e, half=half: e.tensor_copy(
            bv_bc[:, half * 512:(half + 1) * 512], ps_nb.ap[:, :]), inc="VE")
        ps_nb.free = ("VE", tk)

    x_free = [None, None, None, None]
    z_free = [None, None, None, None]
    h_proj_done = [None, None]
    kout_free = None
    qout_free = None
    vout_free = None

    XSEMS = ("XA", "XB", "XC", "XD")
    for slab in range(NCK):
        hb = slab % 2
        base = slab * 4
        dxs = []
        for tt in range(4):
            ti = base + tt
            S.wd("sync", x_free[tt])
            S.op("sync", lambda e, ti=ti, tt=tt: e.dma_start(
                x_t[tt][:, :], xb[ti * 128:(ti + 1) * 128, :]), inc=XSEMS[tt],
                k=16)
            dxs.append(S.n[XSEMS[tt]])
        if slab == 0:
            S.w("scalar", "VE", vSETUP)
            S.w("vector", "VE", vSETUP)
        ac1s = []
        for tt in range(4):
            S.w("scalar", XSEMS[tt], dxs[tt])
            S.op("scalar", lambda e, tt=tt: e.activation(
                scr[:, :], x_t[tt][:, :], AF.Identity,
                accum_out=st_cs[tt][:, 0:1]))
            S.op("scalar", lambda e, tt=tt: e.activation(
                scr_b[:, :], x_t[tt][:, :], AF.Square,
                accum_out=st_cs[tt][:, 1:2]), inc="AC")
            ac1s.append(S.n["AC"])
        ve1s = [None] * 4
        for step in range(3):
            for tt in range(4):
                st = st_cs[tt]
                if step == 0:
                    S.w("vector", "AC", ac1s[tt])
                    S.op("vector", lambda e, st=st: e.tensor_scalar(
                        st[:, 2:3], st[:, 0:1], 1.0 / D, None, ALU.mult))
                elif step == 1:
                    S.op("vector", lambda e, st=st: e.tensor_tensor(
                        st[:, 3:4], st[:, 2:3], st[:, 2:3], ALU.mult))
                else:
                    S.op("vector", lambda e, st=st: e.scalar_tensor_tensor(
                        st[:, 4:5], st[:, 1:2], 1.0 / D, st[:, 3:4], ALU.mult,
                        ALU.subtract))
                    ve1s[tt] = S.tok("vector", "VE")
        acsq = []
        for tt in range(4):
            S.w("scalar", "VE", ve1s[tt])
            S.op("scalar", lambda e, tt=tt: e.activation(
                st_cs[tt][:, 5:6], st_cs[tt][:, 4:5], AF.Sqrt,
                bias=eps_c[:, :]), inc="AC")
            acsq.append(S.n["AC"])
        vez = [None] * 4
        vex = [None] * 4
        for step in range(9):
            for tt in range(4):
                st = st_cs[tt]
                if step == 0:
                    S.w("vector", "AC", acsq[tt])
                    S.op("vector", lambda e, st=st: e.reciprocal(
                        st[:, 5:6], st[:, 5:6]))
                elif step == 1:
                    S.op("vector", lambda e, st=st: e.scalar_tensor_tensor(
                        st[:, 6:7], st[:, 2:3], -1.0, st[:, 5:6], ALU.mult,
                        ALU.mult))
                    vez[tt] = S.tok("vector", "VE")
                elif step == 2:
                    S.op("vector", lambda e, st=st, tt=tt: e.tensor_scalar(
                        st[:, 7:8], x_t[tt][:, 0:1], st[:, 2:3], None,
                        ALU.subtract))
                    vex[tt] = S.tok("vector", "VE")
                elif step == 3:
                    S.op("vector", lambda e, st=st: e.tensor_tensor(
                        st[:, 8:9], st[:, 7:8], st[:, 5:6], ALU.mult))
                elif step == 4:
                    S.op("vector", lambda e, st=st: e.tensor_tensor(
                        st[:, 7:8], st[:, 8:9], ca0_bc[:, :], ALU.mult))
                elif step == 5:
                    S.op("vector", lambda e, st=st: e.tensor_tensor(
                        st[:, 8:9], st[:, 7:8], sa0_bc[:, :], ALU.add))
                elif step == 6:
                    S.op("vector", lambda e, st=st: e.tensor_scalar(
                        st[:, 7:8], st[:, 8:9], zero_c[:, :], None,
                        ALU.is_equal))
                elif step == 7:
                    S.op("vector", lambda e, st=st: e.tensor_scalar(
                        st[:, 9:10], st[:, 7:8], -1.0, None, ALU.mult))
                else:
                    S.op("vector", lambda e, st=st, ti=base + tt: e.tensor_scalar(
                        mask_sb[:, ti:ti + 1], st[:, 9:10], 1.0, None, ALU.add))
        acz = [None] * 4
        for tt in range(4):
            S.w("scalar", "VE", vez[tt])
            S.wd("scalar", z_free[tt])
            S.op("scalar", lambda e, tt=tt: e.activation(
                z_t[tt][:, :], x_t[tt][:, :], AF.Identity,
                bias=st_cs[tt][:, 6:7], scale=st_cs[tt][:, 5:6]), inc="AC")
            acz[tt] = S.n["AC"]
            x_free[tt] = [("AC", acz[tt]), ("VE", vex[tt])]
        for tt in range(4):
            S.w("tensor", "AC", acz[tt])
            bk = ps_tr[tt % 2]
            S.wd("tensor", bk.free)
            if tt == 0:
                S.wd("tensor", h_proj_done[hb])
            for j in range(NCK):
                S.op("tensor", lambda e, j=j, tt=tt, bk=bk: e.transpose(
                    bk.ap[:, j * 128:(j + 1) * 128],
                    z_t[tt][:, j * 128:(j + 1) * 128], idb[:, :]))
            bk.ready = ("PE", S.tok("tensor", "PE"))
            z_free[tt] = bk.ready
            S.wd("scalar", bk.ready)
            for j in range(NCK):
                S.op("scalar", lambda e, j=j, tt=tt, hb=hb, bk=bk: e.activation(
                    h_sl[hb][:, j * 512 + tt * 128: j * 512 + tt * 128 + 128],
                    bk.ap[:, j * 128:(j + 1) * 128], AF.Identity,
                    bias=sa_v[:, j:j + 1], scale=ca1p[:, j:j + 1]))
            tk = S.tok("scalar", "AC")
            bk.free = ("AC", tk)
        h_ready = S.n["AC"]
        if slab == 0 and "hsl0" in dbg_aps:
            S.w("sync", "AC", h_ready)
            S.op("sync", lambda e, hb=hb: e.dma_start(dbg_aps["hsl0"], h_sl[hb][:, :]),
                 inc="DS", k=16)
            S.op("sync", lambda e: e.dma_start(dbg_aps["stc0"], st_cs[0][:, 0:12]), inc="DS", k=16)

        S.w("tensor", "AC", max(h_ready, aWT))
        S.wd("tensor", kout_free)
        for oc in range(NCK):
            bk = ps_pj[oc % 2]
            S.wd("tensor", bk.free)
            for kc in range(NCK):
                S.op("tensor", lambda e, oc=oc, kc=kc, bk=bk, hb=hb: e.matmul(
                    bk.ap[:, :], wk_T[:, kc * D + oc * 128: kc * D + oc * 128 + 128],
                    h_sl[hb][:, kc * 512:(kc + 1) * 512],
                    start=(kc == 0), stop=(kc == NCK - 1)))
            bk.ready = ("PE", S.tok("tensor", "PE"))
            S.wd("scalar", bk.ready)
            tk = S.op("scalar", lambda e, oc=oc, bk=bk: e.activation(
                k_out[:, oc * 512:(oc + 1) * 512], bk.ap[:, :], AF.Identity,
                bias=bias_k[:, oc:oc + 1]), inc="AC")
            bk.free = ("AC", tk)
        ack = S.n["AC"]
        if slab == 0 and "kout0" in dbg_aps:
            S.w("sync", "AC", ack)
            S.op("sync", lambda e: e.dma_start(dbg_aps["kout0"], k_out[:, :]), inc="DS", k=16)
        S.w("vector", "AC", ack)
        S.op("vector", lambda e: e.tensor_tensor(ksq[:, :], k_out[:, :], k_out[:, :],
                                                 ALU.mult))
        vks = S.tok("vector", "VE")
        S.w("tensor", "VE", vks)
        for oc in range(NCK):
            S.wd("tensor", ps_nb.free)
            S.op("tensor", lambda e, oc=oc: e.matmul(
                ps_nb.ap[0:1, :], ones_c[0:64, :], ksq[0:64, oc * 512:(oc + 1) * 512],
                start=True, stop=True))
            S.op("tensor", lambda e, oc=oc: e.matmul(
                ps_nb.ap[64:65, :], ones_c[64:128, :],
                ksq[64:128, oc * 512:(oc + 1) * 512], start=True, stop=True))
            ps_nb.ready = ("PE", S.tok("tensor", "PE"))
            S.wd("vector", ps_nb.ready)
            S.op("vector", lambda e: e.tensor_reduce(st_cs[0][:, 9:10], ps_nb.ap[:, :],
                                                     AXX, ALU.max))
            S.vdr()
            S.op("vector", lambda e, oc=oc: e.tensor_tensor(
                st_cs[0][0:1, 10:11], mk2_sb[0:1, 2 * oc:2 * oc + 1],
                st_cs[0][0:1, 9:10], ALU.max))
            S.vdr()
            S.op("vector", lambda e, oc=oc: e.tensor_copy(
                mk2_sb[0:1, 2 * oc:2 * oc + 1], st_cs[0][0:1, 10:11]))
            S.vdr()
            S.op("vector", lambda e, oc=oc: e.tensor_tensor(
                st_cs[0][64:65, 10:11], mk2_sb[64:65, 2 * oc + 1:2 * oc + 2],
                st_cs[0][64:65, 9:10], ALU.max))
            S.vdr()
            tk = S.op("vector", lambda e, oc=oc: e.tensor_copy(
                mk2_sb[64:65, 2 * oc + 1:2 * oc + 2], st_cs[0][64:65, 10:11]),
                inc="VE")
            ps_nb.free = ("VE", tk)

        if slab >= 4:
            S.wd("tensor", qout_free)
            for oc in range(NCK):
                bk = ps_pj[oc % 2]
                S.wd("tensor", bk.free)
                for kc in range(NCK):
                    S.op("tensor", lambda e, oc=oc, kc=kc, bk=bk, hb=hb: e.matmul(
                        bk.ap[:, :],
                        wq_T[:, kc * D + oc * 128: kc * D + oc * 128 + 128],
                        h_sl[hb][:, kc * 512:(kc + 1) * 512],
                        start=(kc == 0), stop=(kc == NCK - 1)))
                bk.ready = ("PE", S.tok("tensor", "PE"))
                S.wd("scalar", bk.ready)
                tk = S.op("scalar", lambda e, oc=oc, bk=bk: e.activation(
                    q_out[:, oc * 512:(oc + 1) * 512], bk.ap[:, :], AF.Identity,
                    bias=bias_q[:, oc:oc + 1]), inc="AC")
                bk.free = ("AC", tk)

        S.wd("vector", vout_free)
        for st4 in range(4):
            S.op("vector", lambda e, st4=st4: e.memset(
                v_out[:, st4 * H * 65:(st4 + 1) * H * 65]
                .rearrange("p (h n) -> p h n", n=65)[:, :, 64:65], 1.0))
        S.vdr()
        vme = S.tok("vector", "VE")
        for st4 in range(4):
            for on in range(2):
                bk = ps_pj[(st4 * 2 + on) % 2]
                S.wd("tensor", bk.free)
                for kc in range(NCK):
                    S.op("tensor", lambda e, st4=st4, on=on, kc=kc, bk=bk, hb=hb:
                         e.matmul(
                             bk.ap[:, :],
                             h_sl[hb][:, kc * 512 + st4 * 128: kc * 512 + st4 * 128 + 128],
                             wv_T[:, kc * D + on * 512: kc * D + on * 512 + 512],
                             start=(kc == 0), stop=(kc == NCK - 1)))
                bk.ready = ("PE", S.tok("tensor", "PE"))
                S.wd("vector", bk.ready)
                if st4 == 0 and on == 0:
                    S.w("vector", "VE", vme)
                tk = S.op("vector", lambda e, st4=st4, on=on, bk=bk: e.tensor_tensor(
                    v_out[:, st4 * H * 65 + on * 8 * 65:
                          st4 * H * 65 + (on + 1) * 8 * 65]
                    .rearrange("p (h n) -> p h n", n=65)[:, :, 0:64],
                    bk.ap[:, :].rearrange("p (h n) -> p h n", n=64),
                    bv_bc[:, on * 512:(on + 1) * 512]
                    .rearrange("p (h n) -> p h n", n=64), ALU.add), inc="VE")
                bk.free = ("VE", tk)
        S.vdr()
        for st4 in range(4):
            S.op("vector", lambda e, st4=st4, slab=slab: e.tensor_scalar(
                v_out[:, st4 * H * 65:(st4 + 1) * H * 65],
                v_out[:, st4 * H * 65:(st4 + 1) * H * 65],
                mask_sb[:, slab * 4 + st4: slab * 4 + st4 + 1], None, ALU.mult))
            S.vdr()
        S.tok("vector", "VE")
        h_proj_done[hb] = ("PE", S.n["PE"])
        vev = S.n["VE"]

        S.w("sync", "AC", S.n["AC"])
        S.w("sync", "VE", vev)
        S.op("sync", lambda e, slab=slab: e.dma_start(
            s_k[:, :, slab * 512:(slab + 1) * 512].rearrange("c p n -> p c n"),
            k_out[:, :].rearrange("p (c n) -> p c n", n=512)), inc="KS", k=16)
        kout_free = ("KS", S.n["KS"])
        if slab >= 4:
            S.op("sync", lambda e, slab=slab: e.dma_start(
                s_q[:, :, (slab - 4) * 512:(slab - 3) * 512].rearrange("c p n -> p c n"),
                q_out[:, :].rearrange("p (c n) -> p c n", n=512)), inc="QS", k=16)
            qout_free = ("QS", S.n["QS"])
        S.op("sync", lambda e, slab=slab: e.dma_start(
            s_v[slab * 4:(slab + 1) * 4, :, :].rearrange("s p n -> p s n"),
            v_out[:, :].rearrange("p (s n) -> p s n", n=H * 65)), inc="VS", k=16)
        vout_free = ("VS", S.n["VS"])
    dA_done = (S.n["KS"], S.n["QS"], S.n["VS"])
    if "sk" in dbg_aps:
        S.w("sync", "KS", dA_done[0])
        S.w("sync", "QS", dA_done[1])
        S.w("sync", "VS", dA_done[2])
        S.op("sync", lambda e: e.dma_start(dbg_aps["sk"], s_k), inc="DS", k=16)
        S.op("sync", lambda e: e.dma_start(dbg_aps["sq"], s_q), inc="DS", k=16)
        S.op("sync", lambda e: e.dma_start(dbg_aps["sv"], s_v), inc="DS", k=16)
        S.op("sync", lambda e: e.dma_start(dbg_aps["mask"], mask_sb[:, :]), inc="DS", k=16)
        dA_done = S.n["DS"]
    S.barrier()
    S.flush()
    PA.close()
    if stop == "a":
        P.close(); es.close(); S.check(); return nc, dbg_aps

    PBC = ExitStack()
    out_attn = sbt("out_attn", [128, NCK * TOWN], BF16, PBC)

    PB = ExitStack()
    q_sb = sbt("q_sb", [128, NCK * TOWN], BF16, PB)
    v_sb = sbt("v_sb", [128, 32 * H * 65], BF16, PB)
    k_pair = [sbt(f"k_pair{i}", [128, T], BF16, PB) for i in range(2)]
    p_t = [sbt(f"p_t{i}", [128, 1024], BF16, PB) for i in range(3)]
    qsq = sbt("qsq", [128, TOWN], BF16, PB)
    bias_all = sbt("bias_all", [128, H * 32], F32, PB)
    rrow = sbt("rrow", [1, 1024], F32, PB)
    rrow_bf = sbt("rrow_bf", [1, 1024], BF16, PB)
    rb_bc = sbt("rb_bc", [128, 512], F32, PB)
    nrm = sbt("nrm", [128, 4], F32, PB)
    crow = sbt("crow", [1, 16], F32, PB)
    c_bf = sbt("c_bf", [1, 16], BF16, PB)
    ps_sc = [Bank(pst(f"ps_sc{i}", [128, 1024], F32, PB)) for i in range(3)]
    ps_av = [Bank(pst(f"ps_av{i}", [65, 512], F32, PB)) for i in range(2)]

    S.w("sync", "KS", dA_done[0])
    S.w("sync", "QS", dA_done[1])
    S.w("sync", "VS", dA_done[2])
    S.op("sync", lambda e: e.dma_start(
        q_sb[:, :].rearrange("p (c n) -> p c n", n=TOWN),
        s_q.rearrange("c p n -> p c n")), inc="DX", k=16)
    S.op("sync", lambda e: e.dma_start(
        v_sb[:, :].rearrange("p (s n) -> p s n", n=H * 65),
        s_v.rearrange("s p n -> p s n")), inc="DX", k=16)
    dQV = S.n["DX"]

    S.w("vector", "DX", dQV)
    qsq_guard = None
    for cc in range(NCK):
        S.wd("vector", qsq_guard)
        S.op("vector", lambda e, cc=cc: e.tensor_tensor(
            qsq[:, :], q_sb[:, cc * TOWN:(cc + 1) * TOWN],
            q_sb[:, cc * TOWN:(cc + 1) * TOWN], ALU.mult))
        vq = S.tok("vector", "VE")
        S.w("tensor", "VE", vq)
        for nq in range(4):
            S.wd("tensor", ps_sc[0].free)
            S.op("tensor", lambda e, nq=nq: e.matmul(
                ps_sc[0].ap[0:1, 0:512], ones_c[0:64, :], qsq[0:64, nq * 512:(nq + 1) * 512],
                start=True, stop=True))
            S.op("tensor", lambda e, nq=nq: e.matmul(
                ps_sc[0].ap[64:65, 0:512], ones_c[64:128, :],
                qsq[64:128, nq * 512:(nq + 1) * 512], start=True, stop=True))
            ps_sc[0].ready = ("PE", S.tok("tensor", "PE"))
            S.wd("vector", ps_sc[0].ready)
            for rr in (slice(0, 1), slice(64, 65)):
                S.op("vector", lambda e, rr=rr: e.tensor_reduce(
                    nrm[rr, 0:1], ps_sc[0].ap[rr, 0:512], AXX, ALU.max))
            S.vdr()
            if nq == 0:
                for rr in (slice(0, 1), slice(64, 65)):
                    S.op("vector", lambda e, rr=rr: e.tensor_copy(
                        nrm[rr, 1:2], nrm[rr, 0:1]))
            else:
                for rr in (slice(0, 1), slice(64, 65)):
                    S.op("vector", lambda e, rr=rr: e.tensor_tensor(
                        nrm[rr, 2:3], nrm[rr, 1:2], nrm[rr, 0:1], ALU.max))
                S.vdr()
                for rr in (slice(0, 1), slice(64, 65)):
                    S.op("vector", lambda e, rr=rr: e.tensor_copy(
                        nrm[rr, 1:2], nrm[rr, 2:3]))
            S.vdr()
            tk = S.tok("vector", "VE")
            ps_sc[0].free = ("VE", tk)
        qsq_guard = ps_sc[0].ready
        S.op("vector", lambda e, cc=cc: e.tensor_tensor(
            nrm[0:1, 3:4], mk2_sb[0:1, 2 * cc:2 * cc + 1], nrm[0:1, 1:2], ALU.add))
        S.vdr()
        S.op("vector", lambda e, cc=cc: e.tensor_copy(
            mk2_sb[0:1, 2 * cc:2 * cc + 1], nrm[0:1, 3:4]))
        S.vdr()
        S.op("vector", lambda e, cc=cc: e.tensor_tensor(
            nrm[64:65, 3:4], mk2_sb[64:65, 2 * cc + 1:2 * cc + 2],
            nrm[64:65, 1:2], ALU.add))
        S.vdr()
        S.op("vector", lambda e, cc=cc: e.tensor_copy(
            mk2_sb[64:65, 2 * cc + 1:2 * cc + 2], nrm[64:65, 3:4]))
        S.vdr()
        S.tok("vector", "VE")
    vQN = S.n["VE"]
    S.w("tensor", "VE", vQN)
    S.wd("tensor", ps_sc[0].free)
    S.op("tensor", lambda e: e.matmul(ps_sc[0].ap[0:1, 0:16], onef_c[64:65, :],
                                      mk2_sb[64:65, 0:16], start=True, stop=True))
    pRM = S.tok("tensor", "PE")
    S.w("vector", "PE", pRM)
    S.op("vector", lambda e: e.tensor_tensor(crow[0:1, :], mk2_sb[0:1, 0:16],
                                             ps_sc[0].ap[0:1, 0:16], ALU.add))
    S.vdr()
    S.op("vector", lambda e: e.tensor_scalar(mk2_sb[0:1, 0:16], crow[0:1, :], 0.5,
                                             None, ALU.mult))
    S.vdr()
    S.op("vector", lambda e: e.tensor_copy(c_bf[0:1, :], mk2_sb[0:1, 0:16]))
    S.vdr()
    vCB = S.tok("vector", "VE")
    ps_sc[0].free = ("VE", vCB)
    S.w("tensor", "VE", vCB)
    S.wd("tensor", ps_sc[0].free)
    S.op("tensor", lambda e: e.matmul(ps_sc[0].ap[:, 0:16], ones_r[:, :], c_bf[0:1, :],
                                      start=True, stop=True))
    pCB = S.tok("tensor", "PE")
    S.w("vector", "PE", pCB)
    S.op("vector", lambda e: e.tensor_scalar(
        bias_all[:, 0:16], ps_sc[0].ap[:, 0:16], -0.5, None, ALU.mult))
    S.vdr()
    vBA = S.tok("vector", "VE")
    ps_sc[0].free = ("VE", vBA)
    S.w("scalar", "VE", vBA)

    kp_free = [None, None]
    kp_dx = [None, None]
    p_free = [None, None, None, None]
    for pr in range(NCK):
        kb = k_pair[pr % 2]
        S.wd("sync", kp_free[pr % 2])
        ksem = "KA" if pr % 2 == 0 else "KB"
        S.op("sync", lambda e, pr=pr, kb=kb: e.dma_start(kb[:, :], s_k[pr]),
             inc=ksem, k=16)
        kp_dx[pr % 2] = S.n[ksem]
        he, ho = 2 * pr, 2 * pr + 1
        for tq in range(4):
            ave = ps_av[0]
            avo = ps_av[1]

            def emit_scores(sl, tq=tq, pr=pr, kb=kb):
                g, par = sl // 2, sl % 2
                bk = ps_sc[sl % 3]
                S.wd("tensor", bk.free)
                if tq == 0 and sl == 0:
                    S.w("tensor", "KA" if pr % 2 == 0 else "KB",
                        kp_dx[pr % 2])
                base = 0 if par == 0 else 64
                for half in range(2):
                    st = 2 * g + half
                    S.op("tensor", lambda e, st=st, tq=tq, pr=pr, kb=kb, bk=bk,
                         base=base, half=half: e.matmul(
                             bk.ap[:, half * 512:(half + 1) * 512],
                             kb[base:base + 64, st * 128:(st + 1) * 128],
                             q_sb[base:base + 64,
                                  pr * TOWN + tq * 512: pr * TOWN + tq * 512 + 512],
                             start=True, stop=True))
                bk.ready = ("PE", S.tok("tensor", "PE"))
                hh = (2 * pr) + par
                S.wd("scalar", bk.ready)
                S.wd("scalar", p_free[sl % 3])
                tk = S.op("scalar", lambda e, hh=hh, bk=bk, sl=sl:
                          e.activation(p_t[sl % 3][:, :], bk.ap[:, :], AF.Exp,
                                       bias=bias_all[:, hh:hh + 1]), inc="AC")
                bk.free = ("AC", tk)
                return tk

            exp_tok = {}
            exp_tok[0] = emit_scores(0)
            for sl in range(32):
                if sl + 1 < 32:
                    exp_tok[sl + 1] = emit_scores(sl + 1)
                g, par = sl // 2, sl % 2
                av = ave if par == 0 else avo
                S.w("tensor", "AC", exp_tok[sl])
                if g == 0:
                    S.wd("tensor", av.free)
                for half in range(2):
                    st = 2 * g + half
                    S.op("tensor", lambda e, st=st, sl=sl, av=av, par=par, pr=pr,
                         half=half: e.matmul(
                             av.ap[:, :],
                             v_sb[:, st * H * 65 + (2 * pr + par) * 65:
                                  st * H * 65 + (2 * pr + par) * 65 + 65],
                             p_t[sl % 3][:, half * 512:(half + 1) * 512],
                             start=(st == 0), stop=(st == 31)))
                av_tok = ("PE", S.tok("tensor", "PE"))
                p_free[sl % 3] = av_tok
                if g == 15:
                    av.ready = av_tok
            for par, hh, av in ((0, he, ave), (1, ho, avo)):
                S.wd("vector", av.ready)
                S.op("vector", lambda e, par=par, av=av: e.reciprocal(
                    rrow[0:1, par * 512:par * 512 + 512], av.ap[64:65, :]))
                S.vdr()
                S.op("vector", lambda e, par=par: e.tensor_copy(
                    rrow_bf[0:1, par * 512:par * 512 + 512],
                    rrow[0:1, par * 512:par * 512 + 512]))
                vr = S.tok("vector", "VE")
                S.w("tensor", "VE", vr)
                bcb = ps_sc[par]
                S.wd("tensor", bcb.free)
                S.op("tensor", lambda e, par=par, bcb=bcb: e.matmul(
                    bcb.ap[0:64, 0:512], ones_r[0:1, 0:64],
                    rrow_bf[0:1, par * 512:par * 512 + 512], start=True, stop=True))
                bcb.ready = ("PE", S.tok("tensor", "PE"))
                S.wd("vector", bcb.ready)
                base = 0 if par == 0 else 64
                tkb = S.op("vector", lambda e, bcb=bcb: e.tensor_copy(
                    rb_bc[0:64, :], bcb.ap[0:64, 0:512]), inc="VE")
                bcb.free = ("VE", tkb)
                S.vdr()
                tk = S.op("vector", lambda e, par=par, hh=hh, tq=tq, base=base,
                          pr=pr, av=av: e.tensor_tensor(
                              out_attn[base:base + 64,
                                       pr * TOWN + tq * 512: pr * TOWN + tq * 512 + 512],
                              av.ap[0:64, :], rb_bc[0:64, :], ALU.mult),
                          inc="VE")
                av.free = ("VE", tk)
        kp_free[pr % 2] = ("PE", S.n["PE"])
    vATT = S.n["VE"]
    if "oat" in dbg_aps:
        S.w("sync", "VE", vATT)
        S.op("sync", lambda e: e.dma_start(dbg_aps["oat"], out_attn[:, :]), inc="DS", k=16)
        S.op("sync", lambda e: e.dma_start(dbg_aps["biasall"], bias_all[:, :]), inc="DS", k=16)
    S.barrier()
    S.flush()
    PB.close()
    if stop == "b":
        PBC.close(); P.close(); es.close(); S.check(); return nc, dbg_aps

    PC = ExitStack()
    wo_T = sbt("wo_T", [128, NCK * D], BF16, PC)
    wraw2 = sbt("wraw2", [128, NCK * D], BF16, PC)
    wo_fm = sbt("wo_fm", [128, NCK * TOWN], BF16, PC)
    xw = [sbt(f"xw{i}", [128, D], F32, PC) for i in range(2)]
    x2t = [sbt(f"x2t{i}", [128, D], F32, PC) for i in range(2)]
    z2t = [sbt(f"z2t{i}", [128, D], BF16, PC) for i in range(2)]
    h2sl = sbt("h2sl", [128, NCK * 512], BF16, PC)
    scr2 = sbt("scr2", [128, D], F32, PC)
    scr2b = sbt("scr2b", [128, D], F32, PC)
    st2 = sbt("st2", [128, 12], F32, PC)
    ps_wo = [Bank(pst(f"ps_wo{i}", [128, 512], F32, PC)) for i in range(2)]
    ps_tc = [Bank(pst(f"ps_tc{i}", [128, 1024], BF16, PC)) for i in range(2)]

    wraw2_free = None
    S.op("gpsimd", lambda e: e.dma_start(
        wraw2[:, :].rearrange("p (c n) -> p c n", n=D),
        wo.rearrange("(c p) n -> p c n", p=128)), inc="WQ", k=16)
    S.w("tensor", "WQ", S.n["WQ"])
    for oc in range(NCK):
        for ic in range(NCK):
            bk = ps_tc[(oc * NCK + ic) % 2]
            S.wd("tensor", bk.free)
            S.op("tensor", lambda e, oc=oc, ic=ic, bk=bk: e.transpose(
                bk.ap[:, 0:128],
                wraw2[:, oc * D + ic * 128: oc * D + ic * 128 + 128], idb[:, :]))
            bk.ready = ("PE", S.tok("tensor", "PE"))
            S.wd("scalar", bk.ready)
            tk = S.op("scalar", lambda e, oc=oc, ic=ic, bk=bk: e.activation(
                wo_T[:, ic * D + oc * 128: ic * D + oc * 128 + 128],
                bk.ap[:, 0:128], AF.Identity), inc="AC")
            bk.free = ("AC", tk)
    aWO = S.n["AC"]

    S.w("tensor", "AC", aWO)
    S.w("tensor", "VE", vATT)
    for oc in range(NCK):
        for tqc in range(4):
            bk = ps_wo[(oc * 4 + tqc) % 2]
            S.wd("tensor", bk.free)
            for kc in range(NCK):
                S.op("tensor", lambda e, oc=oc, tqc=tqc, kc=kc, bk=bk: e.matmul(
                    bk.ap[:, :], wo_T[:, kc * D + oc * 128: kc * D + oc * 128 + 128],
                    out_attn[:, kc * TOWN + tqc * 512: kc * TOWN + tqc * 512 + 512],
                    start=(kc == 0), stop=(kc == NCK - 1)))
            bk.ready = ("PE", S.tok("tensor", "PE"))
            S.wd("scalar", bk.ready)
            tk = S.op("scalar", lambda e, oc=oc, tqc=tqc, bk=bk: e.activation(
                wo_fm[:, oc * TOWN + tqc * 512: oc * TOWN + tqc * 512 + 512],
                bk.ap[:, :], AF.Identity, bias=gabo[:, oc:oc + 1],
                scale=ga_v[:, oc:oc + 1]), inc="AC")
            bk.free = ("AC", tk)
    aWOF = S.n["AC"]

    xw_free = [None, None]
    x2_free = [None, None]
    z2_free = [None, None]
    h2_slab_store = None
    for ti in range(16):
        tl = ti % 4
        xbuf = xw[ti % 2]
        x2b = x2t[ti % 2]
        z2b = z2t[ti % 2]
        S.wd("sync", xw_free[ti % 2])
        xsem = "XA" if ti % 2 == 0 else "XB"
        S.op("sync", lambda e, ti=ti, xbuf=xbuf: e.dma_start(
            xbuf[:, :], xb[TOWN + ti * 128: TOWN + (ti + 1) * 128, :]), inc=xsem,
            k=16)
        dxo = S.n[xsem]
        bk = ps_tc[ti % 2]
        S.wd("tensor", bk.free)
        if ti == 0:
            S.w("tensor", "AC", aWOF)
        for jc in range(NCK):
            S.op("tensor", lambda e, jc=jc, ti=ti, bk=bk: e.transpose(
                bk.ap[:, jc * 128:(jc + 1) * 128],
                wo_fm[:, jc * TOWN + ti * 128: jc * TOWN + ti * 128 + 128],
                idb[:, :]))
        bk.ready = ("PE", S.tok("tensor", "PE"))
        S.wd("vector", bk.ready)
        S.w("vector", xsem, dxo)
        S.wd("vector", x2_free[ti % 2])
        tk = S.op("vector", lambda e, bk=bk, x2b=x2b, xbuf=xbuf: e.tensor_tensor(
            x2b[:, :], bk.ap[:, :], xbuf[:, :], ALU.add), inc="VE")
        bk.free = ("VE", tk)
        xw_free[ti % 2] = ("VE", tk)
        vx2 = tk
        S.w("sync", "VE", vx2)
        x2sem = "X2A" if ti % 2 == 0 else "X2B"
        S.op("sync", lambda e, ti=ti, x2b=x2b: e.dma_start(s_x2[ti], x2b[:, :]),
             inc=x2sem, k=16)
        dsx2 = S.n[x2sem]
        S.w("scalar", "VE", vx2)
        S.op("scalar", lambda e, x2b=x2b: e.activation(
            scr2[:, :], x2b[:, :], AF.Identity, accum_out=st2[:, 0:1]))
        S.op("scalar", lambda e, x2b=x2b: e.activation(
            scr2b[:, :], x2b[:, :], AF.Square, accum_out=st2[:, 1:2]), inc="AC")
        ac1 = S.n["AC"]
        S.w("vector", "AC", ac1)
        S.op("vector", lambda e: e.tensor_scalar(
            st2[:, 2:3], st2[:, 0:1], 1.0 / D, None, ALU.mult))
        S.vdr()
        S.op("vector", lambda e: e.tensor_tensor(
            st2[:, 3:4], st2[:, 2:3], st2[:, 2:3], ALU.mult))
        S.vdr()
        S.op("vector", lambda e: e.scalar_tensor_tensor(
            st2[:, 4:5], st2[:, 1:2], 1.0 / D, st2[:, 3:4], ALU.mult, ALU.subtract))
        S.vdr()
        ve1 = S.tok("vector", "VE")
        S.w("scalar", "VE", ve1)
        S.op("scalar", lambda e: e.activation(
            st2[:, 5:6], st2[:, 4:5], AF.Sqrt, bias=eps_c[:, :]), inc="AC")
        ac2 = S.n["AC"]
        S.w("vector", "AC", ac2)
        S.op("vector", lambda e: e.reciprocal(st2[:, 5:6], st2[:, 5:6]))
        S.vdr()
        S.op("vector", lambda e: e.scalar_tensor_tensor(
            st2[:, 6:7], st2[:, 2:3], -1.0, st2[:, 5:6], ALU.mult, ALU.mult))
        S.vdr()
        ve2 = S.tok("vector", "VE")
        S.w("scalar", "VE", ve2)
        S.wd("scalar", z2_free[ti % 2])
        S.op("scalar", lambda e, x2b=x2b, z2b=z2b: e.activation(
            z2b[:, :], x2b[:, :], AF.Identity, bias=st2[:, 6:7], scale=st2[:, 5:6]),
            inc="AC")
        acz = S.n["AC"]
        x2_free[ti % 2] = [("AC", acz), (x2sem, dsx2)]
        S.w("tensor", "AC", acz)
        bk2 = ps_tc[(ti + 1) % 2]
        S.wd("tensor", bk2.free)
        if tl == 0:
            S.wd("tensor", h2_slab_store)
        for j in range(NCK):
            S.op("tensor", lambda e, j=j, z2b=z2b, bk2=bk2: e.transpose(
                bk2.ap[:, j * 128:(j + 1) * 128], z2b[:, j * 128:(j + 1) * 128],
                idb[:, :]))
        bk2.ready = ("PE", S.tok("tensor", "PE"))
        z2_free[ti % 2] = bk2.ready
        S.wd("scalar", bk2.ready)
        for j in range(NCK):
            S.op("scalar", lambda e, j=j, tl=tl, bk2=bk2: e.activation(
                h2sl[:, j * 512 + tl * 128: j * 512 + tl * 128 + 128],
                bk2.ap[:, j * 128:(j + 1) * 128], AF.Identity,
                bias=sbv_v[:, j:j + 1], scale=cb1p[:, j:j + 1]))
        tk = S.tok("scalar", "AC")
        bk2.free = ("AC", tk)
        if tl == 3:
            sl = ti // 4
            S.w("sync", "AC", tk)
            S.op("sync", lambda e, sl=sl: e.dma_start(
                s_h2[:, :, sl * 512:(sl + 1) * 512].rearrange("c p n -> p c n"),
                h2sl[:, :].rearrange("p (c n) -> p c n", n=512)), inc="H2S", k=16)
            h2_slab_store = ("H2S", S.n["H2S"])
    dC_done = (S.n["X2A"], S.n["X2B"])
    if "x2" in dbg_aps:
        S.w("sync", "X2A", dC_done[0])
        S.w("sync", "X2B", dC_done[1])
        S.w("sync", "H2S", S.n["H2S"])
        S.op("sync", lambda e: e.dma_start(dbg_aps["x2"], s_x2), inc="DS", k=16)
        S.op("sync", lambda e: e.dma_start(dbg_aps["h2"], s_h2), inc="DS", k=16)
    S.barrier()
    S.flush()
    PC.close()
    PBC.close()
    if stop == "c":
        P.close(); es.close(); S.check(); return nc, dbg_aps

    PD = ExitStack()
    h2blk = sbt("h2blk", [128, NCK * 1024], BF16, PD)
    xnblk = sbt("xnblk", [128, NCK * 1024], BF16, PD)
    abelblk = sbt("abelblk", [128, NCK * 1024], BF16, PD)
    moeblk = sbt("moeblk", [128, NCK * 1024], BF16, PD)
    he_sb2 = [sbt(f"he_sb{i}", [128, 16 * 512], BF16, PD) for i in range(2)]
    abwb = sbt("abwb", [128, NCK * D], BF16, PD)
    ew1b = sbt("ew1b", [128, NCK * DFF], BF16, PD)
    ew2s = [sbt(f"ew2s{i}", [128, 16 * 128], BF16, PD) for i in range(2)]
    Ebuf = abwb
    wd_bf = sbt("wd_bf", [EXPN, 1024], BF16, PD)
    r2bc = sbt("r2bc", [128, 1024], BF16, PD)
    m2bc = sbt("m2bc", [128, 1024], BF16, PD)
    gbc = sbt("gbc", [128, 1024], BF16, PD)
    wbc = sbt("wbc", [128, 1024], BF16, PD)
    rowa = sbt("rowa", [1, 1024], F32, PD)
    rowb = sbt("rowb", [1, 1024], F32, PD)
    rowc = sbt("rowc", [1, 1024], F32, PD)
    row_bf = sbt("row_bf", [1, 1024], BF16, PD)
    sq2 = [sbt(f"sq2_{i}", [128, 512], BF16, PD) for i in range(2)]
    ltok = sbt("ltok", [128, 64], F32, PD)
    sgin = sbt("sgin", [128, 64], F32, PD)
    sel = sbt("sel", [128, 64], F32, PD)
    mx8 = sbt("mx8", [128, 8], F32, PD)
    wdc = sbt("wdc", [128, 2], F32, PD)
    wdv = sbt("wdv", [128, 64], F32, PD)
    wrow = sbt("wrow", [1, 1024], BF16, PD)
    lg_fm = sbt("lg_fm", [EXPN, 1024], F32, PD)
    x2r = [sbt("x2r0", [128, D], F32, PD)]
    x2r = [x2r[0], x2r[0]]
    x3 = sbt("x3", [128, D], F32, PD)

    ps_h = [Bank(pst(f"ps_h{i}", [128, 512], F32, PD)) for i in range(2)]
    ps_m = [Bank(pst(f"ps_m{i}", [128, 512], F32, PD)) for i in range(2)]
    ps_g = Bank(pst("ps_g", [128, 512], F32, PD))
    ps_lg = Bank(pst("ps_lg", [EXPN, 512], F32, PD))
    ps_t8 = Bank(pst("ps_t8", [128, 64], F32, PD))
    ps_ty = Bank(pst("ps_ty", [128, 1024], BF16, PD))

    ew1_free = None
    ew2s_free = [None, None]
    e1_loaded = {}

    S.op("gpsimd", lambda e: e.dma_start(
        abwb[:, :].rearrange("p (c n) -> p c n", n=D),
        abel_w.rearrange("(c p) n -> p c n", p=128)), inc="AB", k=16)
    dABW = S.n["AB"]

    def load_e1(tb, ex):
        S.wd("gpsimd", ew1_free)
        S.op("gpsimd", lambda e, ex=ex: e.dma_start(
            ew1b[:, :].rearrange("p (c n) -> p c n", n=DFF),
            ew1[ex].rearrange("(c p) n -> p c n", p=128)), inc="E1", k=16)
        e1_loaded[(tb, ex)] = S.n["E1"]

    load_e1(0, 0)

    h2_free = None
    Ebuf_free = None
    gbc_free = None
    wd_free = None
    ltok_free = None
    x2r_free2 = [None]
    x2r_free = None
    x3_free = None
    out_final = 0
    stats_pe_guard = None
    sq_free = [None, None]
    stg_free = [None, None]

    for tb in range(2):
        S.wd("sync", h2_free)
        if tb == 0:
            S.w("sync", "X2A", dC_done[0])
            S.w("sync", "X2B", dC_done[1])
            S.w("sync", "H2S", S.n["H2S"])
        S.op("sync", lambda e, tb=tb: e.dma_start(
            h2blk[:, :].rearrange("p (c n) -> p c n", n=1024),
            s_h2[:, :, tb * 1024:(tb + 1) * 1024].rearrange("c p n -> p c n")),
            inc="HH", k=16)
        dH2 = S.n["HH"]

        S.w("tensor", "HH", dH2)
        S.wd("tensor", stats_pe_guard)
        for nb in range(2):
            S.wd("tensor", ps_g.free)
            for kc in range(NCK):
                S.op("tensor", lambda e, nb=nb, kc=kc: e.matmul(
                    ps_g.ap[0:1, :], ones_c[:, :],
                    h2blk[:, kc * 1024 + nb * 512: kc * 1024 + nb * 512 + 512],
                    start=(kc == 0), stop=(kc == NCK - 1)))
            ps_g.ready = ("PE", S.tok("tensor", "PE"))
            S.wd("vector", ps_g.ready)
            if tb == 0 and nb == 0:
                S.w("vector", "HH", dH2)
            tk = S.op("vector", lambda e, nb=nb: e.tensor_scalar(
                rowa[0:1, nb * 512:(nb + 1) * 512], ps_g.ap[0:1, :], 1.0 / D, None,
                ALU.mult), inc="VE")
            ps_g.free = ("VE", tk)
        for nb in range(2):
            for kc in range(NCK):
                sqb = sq2[kc % 2]
                S.wd("vector", sq_free[kc % 2])
                S.op("vector", lambda e, nb=nb, kc=kc, sqb=sqb: e.tensor_tensor(
                    sqb[:, :],
                    h2blk[:, kc * 1024 + nb * 512: kc * 1024 + nb * 512 + 512],
                    h2blk[:, kc * 1024 + nb * 512: kc * 1024 + nb * 512 + 512],
                    ALU.mult))
                vsq = S.tok("vector", "VE")
                S.w("tensor", "VE", vsq)
                if kc == 0:
                    S.wd("tensor", ps_g.free)
                S.op("tensor", lambda e, kc=kc, sqb=sqb: e.matmul(
                    ps_g.ap[0:1, :], ones_c[:, :], sqb[:, :],
                    start=(kc == 0), stop=(kc == NCK - 1)))
                sq_free[kc % 2] = ("PE", S.tok("tensor", "PE"))
            ps_g.ready = ("PE", S.tok("tensor", "PE"))
            S.wd("vector", ps_g.ready)
            tk = S.op("vector", lambda e, nb=nb: e.tensor_scalar(
                rowb[0:1, nb * 512:(nb + 1) * 512], ps_g.ap[0:1, :], 1.0 / D, None,
                ALU.mult), inc="VE")
            ps_g.free = ("VE", tk)
        S.op("vector", lambda e: e.tensor_tensor(rowc[0:1, :], rowa[0:1, :],
                                                 rowa[0:1, :], ALU.mult))
        S.vdr()
        S.op("vector", lambda e: e.tensor_tensor(rowb[0:1, :], rowb[0:1, :],
                                                 rowc[0:1, :], ALU.subtract))
        S.vdr()
        vve = S.tok("vector", "VE")
        S.w("scalar", "VE", vve)
        S.op("scalar", lambda e: e.activation(rowb[0:1, :], rowb[0:1, :], AF.Sqrt,
                                              bias=eps_c[0:1, :]), inc="AC")
        aSq = S.n["AC"]
        S.w("vector", "AC", aSq)
        S.op("vector", lambda e: e.reciprocal(rowb[0:1, :], rowb[0:1, :]))
        S.vdr()
        S.op("vector", lambda e: e.tensor_tensor(rowc[0:1, :], rowa[0:1, :],
                                                 rowb[0:1, :], ALU.mult))
        S.op("vector", lambda e: e.tensor_copy(row_bf[0:1, :], rowb[0:1, :]))
        vr2 = S.tok("vector", "VE")
        S.w("tensor", "VE", vr2)
        for nb in range(2):
            S.wd("tensor", ps_g.free)
            S.op("tensor", lambda e, nb=nb: e.matmul(
                ps_g.ap[:, :], ones_r[:, :], row_bf[0:1, nb * 512:(nb + 1) * 512],
                start=True, stop=True))
            ps_g.ready = ("PE", S.tok("tensor", "PE"))
            S.wd("vector", ps_g.ready)
            tk = S.op("vector", lambda e, nb=nb: e.tensor_copy(
                r2bc[:, nb * 512:(nb + 1) * 512], ps_g.ap[:, :]), inc="VE")
            ps_g.free = ("VE", tk)
        S.op("vector", lambda e: e.tensor_copy(row_bf[0:1, :], rowc[0:1, :]))
        vm2 = S.tok("vector", "VE")
        S.w("tensor", "VE", vm2)
        for nb in range(2):
            S.wd("tensor", ps_g.free)
            S.op("tensor", lambda e, nb=nb: e.matmul(
                ps_g.ap[:, :], ones_r[:, :], row_bf[0:1, nb * 512:(nb + 1) * 512],
                start=True, stop=True))
            ps_g.ready = ("PE", S.tok("tensor", "PE"))
            S.wd("vector", ps_g.ready)
            tk = S.op("vector", lambda e, nb=nb: e.tensor_copy(
                m2bc[:, nb * 512:(nb + 1) * 512], ps_g.ap[:, :]), inc="VE")
            ps_g.free = ("VE", tk)
        for kc in range(NCK):
            S.op("vector", lambda e, kc=kc: e.tensor_tensor(
                xnblk[:, kc * 1024:(kc + 1) * 1024],
                h2blk[:, kc * 1024:(kc + 1) * 1024], r2bc[:, :], ALU.mult))
        S.vdr()
        for kc in range(NCK):
            S.op("vector", lambda e, kc=kc: e.tensor_tensor(
                xnblk[:, kc * 1024:(kc + 1) * 1024],
                xnblk[:, kc * 1024:(kc + 1) * 1024], m2bc[:, :], ALU.subtract))
        S.vdr()
        for kc in range(NCK):
            S.op("vector", lambda e, kc=kc: e.tensor_scalar(
                xnblk[:, kc * 1024:(kc + 1) * 1024],
                xnblk[:, kc * 1024:(kc + 1) * 1024], ln_g[:, kc:kc + 1],
                ln_b[:, kc:kc + 1], ALU.mult, ALU.add))
        vXN = S.tok("vector", "VE")
        if tb == 0 and "xn" in dbg_aps:
            S.w("sync", "VE", vXN)
            S.op("sync", lambda e: e.dma_start(dbg_aps["xn"], xnblk[:, :]), inc="DS", k=16)

        S.w("tensor", "VE", vXN)
        S.w("tensor", "DW", nDWsmall)
        for nb in range(2):
            S.wd("tensor", ps_lg.free)
            for kc in range(NCK):
                S.op("tensor", lambda e, nb=nb, kc=kc: e.matmul(
                    ps_lg.ap[:, :], gw_bf[:, kc * EXPN:(kc + 1) * EXPN],
                    xnblk[:, kc * 1024 + nb * 512: kc * 1024 + nb * 512 + 512],
                    start=(kc == 0), stop=(kc == NCK - 1)))
            ps_lg.ready = ("PE", S.tok("tensor", "PE"))
            S.wd("scalar", ps_lg.ready)
            tk = S.op("scalar", lambda e, nb=nb: e.activation(
                lg_fm[:, nb * 512:(nb + 1) * 512], ps_lg.ap[:, :], AF.Identity,
                bias=gate_b_sb[:, :]), inc="AC")
            ps_lg.free = ("AC", tk)
        aLG = S.n["AC"]
        S.w("tensor", "AC", aLG)
        S.wd("tensor", ps_t8.free)
        S.wd("vector", ltok_free)
        for tl in range(8):
            S.op("tensor", lambda e, tl=tl: e.transpose(
                ps_t8.ap[:, tl * 8:(tl + 1) * 8],
                lg_fm[:, tl * 128:(tl + 1) * 128], idf[0:8, 0:8]))
        ps_t8.ready = ("PE", S.tok("tensor", "PE"))
        S.wd("vector", ps_t8.ready)
        tk = S.op("vector", lambda e: e.tensor_copy(ltok[:, :], ps_t8.ap[:, :]),
                  inc="VE")
        S.vdr()
        ps_t8.free = ("VE", tk)
        for tl in range(8):
            S.op("vector", lambda e, tl=tl: e.max(out=mx8[:, :],
                                                  in_=ltok[:, tl * 8:(tl + 1) * 8]))
            S.vdr()
            S.op("vector", lambda e: e.tensor_tensor(
                wdc[:, 0:1], mx8[:, 0:1], mx8[:, 1:2], ALU.add))
            S.vdr()
            S.op("vector", lambda e, tl=tl: e.tensor_scalar(
                wdv[:, tl * 8:(tl + 1) * 8], ltok[:, tl * 8:(tl + 1) * 8],
                2.0, None, ALU.mult))
            S.vdr()
            S.op("vector", lambda e, tl=tl: e.tensor_scalar(
                sgin[:, tl * 8:(tl + 1) * 8], wdv[:, tl * 8:(tl + 1) * 8],
                wdc[:, 0:1], None, ALU.subtract))
            S.vdr()
            S.op("vector", lambda e, tl=tl: e.tensor_scalar(
                sel[:, tl * 8:(tl + 1) * 8], ltok[:, tl * 8:(tl + 1) * 8],
                mx8[:, 1:2], None, ALU.is_ge))
            S.vdr()
        vTOP = S.tok("vector", "VE")
        S.w("scalar", "VE", vTOP)
        aSG = S.op("scalar", lambda e: e.activation(wdv[:, :], sgin[:, :],
                                                    AF.Sigmoid), inc="AC")
        S.w("vector", "AC", aSG)
        S.op("vector", lambda e: e.tensor_tensor(sgin[:, :], sel[:, :], wdv[:, :],
                                                 ALU.mult))
        vWD = S.tok("vector", "VE")
        ltok_free = ("VE", vWD)
        S.w("tensor", "VE", vWD)
        S.wd("vector", wd_free)
        for half in range(2):
            S.wd("tensor", ps_lg.free)
            for tl4 in range(4):
                tl = half * 4 + tl4
                S.op("tensor", lambda e, tl=tl, tl4=tl4: e.transpose(
                    ps_lg.ap[:, tl4 * 128:(tl4 + 1) * 128],
                    sgin[:, tl * 8:(tl + 1) * 8], idf[:, :]))
            ps_lg.ready = ("PE", S.tok("tensor", "PE"))
            S.wd("vector", ps_lg.ready)
            tk = S.op("vector", lambda e, half=half: e.tensor_copy(
                wd_bf[:, half * 512:(half + 1) * 512], ps_lg.ap[:, :]), inc="VE")
            ps_lg.free = ("VE", tk)
        vWDF = S.n["VE"]
        wd_free = ("VE", vWDF)
        if tb == 0 and "wd" in dbg_aps:
            S.w("sync", "VE", vWDF)
            S.op("sync", lambda e: e.dma_start(dbg_aps["wd"], wd_bf[:, :]), inc="DS", k=16)
            S.op("sync", lambda e: e.dma_start(dbg_aps["lg"], lg_fm[:, :]), inc="DS", k=16)

        S.w("tensor", "AB", dABW)
        for oc in range(NCK):
            for nb2 in range(2):
                bk = ps_h[(oc * 2 + nb2) % 2]
                S.wd("tensor", bk.free)
                for kc in range(NCK):
                    S.op("tensor", lambda e, oc=oc, nb2=nb2, kc=kc, bk=bk: e.matmul(
                        bk.ap[:, :],
                        abwb[:, kc * D + oc * 128: kc * D + oc * 128 + 128],
                        xnblk[:, kc * 1024 + nb2 * 512: kc * 1024 + nb2 * 512 + 512],
                        start=(kc == 0), stop=(kc == NCK - 1)))
                bk.ready = ("PE", S.tok("tensor", "PE"))
                S.wd("scalar", bk.ready)
                tk = S.op("scalar", lambda e, oc=oc, nb2=nb2, bk=bk: e.activation(
                    abelblk[:, oc * 1024 + nb2 * 512: oc * 1024 + nb2 * 512 + 512],
                    bk.ap[:, :], AF.Gelu, bias=abel_bb[:, oc:oc + 1]), inc="AC")
                bk.free = ("AC", tk)
        Ebuf_free = ("PE", S.n["PE"])

        he_free = [None, None]
        for ex in range(EXPN):
            S.w("tensor", "VE", vWDF)
            for half in range(2):
                S.wd("tensor", ps_lg.free)
                for tl4 in range(4):
                    tl = half * 4 + tl4
                    S.op("tensor", lambda e, ex=ex, tl=tl, tl4=tl4: e.transpose(
                        ps_lg.ap[0:1, tl4 * 128:(tl4 + 1) * 128],
                        sgin[:, tl * 8 + ex: tl * 8 + ex + 1], idf[:, :]))
                ps_lg.ready = ("PE", S.tok("tensor", "PE"))
                S.wd("vector", ps_lg.ready)
                tk = S.op("vector", lambda e, half=half: e.tensor_copy(
                    wrow[0:1, half * 512:(half + 1) * 512],
                    ps_lg.ap[0:1, 0:512]), inc="VE")
                ps_lg.free = ("VE", tk)
            vWR = S.n["VE"]
            S.w("tensor", "VE", vWR)
            for nb2 in range(2):
                bk = ps_m[nb2 % 2]
                S.wd("tensor", bk.free)
                S.op("tensor", lambda e, nb2=nb2, bk=bk: e.matmul(
                    bk.ap[:, :], ones_r[:, :],
                    wrow[0:1, nb2 * 512:(nb2 + 1) * 512], start=True, stop=True))
                bk.ready = ("PE", S.tok("tensor", "PE"))
                S.wd("vector", bk.ready)
                tk = S.op("vector", lambda e, nb2=nb2, bk=bk: e.tensor_copy(
                    wbc[:, nb2 * 512:(nb2 + 1) * 512], bk.ap[:, :]), inc="VE")
                bk.free = ("VE", tk)
            S.vdr()
            vWBC = S.n["VE"]
            S.w("tensor", "E1", e1_loaded[(tb, ex)])
            for tc in range(2):
                he_sb = he_sb2[tc]
                for hfc in range(16):
                    bk = ps_h[hfc % 2]
                    S.wd("tensor", bk.free)
                    if hfc == 0:
                        S.wd("tensor", he_free[tc])
                    for kc in range(NCK):
                        S.op("tensor", lambda e, hfc=hfc, tc=tc, kc=kc, bk=bk:
                             e.matmul(
                                 bk.ap[:, :],
                                 ew1b[:, kc * DFF + hfc * 128: kc * DFF + hfc * 128 + 128],
                                 xnblk[:, kc * 1024 + tc * 512: kc * 1024 + tc * 512 + 512],
                                 start=(kc == 0), stop=(kc == NCK - 1)))
                    bk.ready = ("PE", S.tok("tensor", "PE"))
                    S.wd("scalar", bk.ready)
                    S.wd("scalar", stg_free[hfc % 2])
                    stg = sq2[hfc % 2]
                    atk = S.op("scalar", lambda e, hfc=hfc, ex=ex, bk=bk,
                               stg=stg: e.activation(
                                   stg[:, :], bk.ap[:, :],
                                   AF.Gelu,
                                   bias=eb1_sb[:, ex * 16 + hfc: ex * 16 + hfc + 1]),
                               inc="AC")
                    bk.free = ("AC", atk)
                    S.w("vector", "AC", atk)
                    if hfc == 0 and tc == 0:
                        S.w("vector", "VE", vWBC)
                    tkh = S.op("vector", lambda e, hfc=hfc, tc=tc, he_sb=he_sb,
                               stg=stg: e.tensor_tensor(
                                   he_sb[:, hfc * 512:(hfc + 1) * 512],
                                   stg[:, :],
                                   wbc[:, tc * 512:(tc + 1) * 512], ALU.mult),
                               inc="VE")
                    stg_free[hfc % 2] = ("VE", tkh)
                vHE = S.n["VE"]
            pHE = S.n["PE"]
            nex = (tb, ex + 1) if ex < EXPN - 1 else ((1, 0) if tb == 0 else None)
            S.w("tensor", "VE", vHE)
            for oc in range(NCK):
                sb2 = ew2s[oc % 2]
                S.wd("gpsimd", ew2s_free[oc % 2])
                ssem = "SA" if oc % 2 == 0 else "SB"
                S.op("gpsimd", lambda e, ex=ex, oc=oc, sb2=sb2: e.dma_start(
                    sb2[:, :], ew2r[ex, oc]), inc=ssem, k=16)
                dE2 = S.n[ssem]
                if nex is not None:
                    if oc == 0:
                        S.wd("gpsimd", ("PE", pHE))
                    S.op("gpsimd", lambda e, nex=nex, oc=oc: e.dma_start(
                        ew1b[:, oc * DFF:(oc + 1) * DFF],
                        ew1[nex[1]][oc * 128:(oc + 1) * 128, :]), inc="E1", k=16)
                    if oc == NCK - 1:
                        e1_loaded[nex] = S.n["E1"]
                S.w("tensor", ssem, dE2)
                for tc in range(2):
                    he_sb = he_sb2[tc]
                    bk = ps_m[(oc * 2 + tc) % 2]
                    S.wd("tensor", bk.free)
                    if ex == 0:
                        S.op("tensor", lambda e, oc=oc, tc=tc, bk=bk: e.matmul(
                            bk.ap[:, :], eb2b[:, oc * 128:(oc + 1) * 128],
                            wd_bf[:, tc * 512:(tc + 1) * 512], start=True,
                            stop=False))
                    for hfc in range(16):
                        S.op("tensor", lambda e, oc=oc, hfc=hfc, tc=tc, bk=bk, ex=ex,
                             sb2=sb2, he_sb=he_sb: e.matmul(
                                 bk.ap[:, :], sb2[:, hfc * 128:(hfc + 1) * 128],
                                 he_sb[:, hfc * 512:(hfc + 1) * 512],
                                 start=(hfc == 0 and ex != 0), stop=(hfc == 15)))
                    bk.ready = ("PE", S.tok("tensor", "PE"))
                    S.wd("vector", bk.ready)
                    if ex == 0:
                        tk = S.op("vector", lambda e, oc=oc, tc=tc, bk=bk:
                                  e.tensor_copy(
                                      moeblk[:, oc * 1024 + tc * 512:
                                             oc * 1024 + tc * 512 + 512],
                                      bk.ap[:, :]), inc="VE")
                    else:
                        tk = S.op("vector", lambda e, oc=oc, tc=tc, bk=bk:
                                  e.tensor_tensor(
                                      moeblk[:, oc * 1024 + tc * 512:
                                             oc * 1024 + tc * 512 + 512],
                                      moeblk[:, oc * 1024 + tc * 512:
                                             oc * 1024 + tc * 512 + 512],
                                      bk.ap[:, :], ALU.add), inc="VE")
                        S.vdr()
                    bk.free = ("VE", tk)
                ew2s_free[oc % 2] = ("PE", S.n["PE"])
            he_free = [("PE", S.n["PE"]), ("PE", S.n["PE"])]
        vMOE = S.n["VE"]
        if tb == 0 and "moe" in dbg_aps:
            S.w("sync", "VE", vMOE)
            S.w("sync", "AC", S.n["AC"])
            S.op("sync", lambda e: e.dma_start(dbg_aps["moe"], moeblk[:, :]), inc="DS", k=16)
            S.op("sync", lambda e: e.dma_start(dbg_aps["abel"], abelblk[:, :]), inc="DS", k=16)

        rot_seq = [(moeblk, he_sb2[0]), (he_sb2[0], xnblk), (xnblk, moeblk)]
        for ri in range(3):
            rcur, rnext = rot_seq[ri]
            S.wd("sync", Ebuf_free)
            S.op("sync", lambda e, ri=ri: e.dma_start(Ebuf[:, :], s_E[ri]),
                 inc="EE", k=16)
            dEb = S.n["EE"]
            S.w("tensor", "EE", dEb)
            if ri == 0:
                S.w("tensor", "VE", vMOE)
            for mc in range(NCK):
                for nb2 in range(2):
                    bk = ps_h[(mc * 2 + nb2) % 2]
                    S.wd("tensor", bk.free)
                    for kc in range(NCK):
                        S.op("tensor", lambda e, mc=mc, nb2=nb2, kc=kc, bk=bk,
                             rcur=rcur: e.matmul(
                                 bk.ap[:, :],
                                 Ebuf[:, kc * D + mc * 128: kc * D + mc * 128 + 128],
                                 rcur[:, kc * 1024 + nb2 * 512:
                                      kc * 1024 + nb2 * 512 + 512],
                                 start=(kc == 0), stop=(kc == NCK - 1)))
                    bk.ready = ("PE", S.tok("tensor", "PE"))
                    S.wd("scalar", bk.ready)
                    tk = S.op("scalar", lambda e, mc=mc, nb2=nb2, bk=bk, rnext=rnext:
                              e.activation(
                                  rnext[:, mc * 1024 + nb2 * 512:
                                        mc * 1024 + nb2 * 512 + 512],
                                  bk.ap[:, :], AF.Identity), inc="AC")
                    bk.free = ("AC", tk)
            Ebuf_free = ("PE", S.n["PE"])
        aROT = S.n["AC"]
        stats_pe_guard = ("PE", S.n["PE"])
        if tb == 0:
            S.wd("gpsimd", Ebuf_free)
            S.op("gpsimd", lambda e: e.dma_start(
                abwb[:, :].rearrange("p (c n) -> p c n", n=D),
                abel_w.rearrange("(c p) n -> p c n", p=128)), inc="AB", k=16)
            dABW = S.n["AB"]

        S.w("tensor", "AC", aROT)
        for nb2 in range(2):
            S.wd("tensor", ps_g.free)
            for kc in range(16):
                src = abelblk if kc < 8 else moeblk
                kcc = kc % 8
                S.op("tensor", lambda e, nb2=nb2, kc=kc, kcc=kcc, src=src: e.matmul(
                    ps_g.ap[0:1, :], gcw_bf[:, kc:kc + 1],
                    src[:, kcc * 1024 + nb2 * 512: kcc * 1024 + nb2 * 512 + 512],
                    start=(kc == 0), stop=(kc == 15)))
            ps_g.ready = ("PE", S.tok("tensor", "PE"))
            S.wd("scalar", ps_g.ready)
            tk = S.op("scalar", lambda e, nb2=nb2: e.activation(
                rowa[0:1, nb2 * 512:(nb2 + 1) * 512], ps_g.ap[0:1, :], AF.Sigmoid,
                bias=gcb_sb[:, :]), inc="AC")
            ps_g.free = ("AC", tk)
        aG = S.n["AC"]
        S.w("vector", "AC", aG)
        S.wd("vector", gbc_free)
        S.op("vector", lambda e: e.tensor_copy(row_bf[0:1, :], rowa[0:1, :]))
        vG = S.tok("vector", "VE")
        S.w("tensor", "VE", vG)
        for nb2 in range(2):
            S.wd("tensor", ps_g.free)
            S.op("tensor", lambda e, nb2=nb2: e.matmul(
                ps_g.ap[:, :], ones_r[:, :], row_bf[0:1, nb2 * 512:(nb2 + 1) * 512],
                start=True, stop=True))
            ps_g.ready = ("PE", S.tok("tensor", "PE"))
            S.wd("vector", ps_g.ready)
            tk = S.op("vector", lambda e, nb2=nb2: e.tensor_copy(
                gbc[:, nb2 * 512:(nb2 + 1) * 512], ps_g.ap[:, :]), inc="VE")
            ps_g.free = ("VE", tk)
        vGB = S.n["VE"]
        if tb == 0 and "rot3" in dbg_aps:
            S.w("sync", "VE", vGB)
            S.w("sync", "AC", S.n["AC"])
            S.op("sync", lambda e: e.dma_start(dbg_aps["rot3"], moeblk[:, :]), inc="DS", k=16)
            S.op("sync", lambda e: e.dma_start(dbg_aps["grow"], gbc[:, :]), inc="DS", k=16)

        ybuf = xnblk
        ytmp = he_sb2[0]
        S.w("vector", "VE", vGB)
        for kc in range(NCK):
            S.op("vector", lambda e, kc=kc, ybuf=ybuf: e.tensor_tensor(
                ybuf[:, kc * 1024:(kc + 1) * 1024],
                moeblk[:, kc * 1024:(kc + 1) * 1024],
                abelblk[:, kc * 1024:(kc + 1) * 1024], ALU.subtract))
        S.vdr()
        for kc in range(NCK):
            S.op("vector", lambda e, kc=kc, ybuf=ybuf, ytmp=ytmp: e.tensor_tensor(
                ytmp[:, kc * 1024:(kc + 1) * 1024],
                ybuf[:, kc * 1024:(kc + 1) * 1024], gbc[:, :], ALU.mult))
        S.vdr()
        for kc in range(NCK):
            S.op("vector", lambda e, kc=kc, ybuf=ybuf, ytmp=ytmp: e.tensor_tensor(
                ybuf[:, kc * 1024:(kc + 1) * 1024],
                ytmp[:, kc * 1024:(kc + 1) * 1024],
                abelblk[:, kc * 1024:(kc + 1) * 1024], ALU.add))
        S.vdr()
        for kc in range(NCK):
            S.op("vector", lambda e, kc=kc, ybuf=ybuf, ytmp=ytmp: e.tensor_tensor(
                ytmp[:, kc * 1024:(kc + 1) * 1024],
                ybuf[:, kc * 1024:(kc + 1) * 1024],
                h2blk[:, kc * 1024:(kc + 1) * 1024], ALU.add))
        S.vdr()
        for kc in range(NCK):
            S.op("vector", lambda e, kc=kc, ybuf=ybuf, ytmp=ytmp: e.tensor_scalar(
                ybuf[:, kc * 1024:(kc + 1) * 1024],
                ytmp[:, kc * 1024:(kc + 1) * 1024], gb_v[:, kc:kc + 1], None,
                ALU.mult))
        vY = S.tok("vector", "VE")
        gbc_free = ("VE", vY)
        h2_free = ("VE", vY)
        he_free = [("VE", vY), he_free[1]]

        S.w("tensor", "VE", vY)
        for tl in range(8):
            ti = tb * 8 + tl
            S.wd("sync", x2r_free)
            xsem = "XA" if tl % 2 == 0 else "XB"
            S.op("sync", lambda e, ti=ti, tl=tl: e.dma_start(
                x2r[tl % 2][:, :], s_x2[ti]), inc=xsem, k=16)
            dX2R = S.n[xsem]
            S.wd("tensor", ps_ty.free)
            for jc in range(NCK):
                S.op("tensor", lambda e, jc=jc, tl=tl, ybuf=ybuf: e.transpose(
                    ps_ty.ap[:, jc * 128:(jc + 1) * 128],
                    ybuf[:, jc * 1024 + tl * 128: jc * 1024 + tl * 128 + 128],
                    idb[:, :]))
            ps_ty.ready = ("PE", S.tok("tensor", "PE"))
            S.wd("vector", ps_ty.ready)
            S.w("vector", xsem, dX2R)
            S.wd("vector", x3_free)
            tk = S.op("vector", lambda e, tl=tl: e.tensor_tensor(
                x3[:, :], ps_ty.ap[:, :], x2r[tl % 2][:, :], ALU.add), inc="VE")
            ps_ty.free = ("VE", tk)
            x2r_free = ("VE", tk)
            S.w("sync", "VE", tk)
            S.op("sync", lambda e, ti=ti: e.dma_start(
                out_hbm[ti * 128:(ti + 1) * 128, :], x3[:, :]), inc="OS", k=16)
            x3_free = ("OS", S.n["OS"])
        out_final = S.n["OS"]
    S.w("sync", "OS", out_final)
    S.flush()
    PD.close()
    P.close()
    es.close()
    S.check()
    return nc, dbg_aps


_CACHE = {}


def _vec_fm(v):
    return np.ascontiguousarray(np.asarray(v, dtype=np.float32).reshape(8, 128).T)


def kernel(**inputs):
    nc, _ = _CACHE.get("nc") or _CACHE.setdefault("nc", build_nc())

    x = np.asarray(inputs["L0_attn_x"], dtype=np.float32)
    BF = ml_dtypes.bfloat16
    common = {
        "wq": np.ascontiguousarray(inputs["L0_attn_wq_w"], dtype=np.float32),
        "wk": np.ascontiguousarray(inputs["L0_attn_wk_w"], dtype=np.float32),
        "wv": np.ascontiguousarray(inputs["L0_attn_wv_w"], dtype=np.float32),
        "wo": np.ascontiguousarray(inputs["L0_attn_wo_w"], dtype=np.float32),
        "abel_w": np.ascontiguousarray(inputs["L0_ffn_abel_w"], dtype=np.float32),
        "gate_w": np.ascontiguousarray(inputs["L0_ffn_moe_gate_w"], dtype=np.float32),
        "ew1": np.ascontiguousarray(inputs["L0_ffn_ew1"], dtype=np.float32),
        "ew2r": np.ascontiguousarray(
            np.asarray(inputs["L0_ffn_ew2"], dtype=np.float32)
            .reshape(EXPN, 16, 128, NCK, 128).transpose(0, 3, 2, 1, 4)
            .reshape(EXPN, NCK, 128, 16 * 128)),
        "gens": np.ascontiguousarray(inputs["L0_ffn_gens"], dtype=np.float32),
        "wmod": np.ascontiguousarray(inputs["L0_wmod"], dtype=np.float32),
        "eb2": np.ascontiguousarray(inputs["L0_ffn_eb2"], dtype=np.float32),
        "bq_f": _vec_fm(inputs["L0_attn_wq_b"]),
        "bk_f": _vec_fm(inputs["L0_attn_wk_b"]),
        "bo_f": _vec_fm(inputs["L0_attn_wo_b"]),
        "bv_row": np.ascontiguousarray(
            np.asarray(inputs["L0_attn_wv_b"], dtype=np.float32)[None, :]),
        "abel_b_f": _vec_fm(inputs["L0_ffn_abel_b"]),
        "ln_g_f": _vec_fm(inputs["L0_ffn_ln_gamma"]),
        "ln_b_f": _vec_fm(inputs["L0_ffn_ln_beta"]),
        "eb1_f": np.ascontiguousarray(
            np.asarray(inputs["L0_ffn_eb1"], dtype=np.float32)
            .reshape(EXPN, 16, 128).transpose(2, 0, 1).reshape(128, EXPN * 16)),
        "gate_b": np.ascontiguousarray(
            np.asarray(inputs["L0_ffn_moe_gate_b"], dtype=np.float32)[:, None]),
        "gcw_f": np.ascontiguousarray(
            np.asarray(inputs["L0_ffn_gc_w"], dtype=np.float32)
            .reshape(16, 128).T),
        "gcb": np.ascontiguousarray(
            np.asarray(inputs["L0_ffn_gc_b"], dtype=np.float32).reshape(1, 1)),
        "ident_bf": np.eye(128, dtype=BF),
        "ident_f": np.eye(128, dtype=np.float32),
        "ones_row_bf": np.ones((1, 128), dtype=BF),
        "ones_col_bf": np.ones((128, 1), dtype=BF),
        "onesf_c": np.ones((128, 1), dtype=np.float32),
        "eye_fm": np.ascontiguousarray(
            np.eye(D, dtype=BF).reshape(NCK, 128, D).transpose(1, 0, 2)
            .reshape(128, NCK * D)),
    }
    cvec = np.asarray(inputs["L0_c"], dtype=np.float32)
    in_maps = []
    for core in range(8):
        b, half = core // 2, core % 2
        other = x[b, (1 - half) * TOWN:(2 - half) * TOWN]
        own = x[b, half * TOWN:(half + 1) * TOWN]
        m = dict(common)
        m["xb"] = np.ascontiguousarray(np.concatenate([other, own], axis=0))
        m["c_f"] = _vec_fm(cvec[b])
        in_maps.append(m)
    res = run_bass_kernel_spmd(nc, in_maps, core_ids=list(range(8)))
    full = np.empty((4, T, D), dtype=np.float32)
    for core in range(8):
        b, half = core // 2, core % 2
        full[b, half * TOWN:(half + 1) * TOWN] = res.results[core]["out"]
    return full


# version 15
# speedup vs baseline: 1.0385x; 1.0239x over previous
import math
import numpy as np
import ml_dtypes
from contextlib import ExitStack
import concourse.bass as bass
import concourse.mybir as mybir
from concourse.bass_utils import run_bass_kernel_spmd

F32 = mybir.dt.float32
BF16 = mybir.dt.bfloat16
AF = mybir.ActivationFunctionType
ALU = mybir.AluOpType
AXX = mybir.AxisListType.X

D = 1024
NCK = 8
T = 4096
TOWN = 2048
H = 16
EXPN = 8
DFF = 2048
EPS = 1e-5
SQS = float(0.125 ** 0.5)
NEG = -1.0e30
ENGS = ("tensor", "vector", "scalar", "sync", "gpsimd")
CF = [1.0 / math.factorial(k) for k in range(14)]


class Sched:
    def __init__(self, blk, sems):
        self.blk, self.sems = blk, sems
        self.q = {e: [] for e in ENGS}
        self.n = {s: 0 for s in sems}
        self.tr = {e: [] for e in ENGS}

    def w(self, eng, sem, val):
        if val is None or val <= 0:
            return
        self.q[eng].append(("w", sem, int(val)))
        self.tr[eng].append(("w", sem, int(val)))

    def wd(self, eng, dep):
        if dep is None:
            return
        if isinstance(dep, list):
            for d in dep:
                self.wd(eng, d)
            return
        self.w(eng, dep[0], dep[1])

    def op(self, eng, fn, inc=None, k=1):
        it = ["o", fn, inc, k]
        self.q[eng].append(it)
        self.tr[eng].append(it)
        if inc:
            self.n[inc] += k
            return self.n[inc]
        return None

    def vdr(self):
        self.op("vector", lambda e: e.drain())

    def tok(self, eng, inc):
        for it in reversed(self.q[eng]):
            if it[0] == "o":
                if it[2] is None:
                    it[2] = inc
                    it[3] = 1
                    self.n[inc] += 1
                    return self.n[inc]
                if it[2] == inc:
                    it[3] += 1
                    self.n[inc] += 1
                    return self.n[inc]
                break
        raise RuntimeError(f"tok({eng},{inc}): no attachable instruction")

    def flush(self):
        for e in ENGS:
            ops = self.q[e]
            self.q[e] = []
            if not ops:
                continue

            def run(engine, ops=ops, sems=self.sems):
                for it in ops:
                    if it[0] == "w":
                        engine.wait_ge(sems[it[1]], it[2])
                    else:
                        ins = it[1](engine)
                        if it[2] is not None:
                            ins.then_inc(sems[it[2]], it[3])
            getattr(self.blk, e)(run)

    def barrier(self):
        snap = dict(self.n)
        own = {"tensor": "PE", "scalar": "AC", "vector": "VE"}
        for e in ENGS:
            for s, v in snap.items():
                if v > 0 and own.get(e) != s:
                    self.w(e, s, v)

    def check(self):
        ptr = {e: 0 for e in ENGS}
        cnt = {s: 0 for s in self.sems}
        total = sum(len(v) for v in self.tr.values())
        done, progress = 0, True
        while progress:
            progress = False
            for e in ENGS:
                tr = self.tr[e]
                while ptr[e] < len(tr):
                    it = tr[ptr[e]]
                    if it[0] == "w" and cnt[it[1]] < it[2]:
                        break
                    if it[0] == "o" and it[2] is not None:
                        cnt[it[2]] += it[3]
                    ptr[e] += 1
                    done += 1
                    progress = True
        if done != total:
            stuck = {e: (ptr[e], len(self.tr[e]),
                         self.tr[e][ptr[e]] if ptr[e] < len(self.tr[e]) else None)
                     for e in ENGS}
            raise RuntimeError(f"schedule deadlock: {stuck} sems={cnt}")


class Bank:
    def __init__(self, ap):
        self.ap = ap
        self.ready = None
        self.free = None


def build_nc(dbg=(), stop=None):
    nc = bass.Bass("TRN2", target_bir_lowering=False)

    def din(name, shape, dt):
        return nc.dram_tensor(name, shape, dt, kind="ExternalInput").ap()

    def dscr(name, shape, dt):
        return nc.dram_tensor(name, shape, dt, kind="Internal").ap()

    xb = din("xb", [T, D], F32)
    wq = din("wq", [D, D], F32)
    wk = din("wk", [D, D], F32)
    wv = din("wv", [D, D], F32)
    wo = din("wo", [D, D], F32)
    abel_w = din("abel_w", [D, D], F32)
    gate_w = din("gate_w", [D, EXPN], F32)
    ew1 = din("ew1", [EXPN, D, DFF], F32)
    ew2r = din("ew2r", [EXPN, NCK, 128, 16 * 128], F32)
    gens = din("gens", [3, D, D], F32)
    wmod = din("wmod", [D, 6 * D], F32)
    eb2 = din("eb2", [EXPN, D], F32)
    bq_f = din("bq_f", [128, NCK], F32)
    bk_f = din("bk_f", [128, NCK], F32)
    bo_f = din("bo_f", [128, NCK], F32)
    bv_row = din("bv_row", [1, D], F32)
    abel_b_f = din("abel_b_f", [128, NCK], F32)
    ln_g_f = din("ln_g_f", [128, NCK], F32)
    ln_b_f = din("ln_b_f", [128, NCK], F32)
    c_f = din("c_f", [128, NCK], F32)
    eb1_f = din("eb1_f", [128, EXPN * 16], F32)
    gate_b = din("gate_b", [EXPN, 1], F32)
    gcw_f = din("gcw_f", [128, 16], F32)
    gcb = din("gcb", [1, 1], F32)
    ident_bf = din("ident_bf", [128, 128], BF16)
    ident_f = din("ident_f", [128, 128], F32)
    ones_row_bf = din("ones_row_bf", [1, 128], BF16)
    ones_col_bf = din("ones_col_bf", [128, 1], BF16)
    onesf_c = din("onesf_c", [128, 1], F32)
    eye_fm = din("eye_fm", [128, NCK * D], BF16)

    out_hbm = nc.dram_tensor("out", [TOWN, D], F32, kind="ExternalOutput").ap()

    s_q = dscr("s_q", [NCK, 128, TOWN], BF16)
    s_k = dscr("s_k", [NCK, 128, T], BF16)
    s_v = dscr("s_v", [32, 128, H * 65], BF16)
    s_E = dscr("s_E", [3, 128, NCK * D], BF16)
    s_x2 = dscr("s_x2", [16, 128, D], F32)
    s_h2 = dscr("s_h2", [NCK, 128, TOWN], BF16)

    dbg_aps = {}
    for name, shape, dt in dbg:
        dbg_aps[name] = nc.dram_tensor("dbg_" + name, shape, dt,
                                       kind="ExternalOutput").ap()

    es = ExitStack()
    sems = {}
    for s in ("DW", "DX", "DS", "PE", "AC", "VE", "WA", "WB", "XA", "XB",
              "KA", "KB", "HH", "EE", "SA", "SB", "E1", "AB", "WQ", "H2S",
              "OS", "KS", "QS", "VS", "X2A", "X2B", "XC", "XD"):
        sems[s] = es.enter_context(nc.semaphore(s))
    blk = es.enter_context(nc.Block())
    S = Sched(blk, sems)

    def sbt(name, shape, dt, stack):
        return stack.enter_context(nc.sbuf_tensor(name, shape, dt))

    def pst(name, shape, dt, stack):
        return stack.enter_context(nc.psum_tensor(name, shape, dt))

    def tap(name, src_fn):
        if name in dbg_aps:
            S.op("sync", lambda e, n=name, f=src_fn: e.dma_start(dbg_aps[n], f()),
                 inc="DS", k=16)

    P = ExitStack()
    idb = sbt("idb", [128, 128], BF16, P)
    idf = sbt("idf", [128, 128], F32, P)
    ones_r = sbt("ones_r", [1, 128], BF16, P)
    ones_c = sbt("ones_c", [128, 1], BF16, P)
    onef_c = sbt("onef_c", [128, 1], F32, P)
    bias_q = sbt("bias_q", [128, NCK], F32, P)
    bias_k = sbt("bias_k", [128, NCK], F32, P)
    bias_o = sbt("bias_o", [128, NCK], F32, P)
    abel_bb = sbt("abel_bb", [128, NCK], F32, P)
    ln_g = sbt("ln_g", [128, NCK], F32, P)
    ln_b = sbt("ln_b", [128, NCK], F32, P)
    c_sb = sbt("c_sb", [128, NCK], F32, P)
    eb1_sb = sbt("eb1_sb", [128, EXPN * 16], F32, P)
    gate_b_sb = sbt("gate_b_sb", [EXPN, 1], F32, P)
    gcb_sb = sbt("gcb_sb", [1, 1], F32, P)
    gcw_bf = sbt("gcw_bf", [128, 16], BF16, P)
    eb2b = sbt("eb2b", [EXPN, D], BF16, P)
    gw_bf = sbt("gw_bf", [128, NCK * EXPN], BF16, P)
    mod_sb = sbt("mod_sb", [128, 64], F32, P)
    ca1p = sbt("ca1p", [128, NCK], F32, P)
    cb1p = sbt("cb1p", [128, NCK], F32, P)
    sa_v = sbt("sa_v", [128, NCK], F32, P)
    sbv_v = sbt("sbv_v", [128, NCK], F32, P)
    ga_v = sbt("ga_v", [128, NCK], F32, P)
    gb_v = sbt("gb_v", [128, NCK], F32, P)
    gabo = sbt("gabo", [128, NCK], F32, P)
    eps_c = sbt("eps_c", [128, 1], F32, P)
    zero_c = sbt("zero_c", [128, 1], F32, P)
    mask_sb = sbt("mask_sb", [128, 32], F32, P)
    mk2_sb = sbt("mk2_sb", [128, 16], F32, P)
    ca0_bc = sbt("ca0_bc", [128, 1], F32, P)
    sa0_bc = sbt("sa0_bc", [128, 1], F32, P)
    bvr_bf = sbt("bvr_bf", [1, D], BF16, P)
    cst = sbt("cst", [128, 4], F32, P)

    for dst, src in ((idb, ident_bf), (idf, ident_f), (ones_r, ones_row_bf),
                     (ones_c, ones_col_bf), (onef_c, onesf_c), (bias_q, bq_f),
                     (bias_k, bk_f), (bias_o, bo_f), (abel_bb, abel_b_f),
                     (ln_g, ln_g_f), (ln_b, ln_b_f), (c_sb, c_f),
                     (eb1_sb, eb1_f), (gate_b_sb, gate_b), (gcb_sb, gcb)):
        S.op("sync", lambda e, d=dst, s=src: e.dma_start(d[:, :], s), inc="DX", k=16)
    nDX0 = S.n["DX"]
    S.op("gpsimd", lambda e: e.dma_start(eb2b[:, :], eb2), inc="DW", k=16)
    S.op("gpsimd", lambda e: e.dma_start(gcw_bf[:, :], gcw_f), inc="DW", k=16)
    S.op("gpsimd", lambda e: e.dma_start(bvr_bf[:, :], bv_row), inc="DW", k=16)
    S.op("gpsimd", lambda e: e.dma_start(
        gw_bf[:, :].rearrange("p (c n) -> p c n", n=EXPN),
        gate_w.rearrange("(c p) n -> p c n", p=128)), inc="DW", k=16)
    nDWsmall = S.n["DW"]

    S.w("vector", "DX", nDX0)
    S.op("vector", lambda e: e.memset(eps_c[:, :], EPS))
    S.op("vector", lambda e: e.memset(zero_c[:, :], 0.0))
    S.op("vector", lambda e: e.memset(cst[:, 0:1], CF[8]))
    S.op("vector", lambda e: e.memset(cst[:, 1:2], CF[9]))
    S.op("vector", lambda e: e.memset(mk2_sb[:, :], 0.0))
    S.op("vector", lambda e: e.tensor_scalar(mod_sb[:, 48:56], bias_q[:, :], SQS,
                                             None, ALU.mult))
    S.op("vector", lambda e: e.drain())
    S.op("vector", lambda e: e.tensor_copy(bias_q[:, :], mod_sb[:, 48:56]))
    S.op("vector", lambda e: e.drain())
    S.op("vector", lambda e: e.tensor_scalar(mod_sb[:, 48:56], bias_k[:, :], SQS,
                                             None, ALU.mult))
    S.op("vector", lambda e: e.drain())
    S.op("vector", lambda e: e.tensor_copy(bias_k[:, :], mod_sb[:, 48:56]))
    vINIT = S.tok("vector", "VE")

    P0 = ExitStack()
    cs_f = sbt("cs_f", [128, NCK], F32, P0)
    wm_buf = [sbt(f"wm{i}", [128, 6 * D], F32, P0) for i in range(2)]
    ps_mod = pst("ps_mod", [128, 64], F32, P0)
    ps_bc0 = pst("ps_bc0", [128, 512], F32, P0)
    t_bc = sbt("t_bc", [1, 8], BF16, P0)

    S.w("scalar", "DX", nDX0)
    aSIL = S.op("scalar", lambda e: e.activation(cs_f[:, :], c_sb[:, :], AF.Silu),
                inc="AC")
    S.w("vector", "VE", vINIT)
    S.op("vector", lambda e: e.memset(mod_sb[:, 0:48], 0.0))
    vMZ = S.tok("vector", "VE")
    wm_free = [None, None]
    acc_tok = vMZ
    for kc in range(NCK):
        b = kc % 2
        S.wd("sync", wm_free[b])
        wsem = "WA" if b == 0 else "WB"
        S.op("sync", lambda e, kc=kc, b=b: e.dma_start(
            wm_buf[b][:, :], wmod[kc * 128:(kc + 1) * 128, :]), inc=wsem, k=16)
        S.w("tensor", wsem, S.n[wsem])
        if kc == 0:
            S.w("tensor", "AC", aSIL)
        S.w("tensor", "VE", acc_tok)
        for oc in range(48):
            S.op("tensor", lambda e, b=b, oc=oc, kc=kc: e.matmul(
                ps_mod[:, oc:oc + 1], wm_buf[b][:, oc * 128:(oc + 1) * 128],
                cs_f[:, kc:kc + 1], start=True, stop=True))
        wm_free[b] = ("PE", S.tok("tensor", "PE"))
        S.wd("vector", wm_free[b])
        S.op("vector", lambda e: e.tensor_tensor(
            mod_sb[:, 0:48], mod_sb[:, 0:48], ps_mod[:, 0:48], ALU.add))
        S.op("vector", lambda e: e.drain())
        acc_tok = S.tok("vector", "VE")
    pMOD = S.n["PE"]

    S.w("vector", "VE", vINIT)
    S.w("vector", "PE", pMOD)
    S.op("vector", lambda e: e.tensor_scalar(ca1p[:, :], mod_sb[:, 8:16], 1.0, None,
                                             ALU.add))
    S.op("vector", lambda e: e.tensor_scalar(cb1p[:, :], mod_sb[:, 32:40], 1.0, None,
                                             ALU.add))
    S.op("vector", lambda e: e.tensor_copy(sa_v[:, :], mod_sb[:, 0:8]))
    S.op("vector", lambda e: e.tensor_copy(sbv_v[:, :], mod_sb[:, 24:32]))
    for gdst, c0 in ((ga_v, 16), (gb_v, 40)):
        S.op("vector", lambda e, c0=c0: e.tensor_scalar(
            mod_sb[:, 48:56], mod_sb[:, c0:c0 + 8], -1.0, None, ALU.mult))
        S.vdr()
        S.op("vector", lambda e, c0=c0: e.tensor_tensor(
            mod_sb[:, 56:64], mod_sb[:, 48:56], mod_sb[:, c0:c0 + 8], ALU.max))
        S.vdr()
        S.op("vector", lambda e, g=gdst: e.tensor_scalar(
            g[:, :], mod_sb[:, 56:64], 0.25, None, ALU.max))
        S.vdr()
        S.op("vector", lambda e, c0=c0: e.tensor_scalar(
            mod_sb[:, 48:56], mod_sb[:, c0:c0 + 8], zero_c[:, :], None, ALU.is_ge))
        S.vdr()
        S.op("vector", lambda e: e.tensor_scalar(
            mod_sb[:, 56:64], mod_sb[:, 48:56], 2.0, None, ALU.mult))
        S.vdr()
        S.op("vector", lambda e: e.tensor_scalar(
            mod_sb[:, 48:56], mod_sb[:, 56:64], -1.0, None, ALU.add))
        S.vdr()
        S.op("vector", lambda e, g=gdst: e.tensor_tensor(
            mod_sb[:, 56:64], g[:, :], mod_sb[:, 48:56], ALU.mult))
        S.vdr()
        S.op("vector", lambda e, g=gdst: e.tensor_copy(g[:, :], mod_sb[:, 56:64]))
        S.vdr()
    S.op("vector", lambda e: e.tensor_tensor(gabo[:, :], ga_v[:, :], bias_o[:, :],
                                             ALU.mult))
    S.op("vector", lambda e: e.tensor_copy(t_bc[0:1, 0:1], ca1p[0:1, 0:1]))
    S.op("vector", lambda e: e.tensor_copy(t_bc[0:1, 1:2], sa_v[0:1, 0:1]))
    vPAR = S.tok("vector", "VE")

    S.w("tensor", "VE", vPAR)
    S.w("tensor", "DW", nDWsmall)
    S.op("tensor", lambda e: e.matmul(ps_bc0[:, 0:2], ones_r[:, :], t_bc[0:1, 0:2],
                                      start=True, stop=True))
    pB0 = S.tok("tensor", "PE")
    S.w("vector", "PE", pB0)
    S.op("vector", lambda e: e.tensor_copy(ca0_bc[:, :], ps_bc0[:, 0:1]))
    S.op("vector", lambda e: e.tensor_copy(sa0_bc[:, :], ps_bc0[:, 1:2]))
    vSETUP = S.tok("vector", "VE")
    if "mod" in dbg_aps:
        S.w("sync", "VE", vSETUP)
        S.op("sync", lambda e: e.dma_start(dbg_aps["mod"], mod_sb[:, 0:48]), inc="DS", k=16)
        S.op("sync", lambda e: e.dma_start(dbg_aps["ga"], ga_v[:, :]), inc="DS", k=16)
        S.op("sync", lambda e: e.dma_start(dbg_aps["gb"], gb_v[:, :]), inc="DS", k=16)
        S.op("sync", lambda e: e.dma_start(dbg_aps["ca1p"], ca1p[:, :]), inc="DS", k=16)
        S.op("sync", lambda e: e.dma_start(dbg_aps["sa"], sa_v[:, :]), inc="DS", k=16)
    S.barrier()
    S.flush()
    P0.close()
    if stop == "p0":
        P.close(); es.close(); S.check(); return nc, dbg_aps

    PX = ExitStack()
    A_sb = sbt("A_sb", [128, NCK * D], BF16, PX)
    X2 = sbt("X2", [128, NCK * D], BF16, PX)
    X4 = sbt("X4", [128, NCK * D], BF16, PX)
    X6 = sbt("X6", [128, NCK * D], BF16, PX)
    TMP = sbt("TMP", [128, NCK * D], BF16, PX)
    WES = sbt("WES", [128, NCK * D], BF16, PX)
    EB = sbt("EB", [128, NCK * D], BF16, PX)
    OB = sbt("OB", [128, NCK * D], BF16, PX)
    EYE = sbt("EYE", [128, NCK * D], BF16, PX)
    ps_x = [Bank(pst(f"ps_x{i}", [128, 512], F32, PX)) for i in range(2)]

    S.op("sync", lambda e: e.dma_start(EYE[:, :], eye_fm), inc="DX", k=16)
    dEYE = S.n["DX"]

    def mm_big(dst, lhs, rhs, scale):
        for mc in range(NCK):
            for nb in range(2):
                bk = ps_x[(mc * 2 + nb) % 2]
                S.wd("tensor", bk.free)
                for kc in range(NCK):
                    S.op("tensor", lambda e, mc=mc, nb=nb, kc=kc, bk=bk, lhs=lhs,
                         rhs=rhs: e.matmul(
                             bk.ap[:, :],
                             lhs[:, kc * D + mc * 128: kc * D + mc * 128 + 128],
                             rhs[:, kc * D + nb * 512: kc * D + nb * 512 + 512],
                             start=(kc == 0), stop=(kc == NCK - 1)))
                bk.ready = ("PE", S.tok("tensor", "PE"))
                S.wd("scalar", bk.ready)
                tk = S.op("scalar", lambda e, mc=mc, nb=nb, bk=bk, dst=dst,
                          scale=scale: e.activation(
                              dst[:, mc * D + nb * 512: mc * D + nb * 512 + 512],
                              bk.ap[:, :], AF.Identity, scale=scale), inc="AC")
                bk.free = ("AC", tk)
        return S.n["AC"]

    def stt(dst, in0, sc, in1):
        S.op("vector", lambda e, dst=dst, in0=in0, sc=sc, in1=in1:
             e.scalar_tensor_tensor(dst[:, :], in0[:, :], sc, in1[:, :],
                                    ALU.mult, ALU.add))
        S.vdr()

    for gi in range(3):
        S.op("gpsimd", lambda e, gi=gi: e.dma_start(
            A_sb[:, :].rearrange("p (c n) -> p c n", n=D),
            gens[gi].rearrange("(c p) n -> p c n", p=128)), inc="WQ", k=16)
        S.w("tensor", "WQ", S.n["WQ"])
        if gi == 0:
            S.w("tensor", "VE", vSETUP)
            S.w("scalar", "VE", vSETUP)
        a2 = mm_big(X2, A_sb, A_sb, -1.0)
        if stop == "xa2":
            S.flush(); PX.close(); P.close(); es.close(); S.check()
            return nc, dbg_aps
        S.w("tensor", "AC", a2)
        a4 = mm_big(X4, X2, X2, 1.0)
        S.w("tensor", "AC", a4)
        a6 = mm_big(X6, X2, X4, 1.0)
        S.w("vector", "AC", a6)
        if gi == 0:
            S.w("vector", "DX", dEYE)
        stt(TMP, X4, CF[10] / CF[8], X2)
        stt(TMP, X6, CF[12] / CF[8], TMP)
        v1 = S.tok("vector", "VE")
        S.w("tensor", "VE", v1)
        we = mm_big(WES, X6, TMP, cst[:, 0:1])
        S.w("vector", "AC", we)
        stt(TMP, X4, CF[4] / CF[2], X2)
        stt(TMP, X6, CF[6] / CF[2], TMP)
        stt(EB, TMP, CF[2], WES)
        S.op("vector", lambda e: e.tensor_tensor(EB[:, :], EB[:, :], EYE[:, :],
                                                 ALU.add))
        stt(TMP, X4, CF[11] / CF[9], X2)
        stt(TMP, X6, CF[13] / CF[9], TMP)
        v2 = S.tok("vector", "VE")
        S.w("tensor", "VE", v2)
        ws = mm_big(WES, X6, TMP, cst[:, 1:2])
        S.w("vector", "AC", ws)
        stt(TMP, X4, CF[5] / CF[3], X2)
        stt(TMP, X6, CF[7] / CF[3], TMP)
        stt(TMP, TMP, CF[3], WES)
        S.op("vector", lambda e: e.tensor_tensor(TMP[:, :], TMP[:, :], EYE[:, :],
                                                 ALU.add))
        v3 = S.tok("vector", "VE")
        S.w("tensor", "VE", v3)
        ob = mm_big(OB, A_sb, TMP, -1.0)
        S.w("vector", "AC", ob)
        S.op("vector", lambda e: e.tensor_tensor(EB[:, :], EB[:, :], OB[:, :],
                                                 ALU.add))
        v4 = S.tok("vector", "VE")
        S.w("sync", "VE", v4)
        S.op("sync", lambda e, gi=gi: e.dma_start(s_E[gi], EB[:, :]), inc="DS", k=16)
        S.w("gpsimd", "AC", ob)
        S.w("vector", "DS", S.n["DS"])
        if stop == "xg0":
            S.flush(); PX.close(); P.close(); es.close(); S.check()
            return nc, dbg_aps
    if "E0" in dbg_aps:
        S.w("sync", "DS", S.n["DS"])
        for gi in range(3):
            S.op("sync", lambda e, gi=gi: e.dma_start(dbg_aps[f"E{gi}"], s_E[gi]),
                 inc="DS", k=16)
    S.barrier()
    S.flush()
    PX.close()
    if stop == "x":
        P.close(); es.close(); S.check(); return nc, dbg_aps

    PA = ExitStack()
    wq_T = sbt("wq_T", [128, NCK * D], BF16, PA)
    wk_T = sbt("wk_T", [128, NCK * D], BF16, PA)
    wv_T = sbt("wv_T", [128, NCK * D], BF16, PA)
    wraw = sbt("wraw", [128, NCK * D], BF16, PA)
    x_t = [sbt(f"x_t{i}", [128, D], F32, PA) for i in range(4)]
    scr = sbt("scr", [128, D], F32, PA)
    scr_b = sbt("scr_b", [128, D], F32, PA)
    z_t = [sbt(f"z_t{i}", [128, D], BF16, PA) for i in range(4)]
    h_sl = [sbt(f"h_sl{i}", [128, NCK * 512], BF16, PA) for i in range(2)]
    k_out = sbt("k_out", [128, NCK * 512], BF16, PA)
    q_out = sbt("q_out", [128, NCK * 512], BF16, PA)
    v_out = sbt("v_out", [128, 4 * H * 65], BF16, PA)
    ksq = sbt("ksq", [128, NCK * 512], BF16, PA)
    st_cs = [sbt(f"st_c{i}", [128, 12], F32, PA) for i in range(4)]
    bv_bc = sbt("bv_bc", [128, D], F32, PA)
    ps_tr = [Bank(pst(f"ps_tr{i}", [128, 1024], BF16, PA)) for i in range(2)]
    ps_pj = [Bank(pst(f"ps_pj{i}", [128, 512], F32, PA)) for i in range(2)]
    ps_nb = Bank(pst("ps_nb", [128, 512], F32, PA))

    wraw_free = None
    for wsrc, wdst, wscale in ((wq, wq_T, SQS), (wk, wk_T, SQS), (wv, wv_T, 1.0)):
        S.wd("gpsimd", wraw_free)
        S.op("gpsimd", lambda e, wsrc=wsrc: e.dma_start(
            wraw[:, :].rearrange("p (c n) -> p c n", n=D),
            wsrc.rearrange("(c p) n -> p c n", p=128)), inc="WQ", k=16)
        S.w("tensor", "WQ", S.n["WQ"])
        for oc in range(NCK):
            for ic in range(NCK):
                bk = ps_tr[(oc * NCK + ic) % 2]
                S.wd("tensor", bk.free)
                S.op("tensor", lambda e, oc=oc, ic=ic, bk=bk: e.transpose(
                    bk.ap[:, 0:128],
                    wraw[:, oc * D + ic * 128: oc * D + ic * 128 + 128], idb[:, :]))
                bk.ready = ("PE", S.tok("tensor", "PE"))
                S.wd("scalar", bk.ready)
                tk = S.op("scalar", lambda e, oc=oc, ic=ic, bk=bk, wdst=wdst,
                          wscale=wscale: e.activation(
                              wdst[:, ic * D + oc * 128: ic * D + oc * 128 + 128],
                              bk.ap[:, 0:128], AF.Identity, scale=wscale), inc="AC")
                bk.free = ("AC", tk)
        wraw_free = ("AC", S.n["AC"])
    aWT = S.n["AC"]
    if "wkT" in dbg_aps:
        S.w("sync", "AC", aWT)
        S.op("sync", lambda e: e.dma_start(dbg_aps["wkT"], wk_T[:, :]), inc="DS", k=16)
        S.op("sync", lambda e: e.dma_start(dbg_aps["wqT"], wq_T[:, :]), inc="DS", k=16)
        S.op("sync", lambda e: e.dma_start(dbg_aps["wvT"], wv_T[:, :]), inc="DS", k=16)

    S.w("tensor", "VE", vSETUP)
    for half in range(2):
        S.wd("tensor", ps_nb.free)
        S.op("tensor", lambda e, half=half: e.matmul(
            ps_nb.ap[:, :], ones_r[:, :], bvr_bf[0:1, half * 512:(half + 1) * 512],
            start=True, stop=True))
        ps_nb.ready = ("PE", S.tok("tensor", "PE"))
        S.wd("vector", ps_nb.ready)
        tk = S.op("vector", lambda e, half=half: e.tensor_copy(
            bv_bc[:, half * 512:(half + 1) * 512], ps_nb.ap[:, :]), inc="VE")
        ps_nb.free = ("VE", tk)

    x_free = [None, None, None, None]
    z_free = [None, None, None, None]
    h_proj_done = [None, None]
    kout_free = None
    qout_free = None
    vout_free = None

    XSEMS = ("XA", "XB", "XC", "XD")

    def stats_a(slab):
        base = slab * 4
        dxs = []
        for tt in range(4):
            ti = base + tt
            S.wd("sync", x_free[tt])
            S.op("sync", lambda e, ti=ti, tt=tt: e.dma_start(
                x_t[tt][:, :], xb[ti * 128:(ti + 1) * 128, :]), inc=XSEMS[tt],
                k=16)
            dxs.append(S.n[XSEMS[tt]])
        if slab == 0:
            S.w("scalar", "VE", vSETUP)
            S.w("vector", "VE", vSETUP)
        ac1s = []
        for tt in range(4):
            S.w("scalar", XSEMS[tt], dxs[tt])
            S.op("scalar", lambda e, tt=tt: e.activation(
                scr[:, :], x_t[tt][:, :], AF.Identity,
                accum_out=st_cs[tt][:, 0:1]))
            S.op("scalar", lambda e, tt=tt: e.activation(
                scr_b[:, :], x_t[tt][:, :], AF.Square,
                accum_out=st_cs[tt][:, 1:2]), inc="AC")
            ac1s.append(S.n["AC"])
        ve1s = [None] * 4
        for step in range(3):
            for tt in range(4):
                st = st_cs[tt]
                if step == 0:
                    S.w("vector", "AC", ac1s[tt])
                    S.op("vector", lambda e, st=st: e.tensor_scalar(
                        st[:, 2:3], st[:, 0:1], 1.0 / D, None, ALU.mult))
                elif step == 1:
                    S.op("vector", lambda e, st=st: e.tensor_tensor(
                        st[:, 3:4], st[:, 2:3], st[:, 2:3], ALU.mult))
                else:
                    S.op("vector", lambda e, st=st: e.scalar_tensor_tensor(
                        st[:, 4:5], st[:, 1:2], 1.0 / D, st[:, 3:4], ALU.mult,
                        ALU.subtract))
                    ve1s[tt] = S.tok("vector", "VE")
        acsq = []
        for tt in range(4):
            S.w("scalar", "VE", ve1s[tt])
            S.op("scalar", lambda e, tt=tt: e.activation(
                st_cs[tt][:, 5:6], st_cs[tt][:, 4:5], AF.Sqrt,
                bias=eps_c[:, :]), inc="AC")
            acsq.append(S.n["AC"])
        vez = [None] * 4
        vex = [None] * 4
        for step in range(9):
            for tt in range(4):
                st = st_cs[tt]
                if step == 0:
                    S.w("vector", "AC", acsq[tt])
                    S.op("vector", lambda e, st=st: e.reciprocal(
                        st[:, 5:6], st[:, 5:6]))
                elif step == 1:
                    S.op("vector", lambda e, st=st: e.scalar_tensor_tensor(
                        st[:, 6:7], st[:, 2:3], -1.0, st[:, 5:6], ALU.mult,
                        ALU.mult))
                    vez[tt] = S.tok("vector", "VE")
                elif step == 2:
                    S.op("vector", lambda e, st=st, tt=tt: e.tensor_scalar(
                        st[:, 7:8], x_t[tt][:, 0:1], st[:, 2:3], None,
                        ALU.subtract))
                    vex[tt] = S.tok("vector", "VE")
                elif step == 3:
                    S.op("vector", lambda e, st=st: e.tensor_tensor(
                        st[:, 8:9], st[:, 7:8], st[:, 5:6], ALU.mult))
                elif step == 4:
                    S.op("vector", lambda e, st=st: e.tensor_tensor(
                        st[:, 7:8], st[:, 8:9], ca0_bc[:, :], ALU.mult))
                elif step == 5:
                    S.op("vector", lambda e, st=st: e.tensor_tensor(
                        st[:, 8:9], st[:, 7:8], sa0_bc[:, :], ALU.add))
                elif step == 6:
                    S.op("vector", lambda e, st=st: e.tensor_scalar(
                        st[:, 7:8], st[:, 8:9], zero_c[:, :], None,
                        ALU.is_equal))
                elif step == 7:
                    S.op("vector", lambda e, st=st: e.tensor_scalar(
                        st[:, 9:10], st[:, 7:8], -1.0, None, ALU.mult))
                else:
                    S.op("vector", lambda e, st=st, ti=base + tt: e.tensor_scalar(
                        mask_sb[:, ti:ti + 1], st[:, 9:10], 1.0, None, ALU.add))
        return vez, vex

    def stats_b(slab, vez, vex):
        hb = slab % 2
        acz = [None] * 4
        for tt in range(4):
            S.w("scalar", "VE", vez[tt])
            S.wd("scalar", z_free[tt])
            S.op("scalar", lambda e, tt=tt: e.activation(
                z_t[tt][:, :], x_t[tt][:, :], AF.Identity,
                bias=st_cs[tt][:, 6:7], scale=st_cs[tt][:, 5:6]), inc="AC")
            acz[tt] = S.n["AC"]
            x_free[tt] = [("AC", acz[tt]), ("VE", vex[tt])]
        for tt in range(4):
            S.w("tensor", "AC", acz[tt])
            bk = ps_tr[tt % 2]
            S.wd("tensor", bk.free)
            if tt == 0:
                S.wd("tensor", h_proj_done[hb])
            for j in range(NCK):
                S.op("tensor", lambda e, j=j, tt=tt, bk=bk: e.transpose(
                    bk.ap[:, j * 128:(j + 1) * 128],
                    z_t[tt][:, j * 128:(j + 1) * 128], idb[:, :]))
            bk.ready = ("PE", S.tok("tensor", "PE"))
            z_free[tt] = bk.ready
            S.wd("scalar", bk.ready)
            for j in range(NCK):
                S.op("scalar", lambda e, j=j, tt=tt, hb=hb, bk=bk: e.activation(
                    h_sl[hb][:, j * 512 + tt * 128: j * 512 + tt * 128 + 128],
                    bk.ap[:, j * 128:(j + 1) * 128], AF.Identity,
                    bias=sa_v[:, j:j + 1], scale=ca1p[:, j:j + 1]))
            tk = S.tok("scalar", "AC")
            bk.free = ("AC", tk)
        return S.n["AC"]

    _vz, _vx = stats_a(0)
    hr = {0: stats_b(0, _vz, _vx)}
    pend = {}
    for slab in range(NCK):
        hb = slab % 2
        if slab + 1 < NCK:
            pend[slab + 1] = stats_a(slab + 1)
        h_ready = hr[slab]
        if slab == 0 and "hsl0" in dbg_aps:
            S.w("sync", "AC", h_ready)
            S.op("sync", lambda e, hb=hb: e.dma_start(dbg_aps["hsl0"], h_sl[hb][:, :]),
                 inc="DS", k=16)
            S.op("sync", lambda e: e.dma_start(dbg_aps["stc0"], st_cs[0][:, 0:12]), inc="DS", k=16)

        S.w("tensor", "AC", max(h_ready, aWT))
        S.wd("tensor", kout_free)
        for oc in range(NCK):
            bk = ps_pj[oc % 2]
            S.wd("tensor", bk.free)
            for kc in range(NCK):
                S.op("tensor", lambda e, oc=oc, kc=kc, bk=bk, hb=hb: e.matmul(
                    bk.ap[:, :], wk_T[:, kc * D + oc * 128: kc * D + oc * 128 + 128],
                    h_sl[hb][:, kc * 512:(kc + 1) * 512],
                    start=(kc == 0), stop=(kc == NCK - 1)))
            bk.ready = ("PE", S.tok("tensor", "PE"))
            S.wd("scalar", bk.ready)
            tk = S.op("scalar", lambda e, oc=oc, bk=bk: e.activation(
                k_out[:, oc * 512:(oc + 1) * 512], bk.ap[:, :], AF.Identity,
                bias=bias_k[:, oc:oc + 1]), inc="AC")
            bk.free = ("AC", tk)
        ack = S.n["AC"]
        if slab == 0 and "kout0" in dbg_aps:
            S.w("sync", "AC", ack)
            S.op("sync", lambda e: e.dma_start(dbg_aps["kout0"], k_out[:, :]), inc="DS", k=16)
        S.w("vector", "AC", ack)
        S.op("vector", lambda e: e.tensor_tensor(ksq[:, :], k_out[:, :], k_out[:, :],
                                                 ALU.mult))
        vks = S.tok("vector", "VE")
        S.w("tensor", "VE", vks)
        for oc in range(NCK):
            S.wd("tensor", ps_nb.free)
            S.op("tensor", lambda e, oc=oc: e.matmul(
                ps_nb.ap[0:1, :], ones_c[0:64, :], ksq[0:64, oc * 512:(oc + 1) * 512],
                start=True, stop=True))
            S.op("tensor", lambda e, oc=oc: e.matmul(
                ps_nb.ap[64:65, :], ones_c[64:128, :],
                ksq[64:128, oc * 512:(oc + 1) * 512], start=True, stop=True))
            ps_nb.ready = ("PE", S.tok("tensor", "PE"))
            S.wd("vector", ps_nb.ready)
            S.op("vector", lambda e: e.tensor_reduce(st_cs[0][:, 9:10], ps_nb.ap[:, :],
                                                     AXX, ALU.max))
            S.vdr()
            S.op("vector", lambda e, oc=oc: e.tensor_tensor(
                st_cs[0][0:1, 10:11], mk2_sb[0:1, 2 * oc:2 * oc + 1],
                st_cs[0][0:1, 9:10], ALU.max))
            S.vdr()
            S.op("vector", lambda e, oc=oc: e.tensor_copy(
                mk2_sb[0:1, 2 * oc:2 * oc + 1], st_cs[0][0:1, 10:11]))
            S.vdr()
            S.op("vector", lambda e, oc=oc: e.tensor_tensor(
                st_cs[0][64:65, 10:11], mk2_sb[64:65, 2 * oc + 1:2 * oc + 2],
                st_cs[0][64:65, 9:10], ALU.max))
            S.vdr()
            tk = S.op("vector", lambda e, oc=oc: e.tensor_copy(
                mk2_sb[64:65, 2 * oc + 1:2 * oc + 2], st_cs[0][64:65, 10:11]),
                inc="VE")
            ps_nb.free = ("VE", tk)

        if slab + 1 < NCK:
            hr[slab + 1] = stats_b(slab + 1, *pend[slab + 1])

        if slab >= 4:
            S.wd("tensor", qout_free)
            for oc in range(NCK):
                bk = ps_pj[oc % 2]
                S.wd("tensor", bk.free)
                for kc in range(NCK):
                    S.op("tensor", lambda e, oc=oc, kc=kc, bk=bk, hb=hb: e.matmul(
                        bk.ap[:, :],
                        wq_T[:, kc * D + oc * 128: kc * D + oc * 128 + 128],
                        h_sl[hb][:, kc * 512:(kc + 1) * 512],
                        start=(kc == 0), stop=(kc == NCK - 1)))
                bk.ready = ("PE", S.tok("tensor", "PE"))
                S.wd("scalar", bk.ready)
                tk = S.op("scalar", lambda e, oc=oc, bk=bk: e.activation(
                    q_out[:, oc * 512:(oc + 1) * 512], bk.ap[:, :], AF.Identity,
                    bias=bias_q[:, oc:oc + 1]), inc="AC")
                bk.free = ("AC", tk)

        S.wd("vector", vout_free)
        for st4 in range(4):
            S.op("vector", lambda e, st4=st4: e.memset(
                v_out[:, st4 * H * 65:(st4 + 1) * H * 65]
                .rearrange("p (h n) -> p h n", n=65)[:, :, 64:65], 1.0))
        S.vdr()
        vme = S.tok("vector", "VE")
        for st4 in range(4):
            for on in range(2):
                bk = ps_pj[(st4 * 2 + on) % 2]
                S.wd("tensor", bk.free)
                for kc in range(NCK):
                    S.op("tensor", lambda e, st4=st4, on=on, kc=kc, bk=bk, hb=hb:
                         e.matmul(
                             bk.ap[:, :],
                             h_sl[hb][:, kc * 512 + st4 * 128: kc * 512 + st4 * 128 + 128],
                             wv_T[:, kc * D + on * 512: kc * D + on * 512 + 512],
                             start=(kc == 0), stop=(kc == NCK - 1)))
                bk.ready = ("PE", S.tok("tensor", "PE"))
                S.wd("vector", bk.ready)
                if st4 == 0 and on == 0:
                    S.w("vector", "VE", vme)
                tk = S.op("vector", lambda e, st4=st4, on=on, bk=bk: e.tensor_tensor(
                    v_out[:, st4 * H * 65 + on * 8 * 65:
                          st4 * H * 65 + (on + 1) * 8 * 65]
                    .rearrange("p (h n) -> p h n", n=65)[:, :, 0:64],
                    bk.ap[:, :].rearrange("p (h n) -> p h n", n=64),
                    bv_bc[:, on * 512:(on + 1) * 512]
                    .rearrange("p (h n) -> p h n", n=64), ALU.add), inc="VE")
                bk.free = ("VE", tk)
        S.vdr()
        for st4 in range(4):
            S.op("vector", lambda e, st4=st4, slab=slab: e.tensor_scalar(
                v_out[:, st4 * H * 65:(st4 + 1) * H * 65],
                v_out[:, st4 * H * 65:(st4 + 1) * H * 65],
                mask_sb[:, slab * 4 + st4: slab * 4 + st4 + 1], None, ALU.mult))
            S.vdr()
        S.tok("vector", "VE")
        h_proj_done[hb] = ("PE", S.n["PE"])
        vev = S.n["VE"]

        S.w("sync", "AC", S.n["AC"])
        S.w("sync", "VE", vev)
        S.op("sync", lambda e, slab=slab: e.dma_start(
            s_k[:, :, slab * 512:(slab + 1) * 512].rearrange("c p n -> p c n"),
            k_out[:, :].rearrange("p (c n) -> p c n", n=512)), inc="KS", k=16)
        kout_free = ("KS", S.n["KS"])
        if slab >= 4:
            S.op("sync", lambda e, slab=slab: e.dma_start(
                s_q[:, :, (slab - 4) * 512:(slab - 3) * 512].rearrange("c p n -> p c n"),
                q_out[:, :].rearrange("p (c n) -> p c n", n=512)), inc="QS", k=16)
            qout_free = ("QS", S.n["QS"])
        S.op("sync", lambda e, slab=slab: e.dma_start(
            s_v[slab * 4:(slab + 1) * 4, :, :].rearrange("s p n -> p s n"),
            v_out[:, :].rearrange("p (s n) -> p s n", n=H * 65)), inc="VS", k=16)
        vout_free = ("VS", S.n["VS"])
    dA_done = (S.n["KS"], S.n["QS"], S.n["VS"])
    if "sk" in dbg_aps:
        S.w("sync", "KS", dA_done[0])
        S.w("sync", "QS", dA_done[1])
        S.w("sync", "VS", dA_done[2])
        S.op("sync", lambda e: e.dma_start(dbg_aps["sk"], s_k), inc="DS", k=16)
        S.op("sync", lambda e: e.dma_start(dbg_aps["sq"], s_q), inc="DS", k=16)
        S.op("sync", lambda e: e.dma_start(dbg_aps["sv"], s_v), inc="DS", k=16)
        S.op("sync", lambda e: e.dma_start(dbg_aps["mask"], mask_sb[:, :]), inc="DS", k=16)
        dA_done = S.n["DS"]
    S.barrier()
    S.flush()
    PA.close()
    if stop == "a":
        P.close(); es.close(); S.check(); return nc, dbg_aps

    PBC = ExitStack()
    out_attn = sbt("out_attn", [128, NCK * TOWN], BF16, PBC)

    PB = ExitStack()
    q_sb = sbt("q_sb", [128, NCK * TOWN], BF16, PB)
    v_sb = sbt("v_sb", [128, 32 * H * 65], BF16, PB)
    k_pair = [sbt(f"k_pair{i}", [128, T], BF16, PB) for i in range(2)]
    p_t = [sbt(f"p_t{i}", [128, 1024], BF16, PB) for i in range(3)]
    qsq = sbt("qsq", [128, TOWN], BF16, PB)
    bias_all = sbt("bias_all", [128, H * 32], F32, PB)
    rrow = sbt("rrow", [1, 1024], F32, PB)
    rrow_bf = sbt("rrow_bf", [1, 1024], BF16, PB)
    rb_bc = sbt("rb_bc", [128, 512], F32, PB)
    nrm = sbt("nrm", [128, 4], F32, PB)
    crow = sbt("crow", [1, 16], F32, PB)
    c_bf = sbt("c_bf", [1, 16], BF16, PB)
    ps_sc = [Bank(pst(f"ps_sc{i}", [128, 1024], F32, PB)) for i in range(3)]
    ps_av = [Bank(pst(f"ps_av{i}", [65, 512], F32, PB)) for i in range(2)]

    S.w("sync", "KS", dA_done[0])
    S.w("sync", "QS", dA_done[1])
    S.w("sync", "VS", dA_done[2])
    S.op("sync", lambda e: e.dma_start(
        q_sb[:, :].rearrange("p (c n) -> p c n", n=TOWN),
        s_q.rearrange("c p n -> p c n")), inc="DX", k=16)
    S.op("sync", lambda e: e.dma_start(
        v_sb[:, :].rearrange("p (s n) -> p s n", n=H * 65),
        s_v.rearrange("s p n -> p s n")), inc="DX", k=16)
    dQV = S.n["DX"]

    S.w("vector", "DX", dQV)
    qsq_guard = None
    for cc in range(NCK):
        S.wd("vector", qsq_guard)
        S.op("vector", lambda e, cc=cc: e.tensor_tensor(
            qsq[:, :], q_sb[:, cc * TOWN:(cc + 1) * TOWN],
            q_sb[:, cc * TOWN:(cc + 1) * TOWN], ALU.mult))
        vq = S.tok("vector", "VE")
        S.w("tensor", "VE", vq)
        for nq in range(4):
            S.wd("tensor", ps_sc[0].free)
            S.op("tensor", lambda e, nq=nq: e.matmul(
                ps_sc[0].ap[0:1, 0:512], ones_c[0:64, :], qsq[0:64, nq * 512:(nq + 1) * 512],
                start=True, stop=True))
            S.op("tensor", lambda e, nq=nq: e.matmul(
                ps_sc[0].ap[64:65, 0:512], ones_c[64:128, :],
                qsq[64:128, nq * 512:(nq + 1) * 512], start=True, stop=True))
            ps_sc[0].ready = ("PE", S.tok("tensor", "PE"))
            S.wd("vector", ps_sc[0].ready)
            for rr in (slice(0, 1), slice(64, 65)):
                S.op("vector", lambda e, rr=rr: e.tensor_reduce(
                    nrm[rr, 0:1], ps_sc[0].ap[rr, 0:512], AXX, ALU.max))
            S.vdr()
            if nq == 0:
                for rr in (slice(0, 1), slice(64, 65)):
                    S.op("vector", lambda e, rr=rr: e.tensor_copy(
                        nrm[rr, 1:2], nrm[rr, 0:1]))
            else:
                for rr in (slice(0, 1), slice(64, 65)):
                    S.op("vector", lambda e, rr=rr: e.tensor_tensor(
                        nrm[rr, 2:3], nrm[rr, 1:2], nrm[rr, 0:1], ALU.max))
                S.vdr()
                for rr in (slice(0, 1), slice(64, 65)):
                    S.op("vector", lambda e, rr=rr: e.tensor_copy(
                        nrm[rr, 1:2], nrm[rr, 2:3]))
            S.vdr()
            tk = S.tok("vector", "VE")
            ps_sc[0].free = ("VE", tk)
        qsq_guard = ps_sc[0].ready
        S.op("vector", lambda e, cc=cc: e.tensor_tensor(
            nrm[0:1, 3:4], mk2_sb[0:1, 2 * cc:2 * cc + 1], nrm[0:1, 1:2], ALU.add))
        S.vdr()
        S.op("vector", lambda e, cc=cc: e.tensor_copy(
            mk2_sb[0:1, 2 * cc:2 * cc + 1], nrm[0:1, 3:4]))
        S.vdr()
        S.op("vector", lambda e, cc=cc: e.tensor_tensor(
            nrm[64:65, 3:4], mk2_sb[64:65, 2 * cc + 1:2 * cc + 2],
            nrm[64:65, 1:2], ALU.add))
        S.vdr()
        S.op("vector", lambda e, cc=cc: e.tensor_copy(
            mk2_sb[64:65, 2 * cc + 1:2 * cc + 2], nrm[64:65, 3:4]))
        S.vdr()
        S.tok("vector", "VE")
    vQN = S.n["VE"]
    S.w("tensor", "VE", vQN)
    S.wd("tensor", ps_sc[0].free)
    S.op("tensor", lambda e: e.matmul(ps_sc[0].ap[0:1, 0:16], onef_c[64:65, :],
                                      mk2_sb[64:65, 0:16], start=True, stop=True))
    pRM = S.tok("tensor", "PE")
    S.w("vector", "PE", pRM)
    S.op("vector", lambda e: e.tensor_tensor(crow[0:1, :], mk2_sb[0:1, 0:16],
                                             ps_sc[0].ap[0:1, 0:16], ALU.add))
    S.vdr()
    S.op("vector", lambda e: e.tensor_scalar(mk2_sb[0:1, 0:16], crow[0:1, :], 0.5,
                                             None, ALU.mult))
    S.vdr()
    S.op("vector", lambda e: e.tensor_copy(c_bf[0:1, :], mk2_sb[0:1, 0:16]))
    S.vdr()
    vCB = S.tok("vector", "VE")
    ps_sc[0].free = ("VE", vCB)
    S.w("tensor", "VE", vCB)
    S.wd("tensor", ps_sc[0].free)
    S.op("tensor", lambda e: e.matmul(ps_sc[0].ap[:, 0:16], ones_r[:, :], c_bf[0:1, :],
                                      start=True, stop=True))
    pCB = S.tok("tensor", "PE")
    S.w("vector", "PE", pCB)
    S.op("vector", lambda e: e.tensor_scalar(
        bias_all[:, 0:16], ps_sc[0].ap[:, 0:16], -0.5, None, ALU.mult))
    S.vdr()
    vBA = S.tok("vector", "VE")
    ps_sc[0].free = ("VE", vBA)
    S.w("scalar", "VE", vBA)

    kp_free = [None, None]
    kp_dx = [None, None]
    p_free = [None, None, None, None]
    for pr in range(NCK):
        kb = k_pair[pr % 2]
        S.wd("sync", kp_free[pr % 2])
        ksem = "KA" if pr % 2 == 0 else "KB"
        S.op("sync", lambda e, pr=pr, kb=kb: e.dma_start(kb[:, :], s_k[pr]),
             inc=ksem, k=16)
        kp_dx[pr % 2] = S.n[ksem]
        he, ho = 2 * pr, 2 * pr + 1
        for tq in range(4):
            ave = ps_av[0]
            avo = ps_av[1]

            def emit_scores(sl, tq=tq, pr=pr, kb=kb):
                g, par = sl // 2, sl % 2
                bk = ps_sc[sl % 3]
                S.wd("tensor", bk.free)
                if tq == 0 and sl == 0:
                    S.w("tensor", "KA" if pr % 2 == 0 else "KB",
                        kp_dx[pr % 2])
                base = 0 if par == 0 else 64
                for half in range(2):
                    st = 2 * g + half
                    S.op("tensor", lambda e, st=st, tq=tq, pr=pr, kb=kb, bk=bk,
                         base=base, half=half: e.matmul(
                             bk.ap[:, half * 512:(half + 1) * 512],
                             kb[base:base + 64, st * 128:(st + 1) * 128],
                             q_sb[base:base + 64,
                                  pr * TOWN + tq * 512: pr * TOWN + tq * 512 + 512],
                             start=True, stop=True))
                bk.ready = ("PE", S.tok("tensor", "PE"))
                hh = (2 * pr) + par
                S.wd("scalar", bk.ready)
                S.wd("scalar", p_free[sl % 3])
                tk = S.op("scalar", lambda e, hh=hh, bk=bk, sl=sl:
                          e.activation(p_t[sl % 3][:, :], bk.ap[:, :], AF.Exp,
                                       bias=bias_all[:, hh:hh + 1]), inc="AC")
                bk.free = ("AC", tk)
                return tk

            exp_tok = {}
            exp_tok[0] = emit_scores(0)
            for sl in range(32):
                if sl + 1 < 32:
                    exp_tok[sl + 1] = emit_scores(sl + 1)
                g, par = sl // 2, sl % 2
                av = ave if par == 0 else avo
                S.w("tensor", "AC", exp_tok[sl])
                if g == 0:
                    S.wd("tensor", av.free)
                for half in range(2):
                    st = 2 * g + half
                    S.op("tensor", lambda e, st=st, sl=sl, av=av, par=par, pr=pr,
                         half=half: e.matmul(
                             av.ap[:, :],
                             v_sb[:, st * H * 65 + (2 * pr + par) * 65:
                                  st * H * 65 + (2 * pr + par) * 65 + 65],
                             p_t[sl % 3][:, half * 512:(half + 1) * 512],
                             start=(st == 0), stop=(st == 31)))
                av_tok = ("PE", S.tok("tensor", "PE"))
                p_free[sl % 3] = av_tok
                if g == 15:
                    av.ready = av_tok
            for par, hh, av in ((0, he, ave), (1, ho, avo)):
                S.wd("vector", av.ready)
                S.op("vector", lambda e, par=par, av=av: e.reciprocal(
                    rrow[0:1, par * 512:par * 512 + 512], av.ap[64:65, :]))
                S.vdr()
                S.op("vector", lambda e, par=par: e.tensor_copy(
                    rrow_bf[0:1, par * 512:par * 512 + 512],
                    rrow[0:1, par * 512:par * 512 + 512]))
                vr = S.tok("vector", "VE")
                S.w("tensor", "VE", vr)
                bcb = ps_sc[par]
                S.wd("tensor", bcb.free)
                S.op("tensor", lambda e, par=par, bcb=bcb: e.matmul(
                    bcb.ap[0:64, 0:512], ones_r[0:1, 0:64],
                    rrow_bf[0:1, par * 512:par * 512 + 512], start=True, stop=True))
                bcb.ready = ("PE", S.tok("tensor", "PE"))
                S.wd("vector", bcb.ready)
                base = 0 if par == 0 else 64
                tkb = S.op("vector", lambda e, bcb=bcb: e.tensor_copy(
                    rb_bc[0:64, :], bcb.ap[0:64, 0:512]), inc="VE")
                bcb.free = ("VE", tkb)
                S.vdr()
                tk = S.op("vector", lambda e, par=par, hh=hh, tq=tq, base=base,
                          pr=pr, av=av: e.tensor_tensor(
                              out_attn[base:base + 64,
                                       pr * TOWN + tq * 512: pr * TOWN + tq * 512 + 512],
                              av.ap[0:64, :], rb_bc[0:64, :], ALU.mult),
                          inc="VE")
                av.free = ("VE", tk)
        kp_free[pr % 2] = ("PE", S.n["PE"])
    vATT = S.n["VE"]
    if "oat" in dbg_aps:
        S.w("sync", "VE", vATT)
        S.op("sync", lambda e: e.dma_start(dbg_aps["oat"], out_attn[:, :]), inc="DS", k=16)
        S.op("sync", lambda e: e.dma_start(dbg_aps["biasall"], bias_all[:, :]), inc="DS", k=16)
    S.barrier()
    S.flush()
    PB.close()
    if stop == "b":
        PBC.close(); P.close(); es.close(); S.check(); return nc, dbg_aps

    PC = ExitStack()
    wo_T = sbt("wo_T", [128, NCK * D], BF16, PC)
    wraw2 = sbt("wraw2", [128, NCK * D], BF16, PC)
    wo_fm = sbt("wo_fm", [128, NCK * TOWN], BF16, PC)
    xw = [sbt(f"xw{i}", [128, D], F32, PC) for i in range(2)]
    x2t = [sbt(f"x2t{i}", [128, D], F32, PC) for i in range(2)]
    z2t = [sbt(f"z2t{i}", [128, D], BF16, PC) for i in range(2)]
    h2sl = sbt("h2sl", [128, NCK * 512], BF16, PC)
    scr2 = sbt("scr2", [128, D], F32, PC)
    scr2b = sbt("scr2b", [128, D], F32, PC)
    st2 = sbt("st2", [128, 12], F32, PC)
    ps_wo = [Bank(pst(f"ps_wo{i}", [128, 512], F32, PC)) for i in range(2)]
    ps_tc = [Bank(pst(f"ps_tc{i}", [128, 1024], BF16, PC)) for i in range(2)]

    wraw2_free = None
    S.op("gpsimd", lambda e: e.dma_start(
        wraw2[:, :].rearrange("p (c n) -> p c n", n=D),
        wo.rearrange("(c p) n -> p c n", p=128)), inc="WQ", k=16)
    S.w("tensor", "WQ", S.n["WQ"])
    for oc in range(NCK):
        for ic in range(NCK):
            bk = ps_tc[(oc * NCK + ic) % 2]
            S.wd("tensor", bk.free)
            S.op("tensor", lambda e, oc=oc, ic=ic, bk=bk: e.transpose(
                bk.ap[:, 0:128],
                wraw2[:, oc * D + ic * 128: oc * D + ic * 128 + 128], idb[:, :]))
            bk.ready = ("PE", S.tok("tensor", "PE"))
            S.wd("scalar", bk.ready)
            tk = S.op("scalar", lambda e, oc=oc, ic=ic, bk=bk: e.activation(
                wo_T[:, ic * D + oc * 128: ic * D + oc * 128 + 128],
                bk.ap[:, 0:128], AF.Identity), inc="AC")
            bk.free = ("AC", tk)
    aWO = S.n["AC"]

    S.w("tensor", "AC", aWO)
    S.w("tensor", "VE", vATT)
    for oc in range(NCK):
        for tqc in range(4):
            bk = ps_wo[(oc * 4 + tqc) % 2]
            S.wd("tensor", bk.free)
            for kc in range(NCK):
                S.op("tensor", lambda e, oc=oc, tqc=tqc, kc=kc, bk=bk: e.matmul(
                    bk.ap[:, :], wo_T[:, kc * D + oc * 128: kc * D + oc * 128 + 128],
                    out_attn[:, kc * TOWN + tqc * 512: kc * TOWN + tqc * 512 + 512],
                    start=(kc == 0), stop=(kc == NCK - 1)))
            bk.ready = ("PE", S.tok("tensor", "PE"))
            S.wd("scalar", bk.ready)
            tk = S.op("scalar", lambda e, oc=oc, tqc=tqc, bk=bk: e.activation(
                wo_fm[:, oc * TOWN + tqc * 512: oc * TOWN + tqc * 512 + 512],
                bk.ap[:, :], AF.Identity, bias=gabo[:, oc:oc + 1],
                scale=ga_v[:, oc:oc + 1]), inc="AC")
            bk.free = ("AC", tk)
    aWOF = S.n["AC"]

    xw_free = [None, None]
    x2_free = [None, None]
    z2_free = [None, None]
    h2_slab_store = None
    for ti in range(16):
        tl = ti % 4
        xbuf = xw[ti % 2]
        x2b = x2t[ti % 2]
        z2b = z2t[ti % 2]
        S.wd("sync", xw_free[ti % 2])
        xsem = "XA" if ti % 2 == 0 else "XB"
        S.op("sync", lambda e, ti=ti, xbuf=xbuf: e.dma_start(
            xbuf[:, :], xb[TOWN + ti * 128: TOWN + (ti + 1) * 128, :]), inc=xsem,
            k=16)
        dxo = S.n[xsem]
        bk = ps_tc[ti % 2]
        S.wd("tensor", bk.free)
        if ti == 0:
            S.w("tensor", "AC", aWOF)
        for jc in range(NCK):
            S.op("tensor", lambda e, jc=jc, ti=ti, bk=bk: e.transpose(
                bk.ap[:, jc * 128:(jc + 1) * 128],
                wo_fm[:, jc * TOWN + ti * 128: jc * TOWN + ti * 128 + 128],
                idb[:, :]))
        bk.ready = ("PE", S.tok("tensor", "PE"))
        S.wd("vector", bk.ready)
        S.w("vector", xsem, dxo)
        S.wd("vector", x2_free[ti % 2])
        tk = S.op("vector", lambda e, bk=bk, x2b=x2b, xbuf=xbuf: e.tensor_tensor(
            x2b[:, :], bk.ap[:, :], xbuf[:, :], ALU.add), inc="VE")
        bk.free = ("VE", tk)
        xw_free[ti % 2] = ("VE", tk)
        vx2 = tk
        S.w("sync", "VE", vx2)
        x2sem = "X2A" if ti % 2 == 0 else "X2B"
        S.op("sync", lambda e, ti=ti, x2b=x2b: e.dma_start(s_x2[ti], x2b[:, :]),
             inc=x2sem, k=16)
        dsx2 = S.n[x2sem]
        S.w("scalar", "VE", vx2)
        S.op("scalar", lambda e, x2b=x2b: e.activation(
            scr2[:, :], x2b[:, :], AF.Identity, accum_out=st2[:, 0:1]))
        S.op("scalar", lambda e, x2b=x2b: e.activation(
            scr2b[:, :], x2b[:, :], AF.Square, accum_out=st2[:, 1:2]), inc="AC")
        ac1 = S.n["AC"]
        S.w("vector", "AC", ac1)
        S.op("vector", lambda e: e.tensor_scalar(
            st2[:, 2:3], st2[:, 0:1], 1.0 / D, None, ALU.mult))
        S.vdr()
        S.op("vector", lambda e: e.tensor_tensor(
            st2[:, 3:4], st2[:, 2:3], st2[:, 2:3], ALU.mult))
        S.vdr()
        S.op("vector", lambda e: e.scalar_tensor_tensor(
            st2[:, 4:5], st2[:, 1:2], 1.0 / D, st2[:, 3:4], ALU.mult, ALU.subtract))
        S.vdr()
        ve1 = S.tok("vector", "VE")
        S.w("scalar", "VE", ve1)
        S.op("scalar", lambda e: e.activation(
            st2[:, 5:6], st2[:, 4:5], AF.Sqrt, bias=eps_c[:, :]), inc="AC")
        ac2 = S.n["AC"]
        S.w("vector", "AC", ac2)
        S.op("vector", lambda e: e.reciprocal(st2[:, 5:6], st2[:, 5:6]))
        S.vdr()
        S.op("vector", lambda e: e.scalar_tensor_tensor(
            st2[:, 6:7], st2[:, 2:3], -1.0, st2[:, 5:6], ALU.mult, ALU.mult))
        S.vdr()
        ve2 = S.tok("vector", "VE")
        S.w("scalar", "VE", ve2)
        S.wd("scalar", z2_free[ti % 2])
        S.op("scalar", lambda e, x2b=x2b, z2b=z2b: e.activation(
            z2b[:, :], x2b[:, :], AF.Identity, bias=st2[:, 6:7], scale=st2[:, 5:6]),
            inc="AC")
        acz = S.n["AC"]
        x2_free[ti % 2] = [("AC", acz), (x2sem, dsx2)]
        S.w("tensor", "AC", acz)
        bk2 = ps_tc[(ti + 1) % 2]
        S.wd("tensor", bk2.free)
        if tl == 0:
            S.wd("tensor", h2_slab_store)
        for j in range(NCK):
            S.op("tensor", lambda e, j=j, z2b=z2b, bk2=bk2: e.transpose(
                bk2.ap[:, j * 128:(j + 1) * 128], z2b[:, j * 128:(j + 1) * 128],
                idb[:, :]))
        bk2.ready = ("PE", S.tok("tensor", "PE"))
        z2_free[ti % 2] = bk2.ready
        S.wd("scalar", bk2.ready)
        for j in range(NCK):
            S.op("scalar", lambda e, j=j, tl=tl, bk2=bk2: e.activation(
                h2sl[:, j * 512 + tl * 128: j * 512 + tl * 128 + 128],
                bk2.ap[:, j * 128:(j + 1) * 128], AF.Identity,
                bias=sbv_v[:, j:j + 1], scale=cb1p[:, j:j + 1]))
        tk = S.tok("scalar", "AC")
        bk2.free = ("AC", tk)
        if tl == 3:
            sl = ti // 4
            S.w("sync", "AC", tk)
            S.op("sync", lambda e, sl=sl: e.dma_start(
                s_h2[:, :, sl * 512:(sl + 1) * 512].rearrange("c p n -> p c n"),
                h2sl[:, :].rearrange("p (c n) -> p c n", n=512)), inc="H2S", k=16)
            h2_slab_store = ("H2S", S.n["H2S"])
    dC_done = (S.n["X2A"], S.n["X2B"])
    if "x2" in dbg_aps:
        S.w("sync", "X2A", dC_done[0])
        S.w("sync", "X2B", dC_done[1])
        S.w("sync", "H2S", S.n["H2S"])
        S.op("sync", lambda e: e.dma_start(dbg_aps["x2"], s_x2), inc="DS", k=16)
        S.op("sync", lambda e: e.dma_start(dbg_aps["h2"], s_h2), inc="DS", k=16)
    S.barrier()
    S.flush()
    PC.close()
    PBC.close()
    if stop == "c":
        P.close(); es.close(); S.check(); return nc, dbg_aps

    PD = ExitStack()
    h2blk = sbt("h2blk", [128, NCK * 1024], BF16, PD)
    xnblk = sbt("xnblk", [128, NCK * 1024], BF16, PD)
    abelblk = sbt("abelblk", [128, NCK * 1024], BF16, PD)
    moeblk = sbt("moeblk", [128, NCK * 1024], BF16, PD)
    he_sb2 = [sbt(f"he_sb{i}", [128, 16 * 512], BF16, PD) for i in range(2)]
    abwb = sbt("abwb", [128, NCK * D], BF16, PD)
    ew1b = sbt("ew1b", [128, NCK * DFF], BF16, PD)
    ew2s = [sbt(f"ew2s{i}", [128, 16 * 128], BF16, PD) for i in range(2)]
    Ebuf = abwb
    wd_bf = sbt("wd_bf", [EXPN, 1024], BF16, PD)
    r2bc = sbt("r2bc", [128, 1024], BF16, PD)
    m2bc = sbt("m2bc", [128, 1024], BF16, PD)
    gbc = sbt("gbc", [128, 1024], BF16, PD)
    wbc = sbt("wbc", [128, 1024], BF16, PD)
    rowa = sbt("rowa", [1, 1024], F32, PD)
    rowb = sbt("rowb", [1, 1024], F32, PD)
    rowc = sbt("rowc", [1, 1024], F32, PD)
    row_bf = sbt("row_bf", [1, 1024], BF16, PD)
    sq2 = [sbt(f"sq2_{i}", [128, 512], BF16, PD) for i in range(2)]
    ltok = sbt("ltok", [128, 64], F32, PD)
    sgin = sbt("sgin", [128, 64], F32, PD)
    sel = sbt("sel", [128, 64], F32, PD)
    mx8 = sbt("mx8", [128, 8], F32, PD)
    wdc = sbt("wdc", [128, 2], F32, PD)
    wdv = sbt("wdv", [128, 64], F32, PD)
    wrow = sbt("wrow", [1, 1024], BF16, PD)
    lg_fm = sbt("lg_fm", [EXPN, 1024], F32, PD)
    x2r = [sbt("x2r0", [128, D], F32, PD)]
    x2r = [x2r[0], x2r[0]]
    x3 = sbt("x3", [128, D], F32, PD)

    ps_h = [Bank(pst(f"ps_h{i}", [128, 512], F32, PD)) for i in range(2)]
    ps_m = [Bank(pst(f"ps_m{i}", [128, 512], F32, PD)) for i in range(2)]
    ps_g = Bank(pst("ps_g", [128, 512], F32, PD))
    ps_lg = Bank(pst("ps_lg", [EXPN, 512], F32, PD))
    ps_t8 = Bank(pst("ps_t8", [128, 64], F32, PD))
    ps_ty = Bank(pst("ps_ty", [128, 1024], BF16, PD))

    ew1_free = None
    ew2s_free = [None, None]
    e1_loaded = {}

    S.op("gpsimd", lambda e: e.dma_start(
        abwb[:, :].rearrange("p (c n) -> p c n", n=D),
        abel_w.rearrange("(c p) n -> p c n", p=128)), inc="AB", k=16)
    dABW = S.n["AB"]

    def load_e1(tb, ex):
        S.wd("gpsimd", ew1_free)
        S.op("gpsimd", lambda e, ex=ex: e.dma_start(
            ew1b[:, :].rearrange("p (c n) -> p c n", n=DFF),
            ew1[ex].rearrange("(c p) n -> p c n", p=128)), inc="E1", k=16)
        e1_loaded[(tb, ex)] = S.n["E1"]

    load_e1(0, 0)

    h2_free = None
    Ebuf_free = None
    gbc_free = None
    wd_free = None
    ltok_free = None
    x2r_free2 = [None]
    x2r_free = None
    x3_free = None
    out_final = 0
    stats_pe_guard = None
    sq_free = [None, None]
    stg_free = [None, None]

    for tb in range(2):
        S.wd("sync", h2_free)
        if tb == 0:
            S.w("sync", "X2A", dC_done[0])
            S.w("sync", "X2B", dC_done[1])
            S.w("sync", "H2S", S.n["H2S"])
        S.op("sync", lambda e, tb=tb: e.dma_start(
            h2blk[:, :].rearrange("p (c n) -> p c n", n=1024),
            s_h2[:, :, tb * 1024:(tb + 1) * 1024].rearrange("c p n -> p c n")),
            inc="HH", k=16)
        dH2 = S.n["HH"]

        S.w("tensor", "HH", dH2)
        S.wd("tensor", stats_pe_guard)
        for nb in range(2):
            S.wd("tensor", ps_g.free)
            for kc in range(NCK):
                S.op("tensor", lambda e, nb=nb, kc=kc: e.matmul(
                    ps_g.ap[0:1, :], ones_c[:, :],
                    h2blk[:, kc * 1024 + nb * 512: kc * 1024 + nb * 512 + 512],
                    start=(kc == 0), stop=(kc == NCK - 1)))
            ps_g.ready = ("PE", S.tok("tensor", "PE"))
            S.wd("vector", ps_g.ready)
            if tb == 0 and nb == 0:
                S.w("vector", "HH", dH2)
            tk = S.op("vector", lambda e, nb=nb: e.tensor_scalar(
                rowa[0:1, nb * 512:(nb + 1) * 512], ps_g.ap[0:1, :], 1.0 / D, None,
                ALU.mult), inc="VE")
            ps_g.free = ("VE", tk)
        for nb in range(2):
            for kc in range(NCK):
                sqb = sq2[kc % 2]
                S.wd("vector", sq_free[kc % 2])
                S.op("vector", lambda e, nb=nb, kc=kc, sqb=sqb: e.tensor_tensor(
                    sqb[:, :],
                    h2blk[:, kc * 1024 + nb * 512: kc * 1024 + nb * 512 + 512],
                    h2blk[:, kc * 1024 + nb * 512: kc * 1024 + nb * 512 + 512],
                    ALU.mult))
                vsq = S.tok("vector", "VE")
                S.w("tensor", "VE", vsq)
                if kc == 0:
                    S.wd("tensor", ps_g.free)
                S.op("tensor", lambda e, kc=kc, sqb=sqb: e.matmul(
                    ps_g.ap[0:1, :], ones_c[:, :], sqb[:, :],
                    start=(kc == 0), stop=(kc == NCK - 1)))
                sq_free[kc % 2] = ("PE", S.tok("tensor", "PE"))
            ps_g.ready = ("PE", S.tok("tensor", "PE"))
            S.wd("vector", ps_g.ready)
            tk = S.op("vector", lambda e, nb=nb: e.tensor_scalar(
                rowb[0:1, nb * 512:(nb + 1) * 512], ps_g.ap[0:1, :], 1.0 / D, None,
                ALU.mult), inc="VE")
            ps_g.free = ("VE", tk)
        S.op("vector", lambda e: e.tensor_tensor(rowc[0:1, :], rowa[0:1, :],
                                                 rowa[0:1, :], ALU.mult))
        S.vdr()
        S.op("vector", lambda e: e.tensor_tensor(rowb[0:1, :], rowb[0:1, :],
                                                 rowc[0:1, :], ALU.subtract))
        S.vdr()
        vve = S.tok("vector", "VE")
        S.w("scalar", "VE", vve)
        S.op("scalar", lambda e: e.activation(rowb[0:1, :], rowb[0:1, :], AF.Sqrt,
                                              bias=eps_c[0:1, :]), inc="AC")
        aSq = S.n["AC"]
        S.w("vector", "AC", aSq)
        S.op("vector", lambda e: e.reciprocal(rowb[0:1, :], rowb[0:1, :]))
        S.vdr()
        S.op("vector", lambda e: e.tensor_tensor(rowc[0:1, :], rowa[0:1, :],
                                                 rowb[0:1, :], ALU.mult))
        S.op("vector", lambda e: e.tensor_copy(row_bf[0:1, :], rowb[0:1, :]))
        vr2 = S.tok("vector", "VE")
        S.w("tensor", "VE", vr2)
        for nb in range(2):
            S.wd("tensor", ps_g.free)
            S.op("tensor", lambda e, nb=nb: e.matmul(
                ps_g.ap[:, :], ones_r[:, :], row_bf[0:1, nb * 512:(nb + 1) * 512],
                start=True, stop=True))
            ps_g.ready = ("PE", S.tok("tensor", "PE"))
            S.wd("vector", ps_g.ready)
            tk = S.op("vector", lambda e, nb=nb: e.tensor_copy(
                r2bc[:, nb * 512:(nb + 1) * 512], ps_g.ap[:, :]), inc="VE")
            ps_g.free = ("VE", tk)
        S.op("vector", lambda e: e.tensor_copy(row_bf[0:1, :], rowc[0:1, :]))
        vm2 = S.tok("vector", "VE")
        S.w("tensor", "VE", vm2)
        for nb in range(2):
            S.wd("tensor", ps_g.free)
            S.op("tensor", lambda e, nb=nb: e.matmul(
                ps_g.ap[:, :], ones_r[:, :], row_bf[0:1, nb * 512:(nb + 1) * 512],
                start=True, stop=True))
            ps_g.ready = ("PE", S.tok("tensor", "PE"))
            S.wd("vector", ps_g.ready)
            tk = S.op("vector", lambda e, nb=nb: e.tensor_copy(
                m2bc[:, nb * 512:(nb + 1) * 512], ps_g.ap[:, :]), inc="VE")
            ps_g.free = ("VE", tk)
        for kc in range(NCK):
            S.op("vector", lambda e, kc=kc: e.tensor_tensor(
                xnblk[:, kc * 1024:(kc + 1) * 1024],
                h2blk[:, kc * 1024:(kc + 1) * 1024], r2bc[:, :], ALU.mult))
        S.vdr()
        for kc in range(NCK):
            S.op("vector", lambda e, kc=kc: e.tensor_tensor(
                xnblk[:, kc * 1024:(kc + 1) * 1024],
                xnblk[:, kc * 1024:(kc + 1) * 1024], m2bc[:, :], ALU.subtract))
        S.vdr()
        for kc in range(NCK):
            S.op("vector", lambda e, kc=kc: e.tensor_scalar(
                xnblk[:, kc * 1024:(kc + 1) * 1024],
                xnblk[:, kc * 1024:(kc + 1) * 1024], ln_g[:, kc:kc + 1],
                ln_b[:, kc:kc + 1], ALU.mult, ALU.add))
        vXN = S.tok("vector", "VE")
        if tb == 0 and "xn" in dbg_aps:
            S.w("sync", "VE", vXN)
            S.op("sync", lambda e: e.dma_start(dbg_aps["xn"], xnblk[:, :]), inc="DS", k=16)

        S.w("tensor", "VE", vXN)
        S.w("tensor", "DW", nDWsmall)
        for nb in range(2):
            S.wd("tensor", ps_lg.free)
            for kc in range(NCK):
                S.op("tensor", lambda e, nb=nb, kc=kc: e.matmul(
                    ps_lg.ap[:, :], gw_bf[:, kc * EXPN:(kc + 1) * EXPN],
                    xnblk[:, kc * 1024 + nb * 512: kc * 1024 + nb * 512 + 512],
                    start=(kc == 0), stop=(kc == NCK - 1)))
            ps_lg.ready = ("PE", S.tok("tensor", "PE"))
            S.wd("scalar", ps_lg.ready)
            tk = S.op("scalar", lambda e, nb=nb: e.activation(
                lg_fm[:, nb * 512:(nb + 1) * 512], ps_lg.ap[:, :], AF.Identity,
                bias=gate_b_sb[:, :]), inc="AC")
            ps_lg.free = ("AC", tk)
        aLG = S.n["AC"]
        S.w("tensor", "AC", aLG)
        S.wd("tensor", ps_t8.free)
        S.wd("vector", ltok_free)
        for tl in range(8):
            S.op("tensor", lambda e, tl=tl: e.transpose(
                ps_t8.ap[:, tl * 8:(tl + 1) * 8],
                lg_fm[:, tl * 128:(tl + 1) * 128], idf[0:8, 0:8]))
        ps_t8.ready = ("PE", S.tok("tensor", "PE"))
        S.wd("vector", ps_t8.ready)
        tk = S.op("vector", lambda e: e.tensor_copy(ltok[:, :], ps_t8.ap[:, :]),
                  inc="VE")
        S.vdr()
        ps_t8.free = ("VE", tk)
        for tl in range(8):
            S.op("vector", lambda e, tl=tl: e.max(out=mx8[:, :],
                                                  in_=ltok[:, tl * 8:(tl + 1) * 8]))
            S.vdr()
            S.op("vector", lambda e: e.tensor_tensor(
                wdc[:, 0:1], mx8[:, 0:1], mx8[:, 1:2], ALU.add))
            S.vdr()
            S.op("vector", lambda e, tl=tl: e.tensor_scalar(
                wdv[:, tl * 8:(tl + 1) * 8], ltok[:, tl * 8:(tl + 1) * 8],
                2.0, None, ALU.mult))
            S.vdr()
            S.op("vector", lambda e, tl=tl: e.tensor_scalar(
                sgin[:, tl * 8:(tl + 1) * 8], wdv[:, tl * 8:(tl + 1) * 8],
                wdc[:, 0:1], None, ALU.subtract))
            S.vdr()
            S.op("vector", lambda e, tl=tl: e.tensor_scalar(
                sel[:, tl * 8:(tl + 1) * 8], ltok[:, tl * 8:(tl + 1) * 8],
                mx8[:, 1:2], None, ALU.is_ge))
            S.vdr()
        vTOP = S.tok("vector", "VE")
        S.w("scalar", "VE", vTOP)
        aSG = S.op("scalar", lambda e: e.activation(wdv[:, :], sgin[:, :],
                                                    AF.Sigmoid), inc="AC")
        S.w("vector", "AC", aSG)
        S.op("vector", lambda e: e.tensor_tensor(sgin[:, :], sel[:, :], wdv[:, :],
                                                 ALU.mult))
        vWD = S.tok("vector", "VE")
        ltok_free = ("VE", vWD)
        S.w("tensor", "VE", vWD)
        S.wd("vector", wd_free)
        for half in range(2):
            S.wd("tensor", ps_lg.free)
            for tl4 in range(4):
                tl = half * 4 + tl4
                S.op("tensor", lambda e, tl=tl, tl4=tl4: e.transpose(
                    ps_lg.ap[:, tl4 * 128:(tl4 + 1) * 128],
                    sgin[:, tl * 8:(tl + 1) * 8], idf[:, :]))
            ps_lg.ready = ("PE", S.tok("tensor", "PE"))
            S.wd("vector", ps_lg.ready)
            tk = S.op("vector", lambda e, half=half: e.tensor_copy(
                wd_bf[:, half * 512:(half + 1) * 512], ps_lg.ap[:, :]), inc="VE")
            ps_lg.free = ("VE", tk)
        vWDF = S.n["VE"]
        wd_free = ("VE", vWDF)
        if tb == 0 and "wd" in dbg_aps:
            S.w("sync", "VE", vWDF)
            S.op("sync", lambda e: e.dma_start(dbg_aps["wd"], wd_bf[:, :]), inc="DS", k=16)
            S.op("sync", lambda e: e.dma_start(dbg_aps["lg"], lg_fm[:, :]), inc="DS", k=16)

        S.w("tensor", "AB", dABW)
        for oc in range(NCK):
            for nb2 in range(2):
                bk = ps_h[(oc * 2 + nb2) % 2]
                S.wd("tensor", bk.free)
                for kc in range(NCK):
                    S.op("tensor", lambda e, oc=oc, nb2=nb2, kc=kc, bk=bk: e.matmul(
                        bk.ap[:, :],
                        abwb[:, kc * D + oc * 128: kc * D + oc * 128 + 128],
                        xnblk[:, kc * 1024 + nb2 * 512: kc * 1024 + nb2 * 512 + 512],
                        start=(kc == 0), stop=(kc == NCK - 1)))
                bk.ready = ("PE", S.tok("tensor", "PE"))
                S.wd("scalar", bk.ready)
                tk = S.op("scalar", lambda e, oc=oc, nb2=nb2, bk=bk: e.activation(
                    abelblk[:, oc * 1024 + nb2 * 512: oc * 1024 + nb2 * 512 + 512],
                    bk.ap[:, :], AF.Gelu, bias=abel_bb[:, oc:oc + 1]), inc="AC")
                bk.free = ("AC", tk)
        Ebuf_free = ("PE", S.n["PE"])

        he_free = [None, None]
        for ex in range(EXPN):
            S.w("tensor", "VE", vWDF)
            for half in range(2):
                S.wd("tensor", ps_lg.free)
                for tl4 in range(4):
                    tl = half * 4 + tl4
                    S.op("tensor", lambda e, ex=ex, tl=tl, tl4=tl4: e.transpose(
                        ps_lg.ap[0:1, tl4 * 128:(tl4 + 1) * 128],
                        sgin[:, tl * 8 + ex: tl * 8 + ex + 1], idf[:, :]))
                ps_lg.ready = ("PE", S.tok("tensor", "PE"))
                S.wd("vector", ps_lg.ready)
                tk = S.op("vector", lambda e, half=half: e.tensor_copy(
                    wrow[0:1, half * 512:(half + 1) * 512],
                    ps_lg.ap[0:1, 0:512]), inc="VE")
                ps_lg.free = ("VE", tk)
            vWR = S.n["VE"]
            S.w("tensor", "VE", vWR)
            for nb2 in range(2):
                bk = ps_m[nb2 % 2]
                S.wd("tensor", bk.free)
                S.op("tensor", lambda e, nb2=nb2, bk=bk: e.matmul(
                    bk.ap[:, :], ones_r[:, :],
                    wrow[0:1, nb2 * 512:(nb2 + 1) * 512], start=True, stop=True))
                bk.ready = ("PE", S.tok("tensor", "PE"))
                S.wd("vector", bk.ready)
                tk = S.op("vector", lambda e, nb2=nb2, bk=bk: e.tensor_copy(
                    wbc[:, nb2 * 512:(nb2 + 1) * 512], bk.ap[:, :]), inc="VE")
                bk.free = ("VE", tk)
            S.vdr()
            vWBC = S.n["VE"]
            S.w("tensor", "E1", e1_loaded[(tb, ex)])
            for tc in range(2):
                he_sb = he_sb2[tc]
                for hfc in range(16):
                    bk = ps_h[hfc % 2]
                    S.wd("tensor", bk.free)
                    if hfc == 0:
                        S.wd("tensor", he_free[tc])
                    for kc in range(NCK):
                        S.op("tensor", lambda e, hfc=hfc, tc=tc, kc=kc, bk=bk:
                             e.matmul(
                                 bk.ap[:, :],
                                 ew1b[:, kc * DFF + hfc * 128: kc * DFF + hfc * 128 + 128],
                                 xnblk[:, kc * 1024 + tc * 512: kc * 1024 + tc * 512 + 512],
                                 start=(kc == 0), stop=(kc == NCK - 1)))
                    bk.ready = ("PE", S.tok("tensor", "PE"))
                    S.wd("scalar", bk.ready)
                    S.wd("scalar", stg_free[hfc % 2])
                    stg = sq2[hfc % 2]
                    atk = S.op("scalar", lambda e, hfc=hfc, ex=ex, bk=bk,
                               stg=stg: e.activation(
                                   stg[:, :], bk.ap[:, :],
                                   AF.Gelu,
                                   bias=eb1_sb[:, ex * 16 + hfc: ex * 16 + hfc + 1]),
                               inc="AC")
                    bk.free = ("AC", atk)
                    S.w("vector", "AC", atk)
                    if hfc == 0 and tc == 0:
                        S.w("vector", "VE", vWBC)
                    tkh = S.op("vector", lambda e, hfc=hfc, tc=tc, he_sb=he_sb,
                               stg=stg: e.tensor_tensor(
                                   he_sb[:, hfc * 512:(hfc + 1) * 512],
                                   stg[:, :],
                                   wbc[:, tc * 512:(tc + 1) * 512], ALU.mult),
                               inc="VE")
                    stg_free[hfc % 2] = ("VE", tkh)
                vHE = S.n["VE"]
            pHE = S.n["PE"]
            nex = (tb, ex + 1) if ex < EXPN - 1 else ((1, 0) if tb == 0 else None)
            S.w("tensor", "VE", vHE)
            for oc in range(NCK):
                sb2 = ew2s[oc % 2]
                S.wd("gpsimd", ew2s_free[oc % 2])
                ssem = "SA" if oc % 2 == 0 else "SB"
                S.op("gpsimd", lambda e, ex=ex, oc=oc, sb2=sb2: e.dma_start(
                    sb2[:, :], ew2r[ex, oc]), inc=ssem, k=16)
                dE2 = S.n[ssem]
                if nex is not None:
                    if oc == 0:
                        S.wd("gpsimd", ("PE", pHE))
                    S.op("gpsimd", lambda e, nex=nex, oc=oc: e.dma_start(
                        ew1b[:, oc * DFF:(oc + 1) * DFF],
                        ew1[nex[1]][oc * 128:(oc + 1) * 128, :]), inc="E1", k=16)
                    if oc == NCK - 1:
                        e1_loaded[nex] = S.n["E1"]
                S.w("tensor", ssem, dE2)
                for tc in range(2):
                    he_sb = he_sb2[tc]
                    bk = ps_m[(oc * 2 + tc) % 2]
                    S.wd("tensor", bk.free)
                    if ex == 0:
                        S.op("tensor", lambda e, oc=oc, tc=tc, bk=bk: e.matmul(
                            bk.ap[:, :], eb2b[:, oc * 128:(oc + 1) * 128],
                            wd_bf[:, tc * 512:(tc + 1) * 512], start=True,
                            stop=False))
                    for hfc in range(16):
                        S.op("tensor", lambda e, oc=oc, hfc=hfc, tc=tc, bk=bk, ex=ex,
                             sb2=sb2, he_sb=he_sb: e.matmul(
                                 bk.ap[:, :], sb2[:, hfc * 128:(hfc + 1) * 128],
                                 he_sb[:, hfc * 512:(hfc + 1) * 512],
                                 start=(hfc == 0 and ex != 0), stop=(hfc == 15)))
                    bk.ready = ("PE", S.tok("tensor", "PE"))
                    S.wd("vector", bk.ready)
                    if ex == 0:
                        tk = S.op("vector", lambda e, oc=oc, tc=tc, bk=bk:
                                  e.tensor_copy(
                                      moeblk[:, oc * 1024 + tc * 512:
                                             oc * 1024 + tc * 512 + 512],
                                      bk.ap[:, :]), inc="VE")
                    else:
                        tk = S.op("vector", lambda e, oc=oc, tc=tc, bk=bk:
                                  e.tensor_tensor(
                                      moeblk[:, oc * 1024 + tc * 512:
                                             oc * 1024 + tc * 512 + 512],
                                      moeblk[:, oc * 1024 + tc * 512:
                                             oc * 1024 + tc * 512 + 512],
                                      bk.ap[:, :], ALU.add), inc="VE")
                        S.vdr()
                    bk.free = ("VE", tk)
                ew2s_free[oc % 2] = ("PE", S.n["PE"])
            he_free = [("PE", S.n["PE"]), ("PE", S.n["PE"])]
        vMOE = S.n["VE"]
        if tb == 0 and "moe" in dbg_aps:
            S.w("sync", "VE", vMOE)
            S.w("sync", "AC", S.n["AC"])
            S.op("sync", lambda e: e.dma_start(dbg_aps["moe"], moeblk[:, :]), inc="DS", k=16)
            S.op("sync", lambda e: e.dma_start(dbg_aps["abel"], abelblk[:, :]), inc="DS", k=16)

        rot_seq = [(moeblk, he_sb2[0]), (he_sb2[0], xnblk), (xnblk, moeblk)]
        for ri in range(3):
            rcur, rnext = rot_seq[ri]
            S.wd("sync", Ebuf_free)
            S.op("sync", lambda e, ri=ri: e.dma_start(Ebuf[:, :], s_E[ri]),
                 inc="EE", k=16)
            dEb = S.n["EE"]
            S.w("tensor", "EE", dEb)
            if ri == 0:
                S.w("tensor", "VE", vMOE)
            for mc in range(NCK):
                for nb2 in range(2):
                    bk = ps_h[(mc * 2 + nb2) % 2]
                    S.wd("tensor", bk.free)
                    for kc in range(NCK):
                        S.op("tensor", lambda e, mc=mc, nb2=nb2, kc=kc, bk=bk,
                             rcur=rcur: e.matmul(
                                 bk.ap[:, :],
                                 Ebuf[:, kc * D + mc * 128: kc * D + mc * 128 + 128],
                                 rcur[:, kc * 1024 + nb2 * 512:
                                      kc * 1024 + nb2 * 512 + 512],
                                 start=(kc == 0), stop=(kc == NCK - 1)))
                    bk.ready = ("PE", S.tok("tensor", "PE"))
                    S.wd("scalar", bk.ready)
                    tk = S.op("scalar", lambda e, mc=mc, nb2=nb2, bk=bk, rnext=rnext:
                              e.activation(
                                  rnext[:, mc * 1024 + nb2 * 512:
                                        mc * 1024 + nb2 * 512 + 512],
                                  bk.ap[:, :], AF.Identity), inc="AC")
                    bk.free = ("AC", tk)
            Ebuf_free = ("PE", S.n["PE"])
        aROT = S.n["AC"]
        stats_pe_guard = ("PE", S.n["PE"])
        if tb == 0:
            S.wd("gpsimd", Ebuf_free)
            S.op("gpsimd", lambda e: e.dma_start(
                abwb[:, :].rearrange("p (c n) -> p c n", n=D),
                abel_w.rearrange("(c p) n -> p c n", p=128)), inc="AB", k=16)
            dABW = S.n["AB"]

        S.w("tensor", "AC", aROT)
        for nb2 in range(2):
            S.wd("tensor", ps_g.free)
            for kc in range(16):
                src = abelblk if kc < 8 else moeblk
                kcc = kc % 8
                S.op("tensor", lambda e, nb2=nb2, kc=kc, kcc=kcc, src=src: e.matmul(
                    ps_g.ap[0:1, :], gcw_bf[:, kc:kc + 1],
                    src[:, kcc * 1024 + nb2 * 512: kcc * 1024 + nb2 * 512 + 512],
                    start=(kc == 0), stop=(kc == 15)))
            ps_g.ready = ("PE", S.tok("tensor", "PE"))
            S.wd("scalar", ps_g.ready)
            tk = S.op("scalar", lambda e, nb2=nb2: e.activation(
                rowa[0:1, nb2 * 512:(nb2 + 1) * 512], ps_g.ap[0:1, :], AF.Sigmoid,
                bias=gcb_sb[:, :]), inc="AC")
            ps_g.free = ("AC", tk)
        aG = S.n["AC"]
        S.w("vector", "AC", aG)
        S.wd("vector", gbc_free)
        S.op("vector", lambda e: e.tensor_copy(row_bf[0:1, :], rowa[0:1, :]))
        vG = S.tok("vector", "VE")
        S.w("tensor", "VE", vG)
        for nb2 in range(2):
            S.wd("tensor", ps_g.free)
            S.op("tensor", lambda e, nb2=nb2: e.matmul(
                ps_g.ap[:, :], ones_r[:, :], row_bf[0:1, nb2 * 512:(nb2 + 1) * 512],
                start=True, stop=True))
            ps_g.ready = ("PE", S.tok("tensor", "PE"))
            S.wd("vector", ps_g.ready)
            tk = S.op("vector", lambda e, nb2=nb2: e.tensor_copy(
                gbc[:, nb2 * 512:(nb2 + 1) * 512], ps_g.ap[:, :]), inc="VE")
            ps_g.free = ("VE", tk)
        vGB = S.n["VE"]
        if tb == 0 and "rot3" in dbg_aps:
            S.w("sync", "VE", vGB)
            S.w("sync", "AC", S.n["AC"])
            S.op("sync", lambda e: e.dma_start(dbg_aps["rot3"], moeblk[:, :]), inc="DS", k=16)
            S.op("sync", lambda e: e.dma_start(dbg_aps["grow"], gbc[:, :]), inc="DS", k=16)

        ybuf = xnblk
        ytmp = he_sb2[0]
        S.w("vector", "VE", vGB)
        for kc in range(NCK):
            S.op("vector", lambda e, kc=kc, ybuf=ybuf: e.tensor_tensor(
                ybuf[:, kc * 1024:(kc + 1) * 1024],
                moeblk[:, kc * 1024:(kc + 1) * 1024],
                abelblk[:, kc * 1024:(kc + 1) * 1024], ALU.subtract))
        S.vdr()
        for kc in range(NCK):
            S.op("vector", lambda e, kc=kc, ybuf=ybuf, ytmp=ytmp: e.tensor_tensor(
                ytmp[:, kc * 1024:(kc + 1) * 1024],
                ybuf[:, kc * 1024:(kc + 1) * 1024], gbc[:, :], ALU.mult))
        S.vdr()
        for kc in range(NCK):
            S.op("vector", lambda e, kc=kc, ybuf=ybuf, ytmp=ytmp: e.tensor_tensor(
                ybuf[:, kc * 1024:(kc + 1) * 1024],
                ytmp[:, kc * 1024:(kc + 1) * 1024],
                abelblk[:, kc * 1024:(kc + 1) * 1024], ALU.add))
        S.vdr()
        for kc in range(NCK):
            S.op("vector", lambda e, kc=kc, ybuf=ybuf, ytmp=ytmp: e.tensor_tensor(
                ytmp[:, kc * 1024:(kc + 1) * 1024],
                ybuf[:, kc * 1024:(kc + 1) * 1024],
                h2blk[:, kc * 1024:(kc + 1) * 1024], ALU.add))
        S.vdr()
        for kc in range(NCK):
            S.op("vector", lambda e, kc=kc, ybuf=ybuf, ytmp=ytmp: e.tensor_scalar(
                ybuf[:, kc * 1024:(kc + 1) * 1024],
                ytmp[:, kc * 1024:(kc + 1) * 1024], gb_v[:, kc:kc + 1], None,
                ALU.mult))
        vY = S.tok("vector", "VE")
        gbc_free = ("VE", vY)
        h2_free = ("VE", vY)
        he_free = [("VE", vY), he_free[1]]

        S.w("tensor", "VE", vY)
        for tl in range(8):
            ti = tb * 8 + tl
            S.wd("sync", x2r_free)
            xsem = "XA" if tl % 2 == 0 else "XB"
            S.op("sync", lambda e, ti=ti, tl=tl: e.dma_start(
                x2r[tl % 2][:, :], s_x2[ti]), inc=xsem, k=16)
            dX2R = S.n[xsem]
            S.wd("tensor", ps_ty.free)
            for jc in range(NCK):
                S.op("tensor", lambda e, jc=jc, tl=tl, ybuf=ybuf: e.transpose(
                    ps_ty.ap[:, jc * 128:(jc + 1) * 128],
                    ybuf[:, jc * 1024 + tl * 128: jc * 1024 + tl * 128 + 128],
                    idb[:, :]))
            ps_ty.ready = ("PE", S.tok("tensor", "PE"))
            S.wd("vector", ps_ty.ready)
            S.w("vector", xsem, dX2R)
            S.wd("vector", x3_free)
            tk = S.op("vector", lambda e, tl=tl: e.tensor_tensor(
                x3[:, :], ps_ty.ap[:, :], x2r[tl % 2][:, :], ALU.add), inc="VE")
            ps_ty.free = ("VE", tk)
            x2r_free = ("VE", tk)
            S.w("sync", "VE", tk)
            S.op("sync", lambda e, ti=ti: e.dma_start(
                out_hbm[ti * 128:(ti + 1) * 128, :], x3[:, :]), inc="OS", k=16)
            x3_free = ("OS", S.n["OS"])
        out_final = S.n["OS"]
    S.w("sync", "OS", out_final)
    S.flush()
    PD.close()
    P.close()
    es.close()
    S.check()
    return nc, dbg_aps


_CACHE = {}


def _vec_fm(v):
    return np.ascontiguousarray(np.asarray(v, dtype=np.float32).reshape(8, 128).T)


def kernel(**inputs):
    nc, _ = _CACHE.get("nc") or _CACHE.setdefault("nc", build_nc())

    x = np.asarray(inputs["L0_attn_x"], dtype=np.float32)
    BF = ml_dtypes.bfloat16
    common = {
        "wq": np.ascontiguousarray(inputs["L0_attn_wq_w"], dtype=np.float32),
        "wk": np.ascontiguousarray(inputs["L0_attn_wk_w"], dtype=np.float32),
        "wv": np.ascontiguousarray(inputs["L0_attn_wv_w"], dtype=np.float32),
        "wo": np.ascontiguousarray(inputs["L0_attn_wo_w"], dtype=np.float32),
        "abel_w": np.ascontiguousarray(inputs["L0_ffn_abel_w"], dtype=np.float32),
        "gate_w": np.ascontiguousarray(inputs["L0_ffn_moe_gate_w"], dtype=np.float32),
        "ew1": np.ascontiguousarray(inputs["L0_ffn_ew1"], dtype=np.float32),
        "ew2r": np.ascontiguousarray(
            np.asarray(inputs["L0_ffn_ew2"], dtype=np.float32)
            .reshape(EXPN, 16, 128, NCK, 128).transpose(0, 3, 2, 1, 4)
            .reshape(EXPN, NCK, 128, 16 * 128)),
        "gens": np.ascontiguousarray(inputs["L0_ffn_gens"], dtype=np.float32),
        "wmod": np.ascontiguousarray(inputs["L0_wmod"], dtype=np.float32),
        "eb2": np.ascontiguousarray(inputs["L0_ffn_eb2"], dtype=np.float32),
        "bq_f": _vec_fm(inputs["L0_attn_wq_b"]),
        "bk_f": _vec_fm(inputs["L0_attn_wk_b"]),
        "bo_f": _vec_fm(inputs["L0_attn_wo_b"]),
        "bv_row": np.ascontiguousarray(
            np.asarray(inputs["L0_attn_wv_b"], dtype=np.float32)[None, :]),
        "abel_b_f": _vec_fm(inputs["L0_ffn_abel_b"]),
        "ln_g_f": _vec_fm(inputs["L0_ffn_ln_gamma"]),
        "ln_b_f": _vec_fm(inputs["L0_ffn_ln_beta"]),
        "eb1_f": np.ascontiguousarray(
            np.asarray(inputs["L0_ffn_eb1"], dtype=np.float32)
            .reshape(EXPN, 16, 128).transpose(2, 0, 1).reshape(128, EXPN * 16)),
        "gate_b": np.ascontiguousarray(
            np.asarray(inputs["L0_ffn_moe_gate_b"], dtype=np.float32)[:, None]),
        "gcw_f": np.ascontiguousarray(
            np.asarray(inputs["L0_ffn_gc_w"], dtype=np.float32)
            .reshape(16, 128).T),
        "gcb": np.ascontiguousarray(
            np.asarray(inputs["L0_ffn_gc_b"], dtype=np.float32).reshape(1, 1)),
        "ident_bf": np.eye(128, dtype=BF),
        "ident_f": np.eye(128, dtype=np.float32),
        "ones_row_bf": np.ones((1, 128), dtype=BF),
        "ones_col_bf": np.ones((128, 1), dtype=BF),
        "onesf_c": np.ones((128, 1), dtype=np.float32),
        "eye_fm": np.ascontiguousarray(
            np.eye(D, dtype=BF).reshape(NCK, 128, D).transpose(1, 0, 2)
            .reshape(128, NCK * D)),
    }
    cvec = np.asarray(inputs["L0_c"], dtype=np.float32)
    in_maps = []
    for core in range(8):
        b, half = core // 2, core % 2
        other = x[b, (1 - half) * TOWN:(2 - half) * TOWN]
        own = x[b, half * TOWN:(half + 1) * TOWN]
        m = dict(common)
        m["xb"] = np.ascontiguousarray(np.concatenate([other, own], axis=0))
        m["c_f"] = _vec_fm(cvec[b])
        in_maps.append(m)
    res = run_bass_kernel_spmd(nc, in_maps, core_ids=list(range(8)))
    full = np.empty((4, T, D), dtype=np.float32)
    for core in range(8):
        b, half = core // 2, core % 2
        full[b, half * TOWN:(half + 1) * TOWN] = res.results[core]["out"]
    return full
